# version 16
# baseline (speedup 1.0000x reference)
import contextlib
import numpy as np

import concourse.bass as bass
import concourse.mybir as mybir
from concourse.bass_utils import run_bass_kernel_spmd

F32 = mybir.dt.float32
BF16 = mybir.dt.bfloat16
AF = mybir.ActivationFunctionType
ALU = mybir.AluOpType

NT = 2048
D = 1024
NKC = 8
TS = 512
NSTRIP = NT // TS
HE = 1024
NSUP = 12
EPS = 1e-5
SCALE = 1.0 / 32.0

ENGINES = ("gpsimd", "tensor", "vector", "scalar", "sync")


def pos_of_c(c):
    return (c // 2) if c % 2 == 0 else 8 + (c // 2)


def c_of_pos(p):
    return 2 * p if p < 8 else 2 * (p - 8) + 1


class Sched:

    def __init__(self):
        self.segs = []
        self.ops = {e: [] for e in ENGINES}
        self.cnt = {}
        self.lastwait = {e: {} for e in ENGINES}

    def seg_break(self):
        self.segs.append(self.ops)
        self.ops = {e: [] for e in ENGINES}

    def finish(self):
        self.seg_break()
        return self.segs

    def emit(self, eng, fn):
        self.ops[eng].append(fn)

    def wait(self, eng, sem, val):
        if not val or val <= 0:
            return
        if self.lastwait[eng].get(sem, -1) >= val:
            return
        self.lastwait[eng][sem] = val
        self.emit(eng, lambda e, S, _s=sem, _v=val: e.wait_ge(S[_s], _v))

    def bump(self, sem, by):
        self.cnt[sem] = self.cnt.get(sem, 0) + by
        return self.cnt[sem]

    def val(self, sem):
        return self.cnt.get(sem, 0)


def build(phases=("ffn1", "attn", "ffn2")):
    nc = bass.Bass("TRN2", target_bir_lowering=False)

    x_h = nc.dram_tensor("x", [D, NT], F32, kind="ExternalInput").ap()
    w1c = nc.dram_tensor("w1c", [D, 3 * 4096], F32, kind="ExternalInput").ap()
    w2c = nc.dram_tensor("w2c", [3 * 4096, D], F32, kind="ExternalInput").ap()
    wqt = nc.dram_tensor("wqt", [D, D], F32, kind="ExternalInput").ap()
    wut = nc.dram_tensor("wut", [D, D], F32, kind="ExternalInput").ap()
    b1h = nc.dram_tensor("b1h", [128, 96], F32, kind="ExternalInput").ap()
    b2x = nc.dram_tensor("b2x", [128, 24], F32, kind="ExternalInput").ap()
    bqc = nc.dram_tensor("bqc", [128, 8], F32, kind="ExternalInput").ap()
    buc = nc.dram_tensor("buc", [128, 8], F32, kind="ExternalInput").ap()
    gwc = nc.dram_tensor("gwc", [128, 16], F32, kind="ExternalInput").ap()
    y_h = nc.dram_tensor("y", [D, NT], F32, kind="ExternalOutput").ap()

    es = contextlib.ExitStack()
    with es:
        sb = lambda n, s, d: es.enter_context(nc.sbuf_tensor(n, s, d))
        mksem = lambda n: es.enter_context(nc.semaphore(n))

        x_sb = sb("x_sb", [128, NKC * NT], F32)
        ar_ln = sb("ar_ln", [128, 16384], BF16)
        ar_w1 = sb("ar_w1", [128, 16384], BF16)
        ar_w2 = sb("ar_w2", [128, 16384], BF16)
        ar_h = sb("ar_h", [128, 4096], BF16)
        ar_b = sb("ar_b", [128, 8192], BF16)
        p_ex = sb("p_ex", [128, 8192], BF16)
        rzb_sb = sb("rzb_sb", [128, TS], BF16)
        xb_s = rzb_sb
        m0row = sb("m0row", [1, TS], BF16)
        m1row = sb("m1row", [1, TS], BF16)
        ones_col = sb("ones_col", [128, 1], BF16)
        onesLf = sb("onesLf", [1, 128], F32)
        onesLb = sb("onesLb", [1, 128], BF16)
        ident = sb("ident", [128, 128], BF16)
        e2mat = sb("e2mat", [128, 128], BF16)
        sel16 = sb("sel16", [16, 1024], BF16)
        b1hf = sb("b1hf", [128, 96], F32)
        b2sh = sb("b2sh", [128, 8], F32)
        bq_sb = sb("bq_sb", [128, 8], F32)
        bu_sb = sb("bu_sb", [128, 8], F32)
        gw_bf = sb("gw_bf", [128, 16], BF16)
        zcols = sb("zcols", [128, 16], F32)
        zc4a = sb("zc4a", [128, 64], F32)
        zc4b = sb("zc4b", [128, 64], F32)
        rzbf = sb("rzbf", [128, 16], BF16)
        trz = sb("trz", [16, 128], BF16)

        ps_a = es.enter_context(nc.psum_tensor("ps_a", [128, TS], F32))
        ps_b = es.enter_context(nc.psum_tensor("ps_b", [128, TS], F32))
        ps_c = es.enter_context(nc.psum_tensor("ps_c", [128, TS], F32))

        sems = {}
        for s in ("w1a", "w1b", "w2a", "w2b", "wx", "x", "mm", "act", "dve", "gp", "out"):
            sems[s] = mksem("s_" + s)

        sch = Sched()
        X, MM, ACTs, DVE, GP, OUT = "x", "mm", "act", "dve", "gp", "out"

        scaleB = ar_b[:, 6144:7168].bitcast(F32)
        meanB = ar_b[:, 7168:8192].bitcast(F32)
        r1 = meanB[0:1, :]
        r2 = scaleB[0:1, :]
        r3 = m0row

        def lnm_ap(kc):
            return ar_b[:, kc * TS:(kc + 1) * TS]

        def mask_ap(which, strip):
            off = 8 * TS + which * NT
            return ar_b[:, off + strip * TS: off + (strip + 1) * TS]

        def q_ap(kc, t0, w):
            return ar_b[:, kc * 1024 + t0: kc * 1024 + t0 + w]

        def ln_ap(kc, t0, w):
            half = t0 // 1024
            tt = t0 - half * 1024
            base = half * 8192 + kc * 1024 + tt
            return ar_ln[:, base: base + w]

        def prow_ap(a):
            if a < 8:
                return ar_w2[:, a * NT:(a + 1) * NT]
            if a < 12:
                return ar_ln[:, (a - 8) * NT:(a - 7) * NT]
            return p_ex[:, (a - 12) * NT:(a - 11) * NT]

        def w1blk(slot, kc, hc):
            o = slot * 8192 + kc * 1024 + hc * 128
            return ar_w1[:, o:o + 128]

        def w2blk(slot, hc, dc):
            o = slot * 8192 + hc * 1024 + dc * 128
            return ar_w2[:, o:o + 128]

        def wqtblk(kc, dc):
            o = kc * 1024 + dc * 128
            return ar_w1[:, o:o + 128]

        def wutblk(kc, dc):
            o = 8192 + kc * 1024 + dc * 128
            return ar_w1[:, o:o + 128]

        def qq_ap(posc, lo64):
            return ar_h[lo64:lo64 + 64, 128 * posc: 128 * posc + 128]

        def qT_ap(cp, w=64):
            return ar_h[:, 2048 + 64 * cp: 2048 + 64 * cp + w]

        def ofm_ap(ec, lo, hi):
            return ar_h[lo:hi, 3072 + 128 * ec: 3072 + 128 * ec + 128]

        def g_init(g, S):
            n = 0
            g.memset(ones_col[:, :], 1.0)
            g.memset(onesLf[:, :], 1.0)
            g.memset(onesLb[:, :], 1.0)
            g.memset(ident[:, :], 0.0).then_inc(S[GP], 1); n += 1
            g.wait_ge(S[GP], n)
            g.affine_select(out=ident[:, :], in_=ident[:, :], compare_op=ALU.not_equal,
                            fill=1.0, base=0, pattern=[[-1, 128]],
                            channel_multiplier=1).then_inc(S[GP], 1); n += 1
            g.memset(e2mat[:, :], 0.0).then_inc(S[GP], 1); n += 1
            for base in (0, 64, -64):
                g.wait_ge(S[GP], n)
                g.affine_select(out=e2mat[:, :], in_=e2mat[:, :], compare_op=ALU.not_equal,
                                fill=1.0, base=base, pattern=[[-1, 128]],
                                channel_multiplier=1).then_inc(S[GP], 1); n += 1
            g.memset(sel16[:, :], 0.0).then_inc(S[GP], 1); n += 1
            g.wait_ge(S[GP], n)
            g.affine_select(
                out=sel16[:, :].rearrange("p (c d) -> p c d", c=16),
                in_=sel16[:, :].rearrange("p (c d) -> p c d", c=16),
                compare_op=ALU.not_equal, fill=1.0, base=0,
                pattern=[[-1, 16], [0, 64]], channel_multiplier=1,
            ).then_inc(S[GP], 1); n += 1
        sch.emit("gpsimd", g_init)
        tok_gpinit = sch.bump(GP, 8)

        def sy_load(sy, S):
            sy.dma_start(
                out=x_sb[:, :].rearrange("p (kc t) -> p kc t", kc=NKC),
                in_=x_h.rearrange("(kc p) t -> p kc t", p=128),
            ).then_inc(S[X], 16)
            for src, dst in ((b1h, b1hf), (bqc, bq_sb), (buc, bu_sb)):
                sy.dma_start(out=dst[:, :], in_=src[:, :]).then_inc(S[X], 16)
            sy.dma_start(out=b2sh[:, :], in_=b2x[:, 16:24]).then_inc(S[X], 16)
            sy.dma_start(out=meanB[:, 0:16], in_=b2x[:, 0:16]).then_inc(S[X], 16)
        sch.emit("sync", sy_load)
        tok_xload = sch.bump(X, 16 * 6)

        def g_gw(g, S):
            g.dma_start(out=gw_bf[:, :], in_=gwc[:, :]).then_inc(S["wx"], 16)
        sch.emit("gpsimd", g_gw)
        tok_gw = ("wx", sch.bump("wx", 16))

        sch.wait("scalar", X, tok_xload)
        def sc_bias(s, S):
            s.mul(b1hf[:, :], b1hf[:, :], 0.5).then_inc(S[ACTs], 1)
        sch.emit("scalar", sc_bias)
        tok_b1 = sch.bump(ACTs, 1)

        sch.wait("vector", X, tok_xload)
        base_dve = sch.val(DVE)
        def dv_bias(v, S, _b=base_dve):
            v.tensor_add(b2sh[:, :], b2sh[:, :], meanB[:, 0:8]).then_inc(S[DVE], 1)
            v.wait_ge(S[DVE], _b + 1)
            v.tensor_add(b2sh[:, :], b2sh[:, :], meanB[:, 8:16]).then_inc(S[DVE], 1)
            v.wait_ge(S[DVE], _b + 2)
            v.tensor_scalar_mul(b2sh[:, :], b2sh[:, :], 0.5).then_inc(S[DVE], 1)
        sch.emit("vector", dv_bias)
        tok_b2 = sch.bump(DVE, 3)

        wready = {}

        def issue_w1(sup, slot, wait_mm, arena_name):
            sch.wait("gpsimd", MM, wait_mm)
            wsem = "w1a" if slot == 0 else "w1b"
            def f(g, S, _sup=sup, _slot=slot, _ws=wsem):
                g.dma_start(
                    out=ar_w1[:, _slot * 8192:(_slot + 1) * 8192].rearrange(
                        "p (kc h) -> p kc h", kc=NKC),
                    in_=w1c[:, _sup * HE:(_sup + 1) * HE].rearrange(
                        "(kc p) h -> p kc h", p=128),
                ).then_inc(S[_ws], 16)
            sch.emit("gpsimd", f)
            return (wsem, sch.bump(wsem, 16))

        def issue_w2(sup, slot, wait_mm):
            sch.wait("gpsimd", MM, wait_mm)
            wsem = "w2a" if slot == 0 else "w2b"
            def f(g, S, _sup=sup, _slot=slot, _ws=wsem):
                g.dma_start(
                    out=ar_w2[:, _slot * 8192:(_slot + 1) * 8192].rearrange(
                        "p (hc d) -> p hc d", hc=NKC),
                    in_=w2c[_sup * HE:(_sup + 1) * HE, :].rearrange(
                        "(hc p) d -> p hc d", p=128),
                ).then_inc(S[_ws], 16)
            sch.emit("gpsimd", f)
            return (wsem, sch.bump(wsem, 16))

        def issue_wqut(which, wait_mm):
            sch.wait("gpsimd", MM, wait_mm)
            src = wqt if which == "q" else wut
            off = 0 if which == "q" else 8192
            def f(g, S, _src=src, _off=off):
                g.dma_start(
                    out=ar_w1[:, _off:_off + 8192].rearrange("p (kc d) -> p kc d", kc=NKC),
                    in_=_src.rearrange("(kc p) d -> p kc d", p=128),
                ).then_inc(S["wx"], 16)
            sch.emit("gpsimd", f)
            return ("wx", sch.bump("wx", 16))

        pexf = p_ex[:, 0:4096].bitcast(F32)
        mean2 = pexf[:, 1024:1536]
        scale2 = pexf[:, 1536:2048]
        xb2 = p_ex[:, 4096:4608]
        gate_r1 = p_ex[:, 4608:5632].bitcast(F32)[0:1, :]
        gate_r2 = p_ex[:, 5632:6656].bitcast(F32)[0:1, :]

        def make_ln_stages(is_ffn, xwaits, statps, bcps, ext=None):
            ext = ext if ext is not None else {}
            ln_tok = [0] * NSTRIP
            mask_tok = [0] * NSTRIP
            s1ev = [0] * NSTRIP
            s2ev = [0] * NSTRIP
            statmm = [0] * NSTRIP
            sqtok = [0] * NSTRIP
            bcev = [0] * NSTRIP

            def tiles(st):
                if st % 2 == 0:
                    return meanB, scaleB
                return mean2, scale2

            def scratch(st):
                return xb_s if st % 2 == 0 else xb2

            def stage_stats(st):
                t0 = st * TS
                xb = scratch(st)
                sp1 = statps[0]
                sp2 = statps[1]
                mB, sB = tiles(st)
                r1, r2 = mB[0:1, :], sB[0:1, :]
                sch.wait("vector", DVE, max(xwaits[st], tok_b2))
                sch.wait("scalar", DVE, max(xwaits[st], tok_b2))
                prev_mm = statmm[st - 1] if st > 0 else 0
                sch.wait("vector", X, tok_xload)
                if st >= 2:
                    sch.wait("vector", DVE, ln_tok[st - 2])
                for kc in range(NKC):
                    sch.wait("vector", MM, prev_mm)
                    sch.wait("vector", ACTs, sqtok[st - 2] if st >= 2 else 0)
                    def fcast(v, S, _kc=kc, _t0=t0, _xb=xb):
                        v.tensor_copy(
                            _xb[:, :], x_sb[:, _kc * NT + _t0: _kc * NT + _t0 + TS]
                        ).then_inc(S[DVE], 1)
                    sch.emit("vector", fcast)
                    ctok = sch.bump(DVE, 1)
                    sch.wait("tensor", GP, tok_gpinit)
                    sch.wait("tensor", DVE, max(ctok, s1ev[st - 1] if st >= 1 else 0))
                    def fmm(t, S, _kc=kc, _sp=sp1, _xb=xb):
                        t.matmul(_sp, ones_col[:, :], _xb[:, :],
                                 start=(_kc == 0), stop=(_kc == NKC - 1)).then_inc(S[MM], 1)
                    sch.emit("tensor", fmm)
                    prev_mm = sch.bump(MM, 1)
                s1mm = prev_mm
                sch.wait("vector", MM, s1mm)
                def fs1(v, S, _sp=sp1, _r=r1):
                    v.tensor_scalar_mul(_r, _sp, 1.0 / D).then_inc(S[DVE], 1)
                sch.emit("vector", fs1)
                s1ev[st] = sch.bump(DVE, 1)
                for kc in range(NKC):
                    sch.wait("scalar", MM, prev_mm)
                    sch.wait("scalar", X, tok_xload)
                    def fsq(s, S, _kc=kc, _t0=t0, _xb=xb):
                        s.activation(_xb[:, :],
                                     x_sb[:, _kc * NT + _t0: _kc * NT + _t0 + TS],
                                     AF.Square).then_inc(S[ACTs], 1)
                    sch.emit("scalar", fsq)
                    sqtok[st] = sch.bump(ACTs, 1)
                    sch.wait("tensor", ACTs, sqtok[st])
                    sch.wait("tensor", DVE, s2ev[st - 1] if st >= 1 else 0)
                    def fmm2(t, S, _kc=kc, _sp=sp2, _xb=xb):
                        t.matmul(_sp, ones_col[:, :], _xb[:, :],
                                 start=(_kc == 0), stop=(_kc == NKC - 1)).then_inc(S[MM], 1)
                    sch.emit("tensor", fmm2)
                    prev_mm = sch.bump(MM, 1)
                statmm[st] = prev_mm
                sch.wait("vector", MM, prev_mm)
                def fs2(v, S, _sp=sp2, _r=r2):
                    v.tensor_scalar(out=_r, in0=_sp, scalar1=1.0 / D, scalar2=EPS,
                                    op0=ALU.mult, op1=ALU.add).then_inc(S[DVE], 1)
                sch.emit("vector", fs2)
                s2ev[st] = sch.bump(DVE, 1)

            def stage_post(st):
                t0 = st * TS
                mB, sB = tiles(st)
                r1, r2 = mB[0:1, :], sB[0:1, :]
                bA = bcps[0]
                bB = bcps[1]
                sch.wait("vector", DVE, s2ev[st])
                base_r = sch.val(DVE)
                def frow2(v, S, _r1=r1, _r2=r2, _b=base_r):
                    v.wait_ge(S[DVE], _b)
                    v.tensor_mul(r3[:, :], _r1, _r1).then_inc(S[DVE], 1)
                    v.wait_ge(S[DVE], _b + 1)
                    v.scalar_tensor_tensor(out=_r2, in0=r3[:, :], scalar=-1.0,
                                           in1=_r2, op0=ALU.mult,
                                           op1=ALU.add).then_inc(S[DVE], 1)
                sch.emit("vector", frow2)
                rowb = sch.bump(DVE, 2)
                sch.wait("scalar", DVE, rowb)
                def frow3(s, S, _r2=r2):
                    s.activation(_r2, _r2, AF.Sqrt).then_inc(S[ACTs], 1)
                sch.emit("scalar", frow3)
                rowc = sch.bump(ACTs, 1)
                sch.wait("vector", ACTs, rowc)
                def frow4(v, S, _r2=r2):
                    v.reciprocal(_r2, _r2).then_inc(S[DVE], 1)
                sch.emit("vector", frow4)
                rowd = sch.bump(DVE, 1)
                sch.wait("tensor", DVE, rowd)
                sch.wait("tensor", DVE, ext.get("bc_wait", 0))
                if st >= 1:
                    sch.wait("tensor", DVE,
                             mask_tok[st - 1] if is_ffn else bcev[st - 1])
                def fbc1(t, S, _r=r1, _bp=bA):
                    t.matmul(_bp, onesLf[:, :], _r, start=True, stop=True).then_inc(S[MM], 1)
                sch.emit("tensor", fbc1)
                bct1 = sch.bump(MM, 1)
                def fbc2(t, S, _r=r2, _bp=bB):
                    t.matmul(_bp, onesLf[:, :], _r, start=True, stop=True).then_inc(S[MM], 1)
                sch.emit("tensor", fbc2)
                bct2 = sch.bump(MM, 1)
                sch.wait("vector", MM, bct2)
                base_e = sch.val(DVE)
                def fbe(v, S, _mB=mB, _sB=sB, _bA=bA, _bB=bB, _b=base_e):
                    v.tensor_copy(_mB[:, :], _bA).then_inc(S[DVE], 1)
                    v.wait_ge(S[DVE], _b + 1)
                    v.tensor_copy(_sB[:, :], _bB).then_inc(S[DVE], 1)
                sch.emit("vector", fbe)
                bcev[st] = sch.bump(DVE, 2)
                xb = scratch(st)
                sch.wait("vector", DVE, bcev[st])
                sch.wait("vector", ACTs, sqtok[st])
                for kc in range(NKC):
                    base_n = sch.val(DVE)
                    def fn(v, S, _kc=kc, _t0=t0, _b=base_n, _xb=xb, _mB=mB, _sB=sB):
                        v.wait_ge(S[DVE], _b)
                        v.tensor_sub(_xb[:, :],
                                     x_sb[:, _kc * NT + _t0: _kc * NT + _t0 + TS],
                                     _mB[:, :]).then_inc(S[DVE], 1)
                        v.wait_ge(S[DVE], _b + 1)
                        v.tensor_mul(ln_ap(_kc, _t0, TS), _xb[:, :],
                                     _sB[:, :]).then_inc(S[DVE], 1)
                    sch.emit("vector", fn)
                    ln_tok[st] = sch.bump(DVE, 2)


            def stage_gate(st):
                t0 = st * TS
                r1, r2 = gate_r1, gate_r2
                bA = bcps[0]
                gevac = 0
                for e in range(2):
                    gp_ps = bcps[e]
                    for kc in range(NKC):
                        sch.wait("tensor", *tok_gw)
                        sch.wait("tensor", DVE, ln_tok[st])
                        sch.wait("tensor", DVE, ext.get("bc_wait", 0))
                        def fg(t, S, _e=e, _kc=kc, _t0=t0, _sp=gp_ps):
                            t.matmul(_sp[0:1, :],
                                     gw_bf[:, _e * 8 + _kc:_e * 8 + _kc + 1],
                                     ln_ap(_kc, _t0, TS),
                                     start=(_kc == 0), stop=(_kc == NKC - 1)
                                     ).then_inc(S[MM], 1)
                        sch.emit("tensor", fg)
                        gtok = sch.bump(MM, 1)
                    sch.wait("vector", MM, gtok)
                    sch.wait("vector", DVE, ln_tok[st])
                    dstrow = r1 if e == 0 else r2
                    def fge(v, S, _d=dstrow, _sp=gp_ps):
                        v.tensor_copy(_d, _sp[0:1, :]).then_inc(S[DVE], 1)
                    sch.emit("vector", fge)
                    gevac = sch.bump(DVE, 1)
                sch.wait("vector", DVE, gevac)
                base_mr = sch.val(DVE)
                def fmrow(v, S, _r1=r1, _r2=r2, _b=base_mr):
                    v.tensor_tensor(m0row[:, :], _r1, _r2,
                                    ALU.is_ge).then_inc(S[DVE], 1)
                    v.wait_ge(S[DVE], _b + 1)
                    v.tensor_tensor(m1row[:, :], _r1, _r2,
                                    ALU.is_lt).then_inc(S[DVE], 1)
                sch.emit("vector", fmrow)
                mrow_tok = sch.bump(DVE, 2)
                sch.wait("tensor", DVE, mrow_tok)
                bB2 = bcps[1]
                def fmb(t, S, _bp=bA, _bp2=bB2):
                    t.matmul(_bp, onesLb[:, :], m0row[:, :],
                             start=True, stop=True).then_inc(S[MM], 1)
                    t.matmul(_bp2, onesLb[:, :], m1row[:, :],
                             start=True, stop=True).then_inc(S[MM], 1)
                sch.emit("tensor", fmb)
                mbt = sch.bump(MM, 2)
                sch.wait("vector", MM, mbt)
                sch.wait("vector", DVE, ext.get("all_ln", 0))
                base_me = sch.val(DVE)
                def fmbe(v, S, _st=st, _bp=bA, _bp2=bB2, _b=base_me):
                    v.tensor_copy(mask_ap(0, _st), _bp).then_inc(S[DVE], 1)
                    v.wait_ge(S[DVE], _b + 1)
                    v.tensor_copy(mask_ap(1, _st), _bp2).then_inc(S[DVE], 1)
                sch.emit("vector", fmbe)
                mask_tok[st] = sch.bump(DVE, 2)

            stats_fns = [lambda _st=st: stage_stats(_st) for st in range(NSTRIP)]
            post_fns = [lambda _st=st: stage_post(_st) for st in range(NSTRIP)]
            gate_fns = [lambda _st=st: stage_gate(_st) for st in range(NSTRIP)]
            return stats_fns, post_fns, gate_fns, ln_tok, mask_tok

        def emit_ln(is_ffn, xwaits, statps, bcps):
            sf, pf, gf, ln_tok, mask_tok = make_ln_stages(is_ffn, xwaits, statps, bcps)
            for st in range(NSTRIP):
                sf[st]()
                if st > 0:
                    pf[st - 1]()
            pf[NSTRIP - 1]()
            if is_ffn:
                for st in range(NSTRIP):
                    gf[st]()
            return ln_tok, mask_tok

        def emit_ffn(xwait_dve, wst, ps_d, ps_w, ps_w2, ps_e, ps_f):
            ext = {"bc_wait": 0}
            sf, pf, gf, ln_tok, mask_tok = make_ln_stages(
                True, [xwait_dve] * NSTRIP,
                (ps_w[:, :], ps_w2[:, :]), (ps_d[:, :], ps_c[:, :]), ext)
            h_relu_tok = [0] * NKC
            stt_tok = [[0] * NSTRIP for _ in range(NKC)]
            mm1_banks = (ps_a, ps_b, ps_e)
            mm2_banks = (ps_c, ps_d, ps_f)
            ps1_free = [0, 0, 0]
            ps2_free = [0, 0, 0]
            prev_mm1 = [0]
            lnm_tok = [None]

            pending_lnm = []

            def build_lnm_pieces(st, klass, wait_mm):
                which = 0 if klass == 1 else 1
                def piece(kc):
                    sch.wait("vector", DVE, max(ln_tok[st], mask_tok[st]))
                    sch.wait("vector", MM, wait_mm)
                    def fm(v, S, _kc=kc, _st=st, _w=which):
                        v.tensor_mul(lnm_ap(_kc), ln_ap(_kc, _st * TS, TS),
                                     mask_ap(_w, _st)).then_inc(S[DVE], 1)
                    sch.emit("vector", fm)
                    lnm_tok[0] = sch.bump(DVE, 1)
                return [lambda _kc=kc: piece(_kc) for kc in range(NKC)]

            def build_lnm(st, klass, wait_mm):
                for p in build_lnm_pieces(st, klass, wait_mm):
                    p()
                return lnm_tok[0]

            def do_mm1(sup, st, klass, slot):
                if wst["w1"][sup]:
                    sch.wait("tensor", *wst["w1"][sup])
                if wst["w2"][sup]:
                    sch.wait("tensor", *wst["w2"][sup])
                if klass == 0:
                    rhs_sem, rhs_tok = DVE, ln_tok[st]
                    rhs = lambda kc, _t0=st * TS: ln_ap(kc, _t0, TS)
                else:
                    rhs_sem, rhs_tok = DVE, lnm_tok[0]
                    rhs = lambda kc: lnm_ap(kc)
                for hc in range(NKC):
                    pst = mm1_banks[hc % 3]
                    sch.wait("tensor", ACTs, ps1_free[hc % 3])
                    sch.wait("tensor", rhs_sem, rhs_tok)
                    for kc in range(NKC):
                        def f1(t, S, _kc=kc, _hc=hc, _slot=slot, _pst=pst, _rhs=rhs):
                            ins = t.matmul(_pst[:, :], w1blk(_slot, _kc, _hc),
                                           _rhs(_kc),
                                           start=(_kc == 0), stop=(_kc == NKC - 1))
                            if _kc == NKC - 1:
                                ins.then_inc(S[MM], 1)
                        sch.emit("tensor", f1)
                    g1 = sch.bump(MM, 1)
                    sch.wait("scalar", MM, g1)
                    sch.wait("scalar", ACTs, tok_b1)
                    def fr(s, S, _hc=hc, _sup=sup, _pst=pst):
                        s.activation(ar_h[:, _hc * TS:(_hc + 1) * TS], _pst[:, :],
                                     AF.Relu,
                                     bias=b1hf[:, _sup * 8 + _hc:_sup * 8 + _hc + 1],
                                     scale=0.5).then_inc(S[ACTs], 1)
                    sch.emit("scalar", fr)
                    h_relu_tok[hc] = sch.bump(ACTs, 1)
                    ps1_free[hc % 3] = h_relu_tok[hc]
                prev_mm1[0] = sch.val(MM)
                if klass > 0:
                    nst, nkl = (st + 1, klass) if st < NSTRIP - 1 else \
                        (0, (sup + 1) // 4 if sup + 1 < NSUP else None)
                    if nkl:
                        pending_lnm.extend(build_lnm_pieces(nst, nkl, prev_mm1[0]))
                    else:
                        lnm_tok[0] = None

            def do_mm2(sup, st, slot, mm2_extra):
                h_all = max(h_relu_tok)
                for dc in range(NKC):
                    pst = mm2_banks[dc % 3]
                    sch.wait("tensor", DVE, max(ps2_free[dc % 3], mm2_extra))
                    sch.wait("tensor", ACTs, h_all)
                    for hc in range(NKC):
                        def f2(t, S, _hc=hc, _dc=dc, _slot=slot, _pst=pst):
                            ins = t.matmul(_pst[:, :], w2blk(_slot, _hc, _dc),
                                           ar_h[:, _hc * TS:(_hc + 1) * TS],
                                           start=(_hc == 0), stop=(_hc == NKC - 1))
                            if _hc == NKC - 1:
                                ins.then_inc(S[MM], 1)
                        sch.emit("tensor", f2)
                    g2 = sch.bump(MM, 1)
                    sch.wait("vector", MM, g2)
                    sch.wait("vector", DVE, tok_b2)
                    o0 = dc * NT + st * TS
                    if sup == 0:
                        def fx(v, S, _dc=dc, _o0=o0, _pst=pst):
                            xsl = x_sb[:, _o0:_o0 + TS]
                            v.scalar_tensor_tensor(
                                out=xsl, in0=_pst[:, :],
                                scalar=b2sh[:, _dc:_dc + 1], in1=xsl,
                                op0=ALU.add, op1=ALU.add).then_inc(S[DVE], 1)
                    else:
                        def fx(v, S, _o0=o0, _pst=pst):
                            xsl = x_sb[:, _o0:_o0 + TS]
                            v.tensor_add(xsl, xsl, _pst[:, :]).then_inc(S[DVE], 1)
                    sch.emit("vector", fx)
                    stt_tok[dc][st] = sch.bump(DVE, 1)
                    ps2_free[dc % 3] = stt_tok[dc][st]
                    if pending_lnm:
                        pending_lnm.pop(0)()
                while pending_lnm:
                    pending_lnm.pop(0)()

            sf[0](); sf[1](); pf[0]()
            for st in range(NSTRIP):
                do_mm1(0, st, 0, 0)
                ext["bc_wait"] = sch.val(DVE)
                if st + 2 < NSTRIP:
                    sf[st + 2]()
                if st + 1 < NSTRIP:
                    pf[st + 1]()
                if st + 1 == NSTRIP:
                    pass
                do_mm2(0, st, 0, sch.val(DVE))
                ext["bc_wait"] = sch.val(DVE)
                if st == 2:
                    ext["all_ln"] = max(ln_tok)
                if st >= 2:
                    gf[st - 2]()
            for st in (2, 3):
                ext["bc_wait"] = sch.val(DVE)
                gf[st]()
            ps2_free[0] = ps2_free[1] = ps2_free[2] = sch.val(DVE)
            wst["sup_last_mm"][0] = sch.val(MM)
            if wst["issue_next"] is not None:
                wst["issue_next"](0)
            for sup in range(1, NSUP):
                klass = sup // 4
                slot = sup % 2
                if klass > 0 and lnm_tok[0] is None:
                    lnm_tok[0] = build_lnm(0, klass, prev_mm1[0])
                for st in range(NSTRIP):
                    do_mm1(sup, st, klass, slot)
                    do_mm2(sup, st, slot, 0)
                wst["sup_last_mm"][sup] = sch.val(MM)
                if wst["issue_next"] is not None:
                    wst["issue_next"](sup)
            xdve = max(max(row) for row in stt_tok)
            xtok = [stt_tok[dc][NSTRIP - 1] for dc in range(NKC)]
            return xdve, xtok, sch.val(MM)

        def emit_attn(xwait_dve, s0, s1, d_att, ps_t, ps_rz, hooks):
            ln_tok, _ = emit_ln(False, [xwait_dve] * NSTRIP,
                                (s0[0:1, :], s1[0:1, :]),
                                (ps_c[:, :], d_att[:, :]))
            ln_done = sch.val(DVE)
            av_of_strip = [(ps_a, 0), (ps_b, 0), (ps_c, 64), (d_att, 64)]
            sregs = (s0, s1)
            stg = (s0, s1)
            stg_u = (ps_a, ps_b)

            state = {
                "exp_tok": {},
                "avtok": {},
                "avfree": 0,
                "stage_tok": {},
                "stage_ev": 0,
                "ufree": 0,
                "qT_tok": {},
                "zsum_tok": 0,
                "last_qT": 0,
                "trz_evac": 0,
                "u_last": 0,
            }

            def staging(i, q_ready):
                qoff = (i % 8) * 128
                act_prev = state["stage_ev"]
                for b4 in range(4):
                    ps = stg[b4 % 2]
                    sch.wait("tensor", ACTs,
                             state["exp_tok"].get((i - 1, 15, 2 + b4 % 2), 0))
                    sch.wait("tensor", DVE, max(q_ready, act_prev))
                    for j in range(4):
                        c = c_of_pos(b4 * 4 + j)
                        hb = (c % 2) * 64
                        def fqq(t, S, _c=c, _hb=hb, _q=qoff, _j=j, _ps=ps):
                            ins = t.matmul(_ps[:, _j * 128:(_j + 1) * 128],
                                           e2mat[_hb:_hb + 64, :],
                                           q_ap(_c // 2, _q, 128)[_hb:_hb + 64, :],
                                           start=(_j == 0), stop=(_j == 3))
                            if _j == 3:
                                ins.then_inc(S[MM], 1)
                        sch.emit("tensor", fqq)
                    gqq = sch.bump(MM, 1)
                    sch.wait("vector", MM, gqq)
                    def fqqe(v, S, _b4=b4, _ps=ps):
                        v.tensor_copy(ar_h[:, _b4 * TS:(_b4 + 1) * TS],
                                      _ps[:, :]).then_inc(S[DVE], 1)
                    sch.emit("vector", fqqe)
                    act_prev = sch.bump(DVE, 1)
                state["stage_ev"] = act_prev
                state["stage_tok"][i] = act_prev

            def transposes(i):
                sch.wait("tensor", DVE, state["stage_tok"][i])
                sch.wait("tensor", ACTs, state["trz_evac"])
                for cp in range(16):
                    def ftr(t, S, _cp=cp):
                        ins = t.matmul(ps_t[:, _cp * 64:(_cp + 1) * 64],
                                       qq_ap(pos_of_c(_cp), 0),
                                       ident[0:64, 0:64],
                                       start=(_cp == 0), stop=(_cp == 15),
                                       is_transpose=True)
                        if _cp == 15:
                            ins.then_inc(S[MM], 1)
                    sch.emit("tensor", ftr)
                gtr = sch.bump(MM, 1)
                sch.wait("vector", MM, gtr)
                def ftre(v, S):
                    v.tensor_copy(ar_h[:, 2048:3072], ps_t[:, 0:1024]).then_inc(S[DVE], 1)
                sch.emit("vector", ftre)
                state["qT_tok"][i] = sch.bump(DVE, 1)
                state["last_qT"] = state["qT_tok"][i]

            def sc_row(i, a):
                pa = pos_of_c(a)
                h = (a % 2) * 64
                for q4 in range(4):
                    qi = a * 4 + q4
                    sreg = sregs[qi % 2]
                    sch.wait("tensor", DVE, state["stage_tok"][i])
                    sch.wait("tensor", ACTs,
                             max(state["exp_tok"].get((i, a, q4 - 2), 0),
                                 state["exp_tok"].get((i, a - 1, q4 + 2), 0)))
                    def fs(t, S, _pa=pa, _h=h, _q4=q4, _sr=sreg):
                        t.matmul(_sr[:, :],
                                 qq_ap(_pa, _h),
                                 ar_h[_h:_h + 64, _q4 * TS:(_q4 + 1) * TS],
                                 start=True, stop=True).then_inc(S[MM], 1)
                    sch.emit("tensor", fs)
                    gsc = sch.bump(MM, 1)
                    sch.wait("scalar", MM, gsc)
                    sch.wait("scalar", MM, state["avtok"].get((i - 1, a), 0))
                    sch.wait("scalar", DVE, state["zsum_tok"])
                    zdst = zc4a if i % 2 == 0 else zc4b
                    def fexp(s, S, _a=a, _q4=q4, _sr=sreg, _z=zdst):
                        s.activation(prow_ap(_a)[:, _q4 * TS:(_q4 + 1) * TS],
                                     _sr[:, :], AF.Exp, scale=SCALE,
                                     accum_out=_z[:, _a * 4 + _q4:_a * 4 + _q4 + 1]
                                     ).then_inc(S[ACTs], 1)
                    sch.emit("scalar", fexp)
                    state["exp_tok"][(i, a, q4)] = sch.bump(ACTs, 1)

            def av_row(i, a):
                if a == 0:
                    sch.wait("tensor", DVE, state["avfree"])
                sch.wait("tensor", ACTs, state["exp_tok"][(i, a, 3)])
                sch.wait("tensor", DVE, state["qT_tok"][i])
                for sg in range(4):
                    bank, lo = av_of_strip[sg]
                    tpos = (0, 0) if lo == 0 else (0, 64)
                    def fav(t, S, _a=a, _sg=sg, _bk=bank, _lo=lo, _tp=tpos):
                        ins = t.matmul(_bk[_lo:_lo + 64, :],
                                       qT_ap(_a),
                                       prow_ap(_a)[:, _sg * TS:(_sg + 1) * TS],
                                       start=(_a == 0), stop=(_a == 15),
                                       tile_position=_tp)
                        if _sg == 3:
                            ins.then_inc(S[MM], 1)
                    sch.emit("tensor", fav)
                state["avtok"][(i, a)] = sch.bump(MM, 1)

            def make_tail_stages(i):
                act_base = sch.val(ACTs)
                zsrc = zc4a if i % 2 == 0 else zc4b
                rzbev = [0]
                avev = [0]
                uev = [0, 0]

                def t_rz():
                    sch.wait("vector", ACTs, state["exp_tok"][(i, 15, 3)])
                    sch.wait("vector", DVE, sch.val(DVE))
                    base_z = sch.val(DVE)
                    def fz(v, S, _z=zsrc, _b=base_z):
                        v.tensor_reduce(
                            zcols[:, :].rearrange("p (a o) -> p a o", o=1),
                            _z[:, :].rearrange("p (a q) -> p a q", q=4),
                            mybir.AxisListType.X, ALU.add).then_inc(S[DVE], 1)
                        v.wait_ge(S[DVE], _b + 1)
                        v.reciprocal(zcols[:, :], zcols[:, :]).then_inc(S[DVE], 1)
                        v.wait_ge(S[DVE], _b + 2)
                        v.tensor_copy(rzbf[:, :], zcols[:, :]).then_inc(S[DVE], 1)
                    sch.emit("vector", fz)
                    state["zsum_tok"] = sch.bump(DVE, 3)
                    sch.wait("tensor", DVE, max(state["zsum_tok"], state["last_qT"]))
                    def frzt(t, S):
                        t.matmul(ps_t[0:16, 0:128], rzbf[:, :], ident[:, :],
                                 start=True, stop=True,
                                 is_transpose=True).then_inc(S[MM], 1)
                    sch.emit("tensor", frzt)
                    grzt = sch.bump(MM, 1)
                    sch.wait("scalar", MM, grzt)
                    def frzte(s, S):
                        s.copy(trz[:, :], ps_t[0:16, 0:128]).then_inc(S[ACTs], 1)
                    sch.emit("scalar", frzte)
                    state["trz_evac"] = sch.bump(ACTs, 1)

                def mk_strip(sg):
                    def t_strip():
                        bk, lo = av_of_strip[sg]
                        tpos = (0, 0) if lo == 0 else (0, 64)
                        sch.wait("tensor", ACTs, max(state["trz_evac"], act_base))
                        sch.wait("tensor", DVE, rzbev[0])
                        for j in range(4):
                            cj = c_of_pos(sg * 4 + j)
                            def frzb(t, S, _cj=cj, _j=j, _lo=lo, _tp=tpos):
                                ins = t.matmul(
                                    ps_rz[_lo:_lo + 64, _j * 128:(_j + 1) * 128],
                                    sel16[:, _cj * 64:(_cj + 1) * 64],
                                    trz[:, :],
                                    start=(_j == 0), stop=(_j == 3),
                                    tile_position=_tp)
                                if _j == 3:
                                    ins.then_inc(S[MM], 1)
                            sch.emit("tensor", frzb)
                        grzb = sch.bump(MM, 1)
                        sch.wait("vector", MM, grzb)
                        sch.wait("vector", DVE, avev[0])
                        def frzbe(v, S, _lo=lo):
                            v.tensor_copy(rzb_sb[_lo:_lo + 64, :],
                                          ps_rz[_lo:_lo + 64, :]).then_inc(S[DVE], 1)
                        sch.emit("vector", frzbe)
                        rzbev[0] = sch.bump(DVE, 1)
                        sch.wait("vector", MM, state["avtok"][(i, 15)])
                        sch.wait("vector", DVE, rzbev[0])
                        def fave(v, S, _sg=sg, _lo=lo, _bk=bk):
                            for j in range(4):
                                c = c_of_pos(_sg * 4 + j)
                                ins = v.tensor_mul(
                                    ofm_ap(c // 2, _lo, _lo + 64),
                                    _bk[_lo:_lo + 64, j * 128:(j + 1) * 128],
                                    rzb_sb[_lo:_lo + 64, j * 128:(j + 1) * 128])
                                if j == 3:
                                    ins.then_inc(S[DVE], 1)
                        sch.emit("vector", fave)
                        avev[0] = sch.bump(DVE, 1)
                        if sg == 3:
                            state["avfree"] = avev[0]
                    return t_strip

                def mk_u(pair):
                    def t_u():
                        if pair == 0:
                            sch.wait("tensor", *wready["wu"])
                        for dc in (pair * 2, pair * 2 + 1):
                            ups = stg_u[dc % 2]
                            sch.wait("tensor", DVE, max(avev[0], uev[dc % 2]))
                            sch.wait("tensor", ACTs, act_base)
                            for ec in range(NKC):
                                def fu(t, S, _ec=ec, _dc=dc, _u=ups):
                                    ins = t.matmul(_u[:, 0:128], wutblk(_ec, _dc),
                                                   ar_h[:, 3072 + 128 * _ec:
                                                        3072 + 128 * _ec + 128],
                                                   start=(_ec == 0),
                                                   stop=(_ec == NKC - 1))
                                    if _ec == NKC - 1:
                                        ins.then_inc(S[MM], 1)
                                sch.emit("tensor", fu)
                            gu = sch.bump(MM, 1)
                            sch.wait("vector", MM, gu)
                            o0 = dc * NT + i * 128
                            def fux(v, S, _dc=dc, _o0=o0, _u=ups):
                                xsl = x_sb[:, _o0:_o0 + 128]
                                v.scalar_tensor_tensor(
                                    out=xsl, in0=_u[:, 0:128],
                                    scalar=bu_sb[:, _dc:_dc + 1], in1=xsl,
                                    op0=ALU.add, op1=ALU.add).then_inc(S[DVE], 1)
                            sch.emit("vector", fux)
                            uev[dc % 2] = sch.bump(DVE, 1)
                        if pair == 3:
                            state["ufree"] = max(uev)
                            state["avfree"] = max(state["avfree"], max(uev))
                            state["u_last"] = max(uev)
                    return t_u

                return [t_rz] + [mk_strip(sg) for sg in range(4)] + \
                    [mk_u(p) for p in range(4)]

            def qproj(half):
                t0h = half * 1024
                qev = [0, 0]
                for dc in range(NKC):
                    for st2 in range(2):
                        t0 = t0h + st2 * TS
                        pst = stg_u[(dc * 2 + st2) % 2]
                        sch.wait("tensor", *wready["wq"])
                        sch.wait("tensor", DVE, max(ln_done, state["avfree"],
                                                    state["ufree"]))
                        sch.wait("tensor", DVE, qev[(dc * 2 + st2) % 2])
                        for kc in range(NKC):
                            def fq(t, S, _kc=kc, _dc=dc, _t0=t0, _pst=pst):
                                ins = t.matmul(_pst[:, :], wqtblk(_kc, _dc),
                                               ln_ap(_kc, _t0, TS),
                                               start=(_kc == 0), stop=(_kc == NKC - 1))
                                if _kc == NKC - 1:
                                    ins.then_inc(S[MM], 1)
                            sch.emit("tensor", fq)
                        gq = sch.bump(MM, 1)
                        sch.wait("vector", MM, gq)
                        def fqe(v, S, _dc=dc, _toff=t0 - t0h, _pst=pst):
                            v.tensor_scalar_add(q_ap(_dc, _toff, TS), _pst[:, :],
                                                bq_sb[:, _dc:_dc + 1]).then_inc(S[DVE], 1)
                        sch.emit("vector", fqe)
                        qev[(dc * 2 + st2) % 2] = sch.bump(DVE, 1)
                return max(qev)

            q_ready = qproj(0)
            if "after_qproj0" in hooks:
                hooks["after_qproj0"](sch.val(MM))
            staging(0, q_ready)
            transposes(0)
            for i in range(16):
                if i == 8:
                    for a in range(3, 16):
                        av_row(7, a)
                    for st_fn in make_tail_stages(7):
                        st_fn()
                    q_ready = qproj(1)
                    if "after_qproj1" in hooks:
                        hooks["after_qproj1"](sch.val(MM))
                    staging(8, q_ready)
                    transposes(8)
                for a in range(3):
                    sc_row(i, a)
                if i % 8 != 0:
                    for a in range(3, 16):
                        av_row(i - 1, a)
                    transposes(i)
                stages = make_tail_stages(i - 1) if i % 8 != 0 else []
                for a in range(3, 16):
                    sc_row(i, a)
                    si = a - 3
                    if 0 <= si < len(stages):
                        stages[si]()
                for a in range(3):
                    av_row(i, a)
                if i % 8 != 7:
                    staging(i + 1, q_ready)
            for a in range(3, 16):
                av_row(15, a)
            for st_fn in make_tail_stages(15):
                st_fn()
            return state["u_last"], sch.val(MM)

        do1, doa, do2 = ("ffn1" in phases), ("attn" in phases), ("ffn2" in phases)
        seg_psums = []

        wst1 = {"w1": [None] * NSUP, "w2": [None] * NSUP,
                "sup_last_mm": [0] * NSUP, "issue_next": None}
        wst2 = {"w1": [None] * NSUP, "w2": [None] * NSUP,
                "sup_last_mm": [0] * NSUP, "issue_next": None}

        xw = 0
        ffn1_mm = 0
        if do1:
            with nc.psum_tensor("ps_d1", [128, TS], F32) as ps_d1, \
                 nc.psum_tensor("ps_w1", [1, TS], F32) as ps_w1, \
                 nc.psum_tensor("ps_v1", [1, TS], F32) as ps_v1, \
                 nc.psum_tensor("ps_e1", [128, TS], F32) as ps_e1, \
                 nc.psum_tensor("ps_f1", [128, TS], F32) as ps_f1:
                for s0 in range(2):
                    wst1["w1"][s0] = issue_w1(s0, s0 % 2, 0, "f1")
                    wst1["w2"][s0] = issue_w2(s0, s0 % 2, 0)

                def issue_next_ffn1(sup):
                    nxt = sup + 2
                    tok = wst1["sup_last_mm"][sup]
                    if nxt < NSUP:
                        wst1["w1"][nxt] = issue_w1(nxt, nxt % 2, tok, "f1")
                        wst1["w2"][nxt] = issue_w2(nxt, nxt % 2, tok)
                wst1["issue_next"] = issue_next_ffn1
                xw, _, ffn1_mm = emit_ffn(0, wst1, ps_d1, ps_w1, ps_v1, ps_e1, ps_f1)
                if doa:
                    wready["wq"] = issue_wqut("q", wst1["sup_last_mm"][10])
            sch.seg_break()

        attn_mm = ffn1_mm
        if doa:
            if "wq" not in wready:
                wready["wq"] = issue_wqut("q", ffn1_mm)
            hooks = {}
            def after_qproj0(mmtok):
                wu_wait = max(mmtok, wst1["sup_last_mm"][11] if do1 else 0)
                wready["wu"] = issue_wqut("u", wu_wait)
            def after_qproj1(mmtok):
                if do2:
                    wst2["w1"][0] = issue_w1(0, 0, mmtok, "f2")
            hooks["after_qproj0"] = after_qproj0
            hooks["after_qproj1"] = after_qproj1
            with nc.psum_tensor("s0", [128, TS], F32) as s0, \
                 nc.psum_tensor("s1", [128, TS], F32) as s1, \
                 nc.psum_tensor("d_att", [128, TS], F32) as d_att, \
                 nc.psum_tensor("ps_t", [128, 1024], BF16) as ps_t, \
                 nc.psum_tensor("ps_rz", [128, TS], F32) as ps_rz:
                xw, attn_mm = emit_attn(xw, s0, s1, d_att, ps_t, ps_rz, hooks)
            sch.seg_break()

        xtok = [xw] * NKC
        if do2:
            with nc.psum_tensor("ps_d2", [128, TS], F32) as ps_d2, \
                 nc.psum_tensor("ps_w2", [1, TS], F32) as ps_w2, \
                 nc.psum_tensor("ps_v2", [1, TS], F32) as ps_v2, \
                 nc.psum_tensor("ps_e2", [128, TS], F32) as ps_e2, \
                 nc.psum_tensor("ps_f2", [128, TS], F32) as ps_f2:
                if wst2["w1"][0] is None:
                    wst2["w1"][0] = issue_w1(0, 0, attn_mm, "f2")
                wst2["w2"][0] = issue_w2(0, 0, attn_mm)
                wst2["w1"][1] = issue_w1(1, 1, attn_mm, "f2")
                wst2["w2"][1] = issue_w2(1, 1, attn_mm)

                def issue_next_ffn2(sup):
                    nxt = sup + 2
                    tok = wst2["sup_last_mm"][sup]
                    if nxt < NSUP:
                        wst2["w1"][nxt] = issue_w1(nxt, nxt % 2, tok, "f2")
                        wst2["w2"][nxt] = issue_w2(nxt, nxt % 2, tok)
                wst2["issue_next"] = issue_next_ffn2
                _, xtok, _ = emit_ffn(xw, wst2, ps_d2, ps_w2, ps_v2, ps_e2, ps_f2)

        for dc in range(NKC):
            sch.wait("sync", DVE, xtok[dc])
            def fy(sy, S, _dc=dc):
                sy.dma_start(out=y_h[_dc * 128:(_dc + 1) * 128, :],
                             in_=x_sb[:, _dc * NT:(_dc + 1) * NT]).then_inc(S[OUT], 16)
            sch.emit("sync", fy)
            sch.bump(OUT, 16)
        sch.wait("sync", OUT, sch.val(OUT))

        segments = sch.finish()
        for seg in segments:
            if not any(seg[e] for e in ENGINES):
                continue
            with nc.Block() as block:
                @block.gpsimd
                def _(g, _seg=seg):
                    for f in _seg["gpsimd"]:
                        f(g, sems)

                @block.tensor
                def _(t, _seg=seg):
                    for f in _seg["tensor"]:
                        f(t, sems)

                @block.vector
                def _(v, _seg=seg):
                    for f in _seg["vector"]:
                        f(v, sems)

                @block.scalar
                def _(s, _seg=seg):
                    for f in _seg["scalar"]:
                        f(s, sems)

                @block.sync
                def _(sy, _seg=seg):
                    for f in _seg["sync"]:
                        f(sy, sems)

    return nc


_NC_CACHE = {}


def _get_nc(phases):
    key = tuple(phases)
    if key not in _NC_CACHE:
        _NC_CACHE[key] = build(phases)
    return _NC_CACHE[key]


def make_inputs(L0_attn_x, L0_attn_wq, L0_attn_bq, L0_attn_wu, L0_attn_bu,
                L0_ffn_gate_w, L0_ffn_w1, L0_ffn_b1, L0_ffn_w2, L0_ffn_b2,
                L0_ffn_sw1, L0_ffn_sb1, L0_ffn_sw2, L0_ffn_sb2):
    f = lambda a: np.asarray(a, dtype=np.float32)
    x = f(L0_attn_x)
    w1 = f(L0_ffn_w1)
    w2 = f(L0_ffn_w2)
    w1cat = np.ascontiguousarray(np.concatenate([f(L0_ffn_sw1), w1[0], w1[1]], axis=1))
    w2cat = np.ascontiguousarray(np.concatenate([f(L0_ffn_sw2), w2[0], w2[1]], axis=0))
    wqt_ = np.ascontiguousarray(f(L0_attn_wq).T)
    wut_ = np.ascontiguousarray(f(L0_attn_wu).T)
    b1cat = np.concatenate([f(L0_ffn_sb1), f(L0_ffn_b1)[0], f(L0_ffn_b1)[1]])
    b1h_ = np.ascontiguousarray(b1cat.reshape(96, 128).T)
    b2 = f(L0_ffn_b2)
    b2x_ = np.ascontiguousarray(np.concatenate(
        [b2[0].reshape(8, 128).T, b2[1].reshape(8, 128).T,
         f(L0_ffn_sb2).reshape(8, 128).T], axis=1))
    bqc_ = np.ascontiguousarray(f(L0_attn_bq).reshape(8, 128).T)
    buc_ = np.ascontiguousarray(f(L0_attn_bu).reshape(8, 128).T)
    gw = f(L0_ffn_gate_w)
    gwc_ = np.ascontiguousarray(np.concatenate(
        [gw[0].reshape(8, 128).T, gw[1].reshape(8, 128).T], axis=1))
    shared = dict(w1c=w1cat, w2c=w2cat, wqt=wqt_, wut=wut_, b1h=b1h_, b2x=b2x_,
                  bqc=bqc_, buc=buc_, gwc=gwc_)
    in_maps = []
    for i in range(8):
        m = dict(shared)
        m["x"] = np.ascontiguousarray(x[i].T)
        in_maps.append(m)
    return in_maps


def kernel(L0_attn_x, L0_attn_wq, L0_attn_bq, L0_attn_wu, L0_attn_bu,
           L0_ffn_gate_w, L0_ffn_w1, L0_ffn_b1, L0_ffn_w2, L0_ffn_b2,
           L0_ffn_sw1, L0_ffn_sb1, L0_ffn_sw2, L0_ffn_sb2,
           _phases=("ffn1", "attn", "ffn2")):
    in_maps = make_inputs(L0_attn_x, L0_attn_wq, L0_attn_bq, L0_attn_wu,
                          L0_attn_bu, L0_ffn_gate_w, L0_ffn_w1, L0_ffn_b1,
                          L0_ffn_w2, L0_ffn_b2, L0_ffn_sw1, L0_ffn_sb1,
                          L0_ffn_sw2, L0_ffn_sb2)
    nc = _get_nc(_phases)
    res = run_bass_kernel_spmd(nc, in_maps, core_ids=list(range(8)))
    out = np.stack([np.ascontiguousarray(r["y"].T) for r in res.results], axis=0)
    return out.astype(np.float32)


# version 17
# speedup vs baseline: 1.0259x; 1.0259x over previous
import contextlib
import numpy as np

import concourse.bass as bass
import concourse.mybir as mybir
from concourse.bass_utils import run_bass_kernel_spmd

F32 = mybir.dt.float32
BF16 = mybir.dt.bfloat16
AF = mybir.ActivationFunctionType
ALU = mybir.AluOpType

NT = 2048
D = 1024
NKC = 8
TS = 512
NSTRIP = NT // TS
HE = 1024
NSUP = 12
EPS = 1e-5
SCALE = 1.0 / 32.0

ENGINES = ("gpsimd", "tensor", "vector", "scalar", "sync")


def pos_of_c(c):
    return (c // 2) if c % 2 == 0 else 8 + (c // 2)


def c_of_pos(p):
    return 2 * p if p < 8 else 2 * (p - 8) + 1


class Sched:

    def __init__(self):
        self.segs = []
        self.ops = {e: [] for e in ENGINES}
        self.cnt = {}
        self.lastwait = {e: {} for e in ENGINES}

    def seg_break(self):
        self.segs.append(self.ops)
        self.ops = {e: [] for e in ENGINES}

    def finish(self):
        self.seg_break()
        return self.segs

    def emit(self, eng, fn):
        self.ops[eng].append(fn)

    def wait(self, eng, sem, val):
        if not val or val <= 0:
            return
        if self.lastwait[eng].get(sem, -1) >= val:
            return
        self.lastwait[eng][sem] = val
        self.emit(eng, lambda e, S, _s=sem, _v=val: e.wait_ge(S[_s], _v))

    def bump(self, sem, by):
        self.cnt[sem] = self.cnt.get(sem, 0) + by
        return self.cnt[sem]

    def val(self, sem):
        return self.cnt.get(sem, 0)


def build(phases=("ffn1", "attn", "ffn2")):
    nc = bass.Bass("TRN2", target_bir_lowering=False)

    x_h = nc.dram_tensor("x", [D, NT], F32, kind="ExternalInput").ap()
    w1c = nc.dram_tensor("w1c", [D, 3 * 4096], F32, kind="ExternalInput").ap()
    w2c = nc.dram_tensor("w2c", [3 * 4096, D], F32, kind="ExternalInput").ap()
    wqt = nc.dram_tensor("wqt", [D, D], F32, kind="ExternalInput").ap()
    wut = nc.dram_tensor("wut", [D, D], F32, kind="ExternalInput").ap()
    b1h = nc.dram_tensor("b1h", [128, 96], F32, kind="ExternalInput").ap()
    b2x = nc.dram_tensor("b2x", [128, 24], F32, kind="ExternalInput").ap()
    bqc = nc.dram_tensor("bqc", [128, 8], F32, kind="ExternalInput").ap()
    buc = nc.dram_tensor("buc", [128, 8], F32, kind="ExternalInput").ap()
    gwc = nc.dram_tensor("gwc", [128, 16], F32, kind="ExternalInput").ap()
    y_h = nc.dram_tensor("y", [D, NT], F32, kind="ExternalOutput").ap()

    es = contextlib.ExitStack()
    with es:
        sb = lambda n, s, d: es.enter_context(nc.sbuf_tensor(n, s, d))
        mksem = lambda n: es.enter_context(nc.semaphore(n))

        x_sb = sb("x_sb", [128, NKC * NT], F32)
        ar_ln = sb("ar_ln", [128, 16384], BF16)
        ar_w1 = sb("ar_w1", [128, 16384], BF16)
        ar_w2 = sb("ar_w2", [128, 16384], BF16)
        ar_h = sb("ar_h", [128, 4096], BF16)
        ar_b = sb("ar_b", [128, 8192], BF16)
        p_ex = sb("p_ex", [128, 8192], BF16)
        rzb_sb = sb("rzb_sb", [128, TS], BF16)
        xb_s = rzb_sb
        m0row = sb("m0row", [1, TS], BF16)
        m1row = sb("m1row", [1, TS], BF16)
        ones_col = sb("ones_col", [128, 1], BF16)
        onesLf = sb("onesLf", [1, 128], F32)
        onesLb = sb("onesLb", [1, 128], BF16)
        ident = sb("ident", [128, 128], BF16)
        e2mat = sb("e2mat", [128, 128], BF16)
        sel16 = sb("sel16", [16, 1024], BF16)
        b1hf = sb("b1hf", [128, 96], F32)
        b2sh = sb("b2sh", [128, 8], F32)
        bq_sb = sb("bq_sb", [128, 8], F32)
        bu_sb = sb("bu_sb", [128, 8], F32)
        gw_bf = sb("gw_bf", [128, 16], BF16)
        zcols = sb("zcols", [128, 16], F32)
        zc4a = sb("zc4a", [128, 64], F32)
        zc4b = sb("zc4b", [128, 64], F32)
        rzbf = sb("rzbf", [128, 16], BF16)
        trz = sb("trz", [16, 128], BF16)

        ps_a = es.enter_context(nc.psum_tensor("ps_a", [128, TS], F32))
        ps_b = es.enter_context(nc.psum_tensor("ps_b", [128, TS], F32))
        ps_c = es.enter_context(nc.psum_tensor("ps_c", [128, TS], F32))

        sems = {}
        for s in ("w1a", "w1b", "w2a", "w2b", "wx", "x", "mm", "act", "dve", "gp", "out"):
            sems[s] = mksem("s_" + s)

        sch = Sched()
        X, MM, ACTs, DVE, GP, OUT = "x", "mm", "act", "dve", "gp", "out"

        scaleB = ar_b[:, 6144:7168].bitcast(F32)
        meanB = ar_b[:, 7168:8192].bitcast(F32)
        r1 = meanB[0:1, :]
        r2 = scaleB[0:1, :]
        r3 = m0row

        def lnm_ap(kc):
            return ar_b[:, kc * TS:(kc + 1) * TS]

        def mask_ap(which, strip):
            off = 8 * TS + which * NT
            return ar_b[:, off + strip * TS: off + (strip + 1) * TS]

        def q_ap(kc, t0, w):
            return ar_b[:, kc * 1024 + t0: kc * 1024 + t0 + w]

        def ln_ap(kc, t0, w):
            half = t0 // 1024
            tt = t0 - half * 1024
            base = half * 8192 + kc * 1024 + tt
            return ar_ln[:, base: base + w]

        def prow_ap(a):
            if a < 8:
                return ar_w2[:, a * NT:(a + 1) * NT]
            if a < 12:
                return ar_ln[:, (a - 8) * NT:(a - 7) * NT]
            return p_ex[:, (a - 12) * NT:(a - 11) * NT]

        def w1blk(slot, kc, hc):
            o = slot * 8192 + kc * 1024 + hc * 128
            return ar_w1[:, o:o + 128]

        def w2blk(slot, hc, dc):
            o = slot * 8192 + hc * 1024 + dc * 128
            return ar_w2[:, o:o + 128]

        def wqtblk(kc, dc):
            o = kc * 1024 + dc * 128
            return ar_w1[:, o:o + 128]

        def wutblk(kc, dc):
            o = 8192 + kc * 1024 + dc * 128
            return ar_w1[:, o:o + 128]

        def qq_ap(posc, lo64):
            return ar_h[lo64:lo64 + 64, 128 * posc: 128 * posc + 128]

        def qT_ap(cp, w=64):
            return ar_h[:, 2048 + 64 * cp: 2048 + 64 * cp + w]

        def ofm_ap(ec, lo, hi):
            return ar_h[lo:hi, 3072 + 128 * ec: 3072 + 128 * ec + 128]

        def g_init(g, S):
            n = 0
            g.memset(ones_col[:, :], 1.0)
            g.memset(onesLf[:, :], 1.0)
            g.memset(onesLb[:, :], 1.0)
            g.memset(ident[:, :], 0.0).then_inc(S[GP], 1); n += 1
            g.wait_ge(S[GP], n)
            g.affine_select(out=ident[:, :], in_=ident[:, :], compare_op=ALU.not_equal,
                            fill=1.0, base=0, pattern=[[-1, 128]],
                            channel_multiplier=1).then_inc(S[GP], 1); n += 1
            g.memset(e2mat[:, :], 0.0).then_inc(S[GP], 1); n += 1
            for base in (0, 64, -64):
                g.wait_ge(S[GP], n)
                g.affine_select(out=e2mat[:, :], in_=e2mat[:, :], compare_op=ALU.not_equal,
                                fill=1.0, base=base, pattern=[[-1, 128]],
                                channel_multiplier=1).then_inc(S[GP], 1); n += 1
            g.memset(sel16[:, :], 0.0).then_inc(S[GP], 1); n += 1
            g.wait_ge(S[GP], n)
            g.affine_select(
                out=sel16[:, :].rearrange("p (c d) -> p c d", c=16),
                in_=sel16[:, :].rearrange("p (c d) -> p c d", c=16),
                compare_op=ALU.not_equal, fill=1.0, base=0,
                pattern=[[-1, 16], [0, 64]], channel_multiplier=1,
            ).then_inc(S[GP], 1); n += 1
        sch.emit("gpsimd", g_init)
        tok_gpinit = sch.bump(GP, 8)

        def sy_load(sy, S):
            sy.dma_start(
                out=x_sb[:, :].rearrange("p (kc t) -> p kc t", kc=NKC),
                in_=x_h.rearrange("(kc p) t -> p kc t", p=128),
            ).then_inc(S[X], 16)
            for src, dst in ((b1h, b1hf), (bqc, bq_sb), (buc, bu_sb)):
                sy.dma_start(out=dst[:, :], in_=src[:, :]).then_inc(S[X], 16)
            sy.dma_start(out=b2sh[:, :], in_=b2x[:, 16:24]).then_inc(S[X], 16)
            sy.dma_start(out=meanB[:, 0:16], in_=b2x[:, 0:16]).then_inc(S[X], 16)
        sch.emit("sync", sy_load)
        tok_xload = sch.bump(X, 16 * 6)

        def g_gw(g, S):
            g.dma_start(out=gw_bf[:, :], in_=gwc[:, :]).then_inc(S["wx"], 16)
        sch.emit("gpsimd", g_gw)
        tok_gw = ("wx", sch.bump("wx", 16))

        sch.wait("scalar", X, tok_xload)
        def sc_bias(s, S):
            s.mul(b1hf[:, :], b1hf[:, :], 0.5).then_inc(S[ACTs], 1)
        sch.emit("scalar", sc_bias)
        tok_b1 = sch.bump(ACTs, 1)

        sch.wait("vector", X, tok_xload)
        base_dve = sch.val(DVE)
        def dv_bias(v, S, _b=base_dve):
            v.tensor_add(b2sh[:, :], b2sh[:, :], meanB[:, 0:8]).then_inc(S[DVE], 1)
            v.wait_ge(S[DVE], _b + 1)
            v.tensor_add(b2sh[:, :], b2sh[:, :], meanB[:, 8:16]).then_inc(S[DVE], 1)
            v.wait_ge(S[DVE], _b + 2)
            v.tensor_scalar_mul(b2sh[:, :], b2sh[:, :], 0.5).then_inc(S[DVE], 1)
        sch.emit("vector", dv_bias)
        tok_b2 = sch.bump(DVE, 3)

        wready = {}

        def issue_w1(sup, slot, wait_mm, arena_name):
            sch.wait("gpsimd", MM, wait_mm)
            wsem = "w1a" if slot == 0 else "w1b"
            def f(g, S, _sup=sup, _slot=slot, _ws=wsem):
                g.dma_start(
                    out=ar_w1[:, _slot * 8192:(_slot + 1) * 8192].rearrange(
                        "p (kc h) -> p kc h", kc=NKC),
                    in_=w1c[:, _sup * HE:(_sup + 1) * HE].rearrange(
                        "(kc p) h -> p kc h", p=128),
                ).then_inc(S[_ws], 16)
            sch.emit("gpsimd", f)
            return (wsem, sch.bump(wsem, 16))

        def issue_w2(sup, slot, wait_mm):
            sch.wait("gpsimd", MM, wait_mm)
            wsem = "w2a" if slot == 0 else "w2b"
            def f(g, S, _sup=sup, _slot=slot, _ws=wsem):
                g.dma_start(
                    out=ar_w2[:, _slot * 8192:(_slot + 1) * 8192].rearrange(
                        "p (hc d) -> p hc d", hc=NKC),
                    in_=w2c[_sup * HE:(_sup + 1) * HE, :].rearrange(
                        "(hc p) d -> p hc d", p=128),
                ).then_inc(S[_ws], 16)
            sch.emit("gpsimd", f)
            return (wsem, sch.bump(wsem, 16))

        def issue_wqut(which, wait_mm):
            sch.wait("gpsimd", MM, wait_mm)
            src = wqt if which == "q" else wut
            off = 0 if which == "q" else 8192
            def f(g, S, _src=src, _off=off):
                g.dma_start(
                    out=ar_w1[:, _off:_off + 8192].rearrange("p (kc d) -> p kc d", kc=NKC),
                    in_=_src.rearrange("(kc p) d -> p kc d", p=128),
                ).then_inc(S["wx"], 16)
            sch.emit("gpsimd", f)
            return ("wx", sch.bump("wx", 16))

        pexf = p_ex[:, 0:4096].bitcast(F32)
        mean2 = pexf[:, 1024:1536]
        scale2 = pexf[:, 1536:2048]
        xb2 = p_ex[:, 4096:4608]
        gate_r1 = p_ex[:, 4608:5632].bitcast(F32)[0:1, :]
        gate_r2 = p_ex[:, 5632:6656].bitcast(F32)[0:1, :]

        def make_ln_stages(is_ffn, xwaits, statps, bcps, ext=None):
            ext = ext if ext is not None else {}
            ln_tok = [0] * NSTRIP
            mask_tok = [0] * NSTRIP
            s1ev = [0] * NSTRIP
            s2ev = [0] * NSTRIP
            statmm = [0] * NSTRIP
            sqtok = [0] * NSTRIP
            bcev = [0] * NSTRIP

            def tiles(st):
                if st % 2 == 0:
                    return meanB, scaleB
                return mean2, scale2

            def scratch(st):
                return xb_s if st % 2 == 0 else xb2

            def stage_stats(st):
                t0 = st * TS
                xb = scratch(st)
                sp1 = statps[0]
                sp2 = statps[1]
                mB, sB = tiles(st)
                r1, r2 = mB[0:1, :], sB[0:1, :]
                sch.wait("vector", DVE, max(xwaits[st], tok_b2))
                sch.wait("scalar", DVE, max(xwaits[st], tok_b2))
                prev_mm = statmm[st - 1] if st > 0 else 0
                sch.wait("vector", X, tok_xload)
                if st >= 2:
                    sch.wait("vector", DVE, ln_tok[st - 2])
                for kc in range(NKC):
                    sch.wait("vector", MM, prev_mm)
                    sch.wait("vector", ACTs, sqtok[st - 2] if st >= 2 else 0)
                    def fcast(v, S, _kc=kc, _t0=t0, _xb=xb):
                        v.tensor_copy(
                            _xb[:, :], x_sb[:, _kc * NT + _t0: _kc * NT + _t0 + TS]
                        ).then_inc(S[DVE], 1)
                    sch.emit("vector", fcast)
                    ctok = sch.bump(DVE, 1)
                    sch.wait("tensor", GP, tok_gpinit)
                    sch.wait("tensor", DVE, max(ctok, s1ev[st - 1] if st >= 1 else 0))
                    def fmm(t, S, _kc=kc, _sp=sp1, _xb=xb):
                        t.matmul(_sp, ones_col[:, :], _xb[:, :],
                                 start=(_kc == 0), stop=(_kc == NKC - 1)).then_inc(S[MM], 1)
                    sch.emit("tensor", fmm)
                    prev_mm = sch.bump(MM, 1)
                s1mm = prev_mm
                sch.wait("vector", MM, s1mm)
                def fs1(v, S, _sp=sp1, _r=r1):
                    v.tensor_scalar_mul(_r, _sp, 1.0 / D).then_inc(S[DVE], 1)
                sch.emit("vector", fs1)
                s1ev[st] = sch.bump(DVE, 1)
                for kc in range(NKC):
                    sch.wait("scalar", MM, prev_mm)
                    sch.wait("scalar", X, tok_xload)
                    def fsq(s, S, _kc=kc, _t0=t0, _xb=xb):
                        s.activation(_xb[:, :],
                                     x_sb[:, _kc * NT + _t0: _kc * NT + _t0 + TS],
                                     AF.Square).then_inc(S[ACTs], 1)
                    sch.emit("scalar", fsq)
                    sqtok[st] = sch.bump(ACTs, 1)
                    sch.wait("tensor", ACTs, sqtok[st])
                    sch.wait("tensor", DVE, s2ev[st - 1] if st >= 1 else 0)
                    def fmm2(t, S, _kc=kc, _sp=sp2, _xb=xb):
                        t.matmul(_sp, ones_col[:, :], _xb[:, :],
                                 start=(_kc == 0), stop=(_kc == NKC - 1)).then_inc(S[MM], 1)
                    sch.emit("tensor", fmm2)
                    prev_mm = sch.bump(MM, 1)
                statmm[st] = prev_mm
                sch.wait("vector", MM, prev_mm)
                def fs2(v, S, _sp=sp2, _r=r2):
                    v.tensor_scalar(out=_r, in0=_sp, scalar1=1.0 / D, scalar2=EPS,
                                    op0=ALU.mult, op1=ALU.add).then_inc(S[DVE], 1)
                sch.emit("vector", fs2)
                s2ev[st] = sch.bump(DVE, 1)

            def stage_post(st):
                t0 = st * TS
                mB, sB = tiles(st)
                r1, r2 = mB[0:1, :], sB[0:1, :]
                bA = bcps[0]
                bB = bcps[1]
                sch.wait("vector", DVE, s2ev[st])
                base_r = sch.val(DVE)
                def frow2(v, S, _r1=r1, _r2=r2, _b=base_r):
                    v.wait_ge(S[DVE], _b)
                    v.tensor_mul(r3[:, :], _r1, _r1).then_inc(S[DVE], 1)
                    v.wait_ge(S[DVE], _b + 1)
                    v.scalar_tensor_tensor(out=_r2, in0=r3[:, :], scalar=-1.0,
                                           in1=_r2, op0=ALU.mult,
                                           op1=ALU.add).then_inc(S[DVE], 1)
                sch.emit("vector", frow2)
                rowb = sch.bump(DVE, 2)
                sch.wait("scalar", DVE, rowb)
                def frow3(s, S, _r2=r2):
                    s.activation(_r2, _r2, AF.Sqrt).then_inc(S[ACTs], 1)
                sch.emit("scalar", frow3)
                rowc = sch.bump(ACTs, 1)
                sch.wait("vector", ACTs, rowc)
                def frow4(v, S, _r2=r2):
                    v.reciprocal(_r2, _r2).then_inc(S[DVE], 1)
                sch.emit("vector", frow4)
                rowd = sch.bump(DVE, 1)
                sch.wait("tensor", DVE, rowd)
                sch.wait("tensor", DVE, ext.get("bc_wait", 0))
                if st >= 1:
                    sch.wait("tensor", DVE,
                             mask_tok[st - 1] if is_ffn else bcev[st - 1])
                def fbc1(t, S, _r=r1, _bp=bA):
                    t.matmul(_bp, onesLf[:, :], _r, start=True, stop=True).then_inc(S[MM], 1)
                sch.emit("tensor", fbc1)
                bct1 = sch.bump(MM, 1)
                def fbc2(t, S, _r=r2, _bp=bB):
                    t.matmul(_bp, onesLf[:, :], _r, start=True, stop=True).then_inc(S[MM], 1)
                sch.emit("tensor", fbc2)
                bct2 = sch.bump(MM, 1)
                sch.wait("vector", MM, bct2)
                base_e = sch.val(DVE)
                def fbe(v, S, _mB=mB, _sB=sB, _bA=bA, _bB=bB, _b=base_e):
                    v.tensor_copy(_mB[:, :], _bA).then_inc(S[DVE], 1)
                    v.wait_ge(S[DVE], _b + 1)
                    v.tensor_copy(_sB[:, :], _bB).then_inc(S[DVE], 1)
                sch.emit("vector", fbe)
                bcev[st] = sch.bump(DVE, 2)
                xb = scratch(st)
                sch.wait("vector", DVE, bcev[st])
                sch.wait("vector", ACTs, sqtok[st])
                for kc in range(NKC):
                    base_n = sch.val(DVE)
                    def fn(v, S, _kc=kc, _t0=t0, _b=base_n, _xb=xb, _mB=mB, _sB=sB):
                        v.wait_ge(S[DVE], _b)
                        v.tensor_sub(_xb[:, :],
                                     x_sb[:, _kc * NT + _t0: _kc * NT + _t0 + TS],
                                     _mB[:, :]).then_inc(S[DVE], 1)
                        v.wait_ge(S[DVE], _b + 1)
                        v.tensor_mul(ln_ap(_kc, _t0, TS), _xb[:, :],
                                     _sB[:, :]).then_inc(S[DVE], 1)
                    sch.emit("vector", fn)
                    ln_tok[st] = sch.bump(DVE, 2)


            def stage_gate(st):
                t0 = st * TS
                r1, r2 = gate_r1, gate_r2
                bA = bcps[0]
                gevac = 0
                for e in range(2):
                    gp_ps = bcps[e]
                    for kc in range(NKC):
                        sch.wait("tensor", *tok_gw)
                        sch.wait("tensor", DVE, ln_tok[st])
                        sch.wait("tensor", DVE, ext.get("bc_wait", 0))
                        def fg(t, S, _e=e, _kc=kc, _t0=t0, _sp=gp_ps):
                            t.matmul(_sp[0:1, :],
                                     gw_bf[:, _e * 8 + _kc:_e * 8 + _kc + 1],
                                     ln_ap(_kc, _t0, TS),
                                     start=(_kc == 0), stop=(_kc == NKC - 1)
                                     ).then_inc(S[MM], 1)
                        sch.emit("tensor", fg)
                        gtok = sch.bump(MM, 1)
                    sch.wait("vector", MM, gtok)
                    sch.wait("vector", DVE, ln_tok[st])
                    dstrow = r1 if e == 0 else r2
                    def fge(v, S, _d=dstrow, _sp=gp_ps):
                        v.tensor_copy(_d, _sp[0:1, :]).then_inc(S[DVE], 1)
                    sch.emit("vector", fge)
                    gevac = sch.bump(DVE, 1)
                sch.wait("vector", DVE, gevac)
                base_mr = sch.val(DVE)
                def fmrow(v, S, _r1=r1, _r2=r2, _b=base_mr):
                    v.tensor_tensor(m0row[:, :], _r1, _r2,
                                    ALU.is_ge).then_inc(S[DVE], 1)
                    v.wait_ge(S[DVE], _b + 1)
                    v.tensor_tensor(m1row[:, :], _r1, _r2,
                                    ALU.is_lt).then_inc(S[DVE], 1)
                sch.emit("vector", fmrow)
                mrow_tok = sch.bump(DVE, 2)
                sch.wait("tensor", DVE, mrow_tok)
                bB2 = bcps[1]
                def fmb(t, S, _bp=bA, _bp2=bB2):
                    t.matmul(_bp, onesLb[:, :], m0row[:, :],
                             start=True, stop=True).then_inc(S[MM], 1)
                    t.matmul(_bp2, onesLb[:, :], m1row[:, :],
                             start=True, stop=True).then_inc(S[MM], 1)
                sch.emit("tensor", fmb)
                mbt = sch.bump(MM, 2)
                sch.wait("vector", MM, mbt)
                sch.wait("vector", DVE, ext.get("all_ln", 0))
                base_me = sch.val(DVE)
                def fmbe(v, S, _st=st, _bp=bA, _bp2=bB2, _b=base_me):
                    v.tensor_copy(mask_ap(0, _st), _bp).then_inc(S[DVE], 1)
                    v.wait_ge(S[DVE], _b + 1)
                    v.tensor_copy(mask_ap(1, _st), _bp2).then_inc(S[DVE], 1)
                sch.emit("vector", fmbe)
                mask_tok[st] = sch.bump(DVE, 2)

            stats_fns = [lambda _st=st: stage_stats(_st) for st in range(NSTRIP)]
            post_fns = [lambda _st=st: stage_post(_st) for st in range(NSTRIP)]
            gate_fns = [lambda _st=st: stage_gate(_st) for st in range(NSTRIP)]
            return stats_fns, post_fns, gate_fns, ln_tok, mask_tok

        def emit_ln(is_ffn, xwaits, statps, bcps):
            sf, pf, gf, ln_tok, mask_tok = make_ln_stages(is_ffn, xwaits, statps, bcps)
            for st in range(NSTRIP):
                sf[st]()
                if st > 0:
                    pf[st - 1]()
            pf[NSTRIP - 1]()
            if is_ffn:
                for st in range(NSTRIP):
                    gf[st]()
            return ln_tok, mask_tok

        def emit_ffn(xwait_dve, wst, ps_d, ps_w, ps_w2):
            ext = {"bc_wait": 0}
            sf, pf, gf, ln_tok, mask_tok = make_ln_stages(
                True, [xwait_dve] * NSTRIP,
                (ps_w[:, :], ps_w2[:, :]), (ps_d[:, :], ps_c[:, :]), ext)
            h_relu_tok = [0] * NKC
            stt_tok = [[0] * NSTRIP for _ in range(NKC)]
            ps1_free = [0, 0]
            ps2_free = [0, 0]
            prev_mm1 = [0]
            lnm_tok = [None]

            pending_lnm = []

            def build_lnm_pieces(st, klass, wait_mm):
                which = 0 if klass == 1 else 1
                def piece(kc):
                    sch.wait("vector", DVE, max(ln_tok[st], mask_tok[st]))
                    sch.wait("vector", MM, wait_mm)
                    def fm(v, S, _kc=kc, _st=st, _w=which):
                        v.tensor_mul(lnm_ap(_kc), ln_ap(_kc, _st * TS, TS),
                                     mask_ap(_w, _st)).then_inc(S[DVE], 1)
                    sch.emit("vector", fm)
                    lnm_tok[0] = sch.bump(DVE, 1)
                return [lambda _kc=kc: piece(_kc) for kc in range(NKC)]

            def build_lnm(st, klass, wait_mm):
                for p in build_lnm_pieces(st, klass, wait_mm):
                    p()
                return lnm_tok[0]

            def do_mm1(sup, st, klass, slot):
                if wst["w1"][sup]:
                    sch.wait("tensor", *wst["w1"][sup])
                if wst["w2"][sup]:
                    sch.wait("tensor", *wst["w2"][sup])
                if klass == 0:
                    rhs_sem, rhs_tok = DVE, ln_tok[st]
                    rhs = lambda kc, _t0=st * TS: ln_ap(kc, _t0, TS)
                else:
                    rhs_sem, rhs_tok = DVE, lnm_tok[0]
                    rhs = lambda kc: lnm_ap(kc)
                for hc in range(NKC):
                    pst = (ps_a, ps_b)[hc % 2]
                    sch.wait("tensor", ACTs, ps1_free[hc % 2])
                    sch.wait("tensor", rhs_sem, rhs_tok)
                    for kc in range(NKC):
                        def f1(t, S, _kc=kc, _hc=hc, _slot=slot, _pst=pst, _rhs=rhs):
                            ins = t.matmul(_pst[:, :], w1blk(_slot, _kc, _hc),
                                           _rhs(_kc),
                                           start=(_kc == 0), stop=(_kc == NKC - 1))
                            if _kc == NKC - 1:
                                ins.then_inc(S[MM], 1)
                        sch.emit("tensor", f1)
                    g1 = sch.bump(MM, 1)
                    sch.wait("scalar", MM, g1)
                    sch.wait("scalar", ACTs, tok_b1)
                    def fr(s, S, _hc=hc, _sup=sup, _pst=pst):
                        s.activation(ar_h[:, _hc * TS:(_hc + 1) * TS], _pst[:, :],
                                     AF.Relu,
                                     bias=b1hf[:, _sup * 8 + _hc:_sup * 8 + _hc + 1],
                                     scale=0.5).then_inc(S[ACTs], 1)
                    sch.emit("scalar", fr)
                    h_relu_tok[hc] = sch.bump(ACTs, 1)
                    ps1_free[hc % 2] = h_relu_tok[hc]
                prev_mm1[0] = sch.val(MM)
                if klass > 0:
                    nst, nkl = (st + 1, klass) if st < NSTRIP - 1 else \
                        (0, (sup + 1) // 4 if sup + 1 < NSUP else None)
                    if nkl:
                        pending_lnm.extend(build_lnm_pieces(nst, nkl, prev_mm1[0]))
                    else:
                        lnm_tok[0] = None

            def do_mm2(sup, st, slot, mm2_extra):
                h_all = max(h_relu_tok)
                for dc in range(NKC):
                    pst = (ps_c, ps_d)[dc % 2]
                    sch.wait("tensor", DVE, max(ps2_free[dc % 2], mm2_extra))
                    sch.wait("tensor", ACTs, h_all)
                    for hc in range(NKC):
                        def f2(t, S, _hc=hc, _dc=dc, _slot=slot, _pst=pst):
                            ins = t.matmul(_pst[:, :], w2blk(_slot, _hc, _dc),
                                           ar_h[:, _hc * TS:(_hc + 1) * TS],
                                           start=(_hc == 0), stop=(_hc == NKC - 1))
                            if _hc == NKC - 1:
                                ins.then_inc(S[MM], 1)
                        sch.emit("tensor", f2)
                    g2 = sch.bump(MM, 1)
                    sch.wait("vector", MM, g2)
                    sch.wait("vector", DVE, tok_b2)
                    o0 = dc * NT + st * TS
                    if sup == 0:
                        def fx(v, S, _dc=dc, _o0=o0, _pst=pst):
                            xsl = x_sb[:, _o0:_o0 + TS]
                            v.scalar_tensor_tensor(
                                out=xsl, in0=_pst[:, :],
                                scalar=b2sh[:, _dc:_dc + 1], in1=xsl,
                                op0=ALU.add, op1=ALU.add).then_inc(S[DVE], 1)
                    else:
                        def fx(v, S, _o0=o0, _pst=pst):
                            xsl = x_sb[:, _o0:_o0 + TS]
                            v.tensor_add(xsl, xsl, _pst[:, :]).then_inc(S[DVE], 1)
                    sch.emit("vector", fx)
                    stt_tok[dc][st] = sch.bump(DVE, 1)
                    ps2_free[dc % 2] = stt_tok[dc][st]
                    if pending_lnm:
                        pending_lnm.pop(0)()
                while pending_lnm:
                    pending_lnm.pop(0)()

            sf[0](); sf[1](); pf[0]()
            for st in range(NSTRIP):
                do_mm1(0, st, 0, 0)
                ext["bc_wait"] = sch.val(DVE)
                if st + 2 < NSTRIP:
                    sf[st + 2]()
                if st + 1 < NSTRIP:
                    pf[st + 1]()
                if st + 1 == NSTRIP:
                    pass
                do_mm2(0, st, 0, sch.val(DVE))
                ext["bc_wait"] = sch.val(DVE)
                if st == 2:
                    ext["all_ln"] = max(ln_tok)
                if st >= 2:
                    gf[st - 2]()
            for st in (2, 3):
                ext["bc_wait"] = sch.val(DVE)
                gf[st]()
            ps2_free[0] = ps2_free[1] = sch.val(DVE)
            wst["sup_last_mm"][0] = sch.val(MM)
            if wst["issue_next"] is not None:
                wst["issue_next"](0)
            for sup in range(1, NSUP):
                klass = sup // 4
                slot = sup % 2
                if klass > 0 and lnm_tok[0] is None:
                    lnm_tok[0] = build_lnm(0, klass, prev_mm1[0])
                for st in range(NSTRIP):
                    do_mm1(sup, st, klass, slot)
                    do_mm2(sup, st, slot, 0)
                wst["sup_last_mm"][sup] = sch.val(MM)
                if wst["issue_next"] is not None:
                    wst["issue_next"](sup)
            xdve = max(max(row) for row in stt_tok)
            xtok = [stt_tok[dc][NSTRIP - 1] for dc in range(NKC)]
            return xdve, xtok, sch.val(MM)

        def emit_attn(xwait_dve, s0, s1, d_att, ps_t, ps_rz, hooks):
            ln_tok, _ = emit_ln(False, [xwait_dve] * NSTRIP,
                                (s0[0:1, :], s1[0:1, :]),
                                (ps_c[:, :], d_att[:, :]))
            ln_done = sch.val(DVE)
            av_of_strip = [(ps_a, 0), (ps_b, 0), (ps_c, 64), (d_att, 64)]
            sregs3 = (s0, s1, ps_t[:, :].bitcast(F32))
            stg = (s0, s1)
            stg_u = (ps_a, ps_b)

            state = {
                "exp_tok": {},
                "avtok": {},
                "avfree": 0,
                "stage_tok": {},
                "stage_ev": 0,
                "ufree": 0,
                "qT_tok": {},
                "zsum_tok": 0,
                "last_qT": 0,
                "trz_evac": 0,
                "u_last": 0,
                "sreg_last": {},
            }

            def staging(i, q_ready):
                qoff = (i % 8) * 128
                act_prev = state["stage_ev"]
                for b4 in range(4):
                    ps = stg[b4 % 2]
                    sch.wait("tensor", ACTs, state["sreg_last"].get(b4 % 2, 0))
                    sch.wait("tensor", DVE, max(q_ready, act_prev))
                    for j in range(4):
                        c = c_of_pos(b4 * 4 + j)
                        hb = (c % 2) * 64
                        def fqq(t, S, _c=c, _hb=hb, _q=qoff, _j=j, _ps=ps):
                            ins = t.matmul(_ps[:, _j * 128:(_j + 1) * 128],
                                           e2mat[_hb:_hb + 64, :],
                                           q_ap(_c // 2, _q, 128)[_hb:_hb + 64, :],
                                           start=(_j == 0), stop=(_j == 3))
                            if _j == 3:
                                ins.then_inc(S[MM], 1)
                        sch.emit("tensor", fqq)
                    gqq = sch.bump(MM, 1)
                    sch.wait("vector", MM, gqq)
                    def fqqe(v, S, _b4=b4, _ps=ps):
                        v.tensor_copy(ar_h[:, _b4 * TS:(_b4 + 1) * TS],
                                      _ps[:, :]).then_inc(S[DVE], 1)
                    sch.emit("vector", fqqe)
                    act_prev = sch.bump(DVE, 1)
                state["stage_ev"] = act_prev
                state["stage_tok"][i] = act_prev

            def transposes(i):
                sch.wait("tensor", DVE, state["stage_tok"][i])
                sch.wait("tensor", ACTs, state["trz_evac"])
                sch.wait("tensor", ACTs, state["sreg_last"].get(2, 0))
                for cp in range(16):
                    def ftr(t, S, _cp=cp):
                        ins = t.matmul(ps_t[:, _cp * 64:(_cp + 1) * 64],
                                       qq_ap(pos_of_c(_cp), 0),
                                       ident[0:64, 0:64],
                                       start=(_cp == 0), stop=(_cp == 15),
                                       is_transpose=True)
                        if _cp == 15:
                            ins.then_inc(S[MM], 1)
                    sch.emit("tensor", ftr)
                gtr = sch.bump(MM, 1)
                sch.wait("vector", MM, gtr)
                def ftre(v, S):
                    v.tensor_copy(ar_h[:, 2048:3072], ps_t[:, 0:1024]).then_inc(S[DVE], 1)
                sch.emit("vector", ftre)
                state["qT_tok"][i] = sch.bump(DVE, 1)
                state["last_qT"] = state["qT_tok"][i]

            def sc_row(i, a):
                pa = pos_of_c(a)
                h = (a % 2) * 64
                for q4 in range(4):
                    qi = a * 4 + q4
                    ridx = qi % 2 if qi < 16 else (qi - 16) % 3
                    sreg = sregs3[ridx]
                    sch.wait("tensor", DVE, state["stage_tok"][i])
                    sch.wait("tensor", ACTs, state["sreg_last"].get(ridx, 0))
                    if ridx == 2:
                        sch.wait("tensor", ACTs, state["trz_evac"])
                        sch.wait("tensor", DVE, state["qT_tok"][i])
                    def fs(t, S, _pa=pa, _h=h, _q4=q4, _sr=sreg):
                        t.matmul(_sr[:, :],
                                 qq_ap(_pa, _h),
                                 ar_h[_h:_h + 64, _q4 * TS:(_q4 + 1) * TS],
                                 start=True, stop=True).then_inc(S[MM], 1)
                    sch.emit("tensor", fs)
                    gsc = sch.bump(MM, 1)
                    sch.wait("scalar", MM, gsc)
                    sch.wait("scalar", MM, state["avtok"].get((i - 1, a), 0))
                    sch.wait("scalar", DVE, state["zsum_tok"])
                    zdst = zc4a if i % 2 == 0 else zc4b
                    def fexp(s, S, _a=a, _q4=q4, _sr=sreg, _z=zdst):
                        s.activation(prow_ap(_a)[:, _q4 * TS:(_q4 + 1) * TS],
                                     _sr[:, :], AF.Exp, scale=SCALE,
                                     accum_out=_z[:, _a * 4 + _q4:_a * 4 + _q4 + 1]
                                     ).then_inc(S[ACTs], 1)
                    sch.emit("scalar", fexp)
                    state["exp_tok"][(i, a, q4)] = sch.bump(ACTs, 1)
                    state["sreg_last"][ridx] = state["exp_tok"][(i, a, q4)]

            def av_row(i, a):
                if a == 0:
                    sch.wait("tensor", DVE, state["avfree"])
                sch.wait("tensor", ACTs, state["exp_tok"][(i, a, 3)])
                sch.wait("tensor", DVE, state["qT_tok"][i])
                for sg in range(4):
                    bank, lo = av_of_strip[sg]
                    tpos = (0, 0) if lo == 0 else (0, 64)
                    def fav(t, S, _a=a, _sg=sg, _bk=bank, _lo=lo, _tp=tpos):
                        ins = t.matmul(_bk[_lo:_lo + 64, :],
                                       qT_ap(_a),
                                       prow_ap(_a)[:, _sg * TS:(_sg + 1) * TS],
                                       start=(_a == 0), stop=(_a == 15),
                                       tile_position=_tp)
                        if _sg == 3:
                            ins.then_inc(S[MM], 1)
                    sch.emit("tensor", fav)
                state["avtok"][(i, a)] = sch.bump(MM, 1)

            def make_tail_stages(i):
                act_base = sch.val(ACTs)
                zsrc = zc4a if i % 2 == 0 else zc4b
                rzbev = [0]
                avev = [0]
                uev = [0, 0]

                def t_rz():
                    sch.wait("vector", ACTs, state["exp_tok"][(i, 15, 3)])
                    sch.wait("vector", DVE, sch.val(DVE))
                    base_z = sch.val(DVE)
                    def fz(v, S, _z=zsrc, _b=base_z):
                        v.tensor_reduce(
                            zcols[:, :].rearrange("p (a o) -> p a o", o=1),
                            _z[:, :].rearrange("p (a q) -> p a q", q=4),
                            mybir.AxisListType.X, ALU.add).then_inc(S[DVE], 1)
                        v.wait_ge(S[DVE], _b + 1)
                        v.reciprocal(zcols[:, :], zcols[:, :]).then_inc(S[DVE], 1)
                        v.wait_ge(S[DVE], _b + 2)
                        v.tensor_copy(rzbf[:, :], zcols[:, :]).then_inc(S[DVE], 1)
                    sch.emit("vector", fz)
                    state["zsum_tok"] = sch.bump(DVE, 3)
                    sch.wait("tensor", DVE, max(state["zsum_tok"], state["last_qT"]))
                    def frzt(t, S):
                        t.matmul(ps_t[0:16, 0:128], rzbf[:, :], ident[:, :],
                                 start=True, stop=True,
                                 is_transpose=True).then_inc(S[MM], 1)
                    sch.emit("tensor", frzt)
                    grzt = sch.bump(MM, 1)
                    sch.wait("scalar", MM, grzt)
                    def frzte(s, S):
                        s.copy(trz[:, :], ps_t[0:16, 0:128]).then_inc(S[ACTs], 1)
                    sch.emit("scalar", frzte)
                    state["trz_evac"] = sch.bump(ACTs, 1)

                def mk_strip(sg):
                    def t_strip():
                        bk, lo = av_of_strip[sg]
                        tpos = (0, 0) if lo == 0 else (0, 64)
                        sch.wait("tensor", ACTs, max(state["trz_evac"], act_base))
                        sch.wait("tensor", DVE, rzbev[0])
                        for j in range(4):
                            cj = c_of_pos(sg * 4 + j)
                            def frzb(t, S, _cj=cj, _j=j, _lo=lo, _tp=tpos):
                                ins = t.matmul(
                                    ps_rz[_lo:_lo + 64, _j * 128:(_j + 1) * 128],
                                    sel16[:, _cj * 64:(_cj + 1) * 64],
                                    trz[:, :],
                                    start=(_j == 0), stop=(_j == 3),
                                    tile_position=_tp)
                                if _j == 3:
                                    ins.then_inc(S[MM], 1)
                            sch.emit("tensor", frzb)
                        grzb = sch.bump(MM, 1)
                        sch.wait("vector", MM, grzb)
                        sch.wait("vector", DVE, avev[0])
                        def frzbe(v, S, _lo=lo):
                            v.tensor_copy(rzb_sb[_lo:_lo + 64, :],
                                          ps_rz[_lo:_lo + 64, :]).then_inc(S[DVE], 1)
                        sch.emit("vector", frzbe)
                        rzbev[0] = sch.bump(DVE, 1)
                        sch.wait("vector", MM, state["avtok"][(i, 15)])
                        sch.wait("vector", DVE, rzbev[0])
                        def fave(v, S, _sg=sg, _lo=lo, _bk=bk):
                            for j in range(4):
                                c = c_of_pos(_sg * 4 + j)
                                ins = v.tensor_mul(
                                    ofm_ap(c // 2, _lo, _lo + 64),
                                    _bk[_lo:_lo + 64, j * 128:(j + 1) * 128],
                                    rzb_sb[_lo:_lo + 64, j * 128:(j + 1) * 128])
                                if j == 3:
                                    ins.then_inc(S[DVE], 1)
                        sch.emit("vector", fave)
                        avev[0] = sch.bump(DVE, 1)
                        if sg == 3:
                            state["avfree"] = avev[0]
                    return t_strip

                def mk_u(pair):
                    def t_u():
                        if pair == 0:
                            sch.wait("tensor", *wready["wu"])
                        for dc in (pair * 2, pair * 2 + 1):
                            ups = stg_u[dc % 2]
                            sch.wait("tensor", DVE, max(avev[0], uev[dc % 2]))
                            sch.wait("tensor", ACTs, act_base)
                            for ec in range(NKC):
                                def fu(t, S, _ec=ec, _dc=dc, _u=ups):
                                    ins = t.matmul(_u[:, 0:128], wutblk(_ec, _dc),
                                                   ar_h[:, 3072 + 128 * _ec:
                                                        3072 + 128 * _ec + 128],
                                                   start=(_ec == 0),
                                                   stop=(_ec == NKC - 1))
                                    if _ec == NKC - 1:
                                        ins.then_inc(S[MM], 1)
                                sch.emit("tensor", fu)
                            gu = sch.bump(MM, 1)
                            sch.wait("vector", MM, gu)
                            o0 = dc * NT + i * 128
                            def fux(v, S, _dc=dc, _o0=o0, _u=ups):
                                xsl = x_sb[:, _o0:_o0 + 128]
                                v.scalar_tensor_tensor(
                                    out=xsl, in0=_u[:, 0:128],
                                    scalar=bu_sb[:, _dc:_dc + 1], in1=xsl,
                                    op0=ALU.add, op1=ALU.add).then_inc(S[DVE], 1)
                            sch.emit("vector", fux)
                            uev[dc % 2] = sch.bump(DVE, 1)
                        if pair == 3:
                            state["ufree"] = max(uev)
                            state["avfree"] = max(state["avfree"], max(uev))
                            state["u_last"] = max(uev)
                    return t_u

                return [t_rz] + [mk_strip(sg) for sg in range(4)] + \
                    [mk_u(p) for p in range(4)]

            def qproj(half):
                t0h = half * 1024
                qev = [0, 0]
                for dc in range(NKC):
                    for st2 in range(2):
                        t0 = t0h + st2 * TS
                        pst = stg_u[(dc * 2 + st2) % 2]
                        sch.wait("tensor", *wready["wq"])
                        sch.wait("tensor", DVE, max(ln_done, state["avfree"],
                                                    state["ufree"]))
                        sch.wait("tensor", DVE, qev[(dc * 2 + st2) % 2])
                        for kc in range(NKC):
                            def fq(t, S, _kc=kc, _dc=dc, _t0=t0, _pst=pst):
                                ins = t.matmul(_pst[:, :], wqtblk(_kc, _dc),
                                               ln_ap(_kc, _t0, TS),
                                               start=(_kc == 0), stop=(_kc == NKC - 1))
                                if _kc == NKC - 1:
                                    ins.then_inc(S[MM], 1)
                            sch.emit("tensor", fq)
                        gq = sch.bump(MM, 1)
                        sch.wait("vector", MM, gq)
                        def fqe(v, S, _dc=dc, _toff=t0 - t0h, _pst=pst):
                            v.tensor_scalar_add(q_ap(_dc, _toff, TS), _pst[:, :],
                                                bq_sb[:, _dc:_dc + 1]).then_inc(S[DVE], 1)
                        sch.emit("vector", fqe)
                        qev[(dc * 2 + st2) % 2] = sch.bump(DVE, 1)
                return max(qev)

            q_ready = qproj(0)
            if "after_qproj0" in hooks:
                hooks["after_qproj0"](sch.val(MM))
            staging(0, q_ready)
            transposes(0)
            for i in range(16):
                if i == 8:
                    for a in range(3, 16):
                        av_row(7, a)
                    for st_fn in make_tail_stages(7):
                        st_fn()
                    q_ready = qproj(1)
                    if "after_qproj1" in hooks:
                        hooks["after_qproj1"](sch.val(MM))
                    staging(8, q_ready)
                    transposes(8)
                for a in range(3):
                    sc_row(i, a)
                if i % 8 != 0:
                    for a in range(3, 16):
                        av_row(i - 1, a)
                    transposes(i)
                stages = make_tail_stages(i - 1) if i % 8 != 0 else []
                for a in range(3, 16):
                    sc_row(i, a)
                    si = a - 3
                    if 0 <= si < len(stages):
                        stages[si]()
                for a in range(3):
                    av_row(i, a)
                if i % 8 != 7:
                    staging(i + 1, q_ready)
            for a in range(3, 16):
                av_row(15, a)
            for st_fn in make_tail_stages(15):
                st_fn()
            return state["u_last"], sch.val(MM)

        do1, doa, do2 = ("ffn1" in phases), ("attn" in phases), ("ffn2" in phases)
        seg_psums = []

        wst1 = {"w1": [None] * NSUP, "w2": [None] * NSUP,
                "sup_last_mm": [0] * NSUP, "issue_next": None}
        wst2 = {"w1": [None] * NSUP, "w2": [None] * NSUP,
                "sup_last_mm": [0] * NSUP, "issue_next": None}

        xw = 0
        ffn1_mm = 0
        if do1:
            with nc.psum_tensor("ps_d1", [128, TS], F32) as ps_d1, \
                 nc.psum_tensor("ps_w1", [1, TS], F32) as ps_w1, \
                 nc.psum_tensor("ps_v1", [1, TS], F32) as ps_v1:
                for s0 in range(2):
                    wst1["w1"][s0] = issue_w1(s0, s0 % 2, 0, "f1")
                    wst1["w2"][s0] = issue_w2(s0, s0 % 2, 0)

                def issue_next_ffn1(sup):
                    nxt = sup + 2
                    tok = wst1["sup_last_mm"][sup]
                    if nxt < NSUP:
                        wst1["w1"][nxt] = issue_w1(nxt, nxt % 2, tok, "f1")
                        wst1["w2"][nxt] = issue_w2(nxt, nxt % 2, tok)
                wst1["issue_next"] = issue_next_ffn1
                xw, _, ffn1_mm = emit_ffn(0, wst1, ps_d1, ps_w1, ps_v1)
                if doa:
                    wready["wq"] = issue_wqut("q", wst1["sup_last_mm"][10])
            sch.seg_break()

        attn_mm = ffn1_mm
        if doa:
            if "wq" not in wready:
                wready["wq"] = issue_wqut("q", ffn1_mm)
            hooks = {}
            def after_qproj0(mmtok):
                wu_wait = max(mmtok, wst1["sup_last_mm"][11] if do1 else 0)
                wready["wu"] = issue_wqut("u", wu_wait)
            def after_qproj1(mmtok):
                if do2:
                    wst2["w1"][0] = issue_w1(0, 0, mmtok, "f2")
            hooks["after_qproj0"] = after_qproj0
            hooks["after_qproj1"] = after_qproj1
            with nc.psum_tensor("s0", [128, TS], F32) as s0, \
                 nc.psum_tensor("s1", [128, TS], F32) as s1, \
                 nc.psum_tensor("d_att", [128, TS], F32) as d_att, \
                 nc.psum_tensor("ps_t", [128, 1024], BF16) as ps_t, \
                 nc.psum_tensor("ps_rz", [128, TS], F32) as ps_rz:
                xw, attn_mm = emit_attn(xw, s0, s1, d_att, ps_t, ps_rz, hooks)
            sch.seg_break()

        xtok = [xw] * NKC
        if do2:
            with nc.psum_tensor("ps_d2", [128, TS], F32) as ps_d2, \
                 nc.psum_tensor("ps_w2", [1, TS], F32) as ps_w2, \
                 nc.psum_tensor("ps_v2", [1, TS], F32) as ps_v2:
                if wst2["w1"][0] is None:
                    wst2["w1"][0] = issue_w1(0, 0, attn_mm, "f2")
                wst2["w2"][0] = issue_w2(0, 0, attn_mm)
                wst2["w1"][1] = issue_w1(1, 1, attn_mm, "f2")
                wst2["w2"][1] = issue_w2(1, 1, attn_mm)

                def issue_next_ffn2(sup):
                    nxt = sup + 2
                    tok = wst2["sup_last_mm"][sup]
                    if nxt < NSUP:
                        wst2["w1"][nxt] = issue_w1(nxt, nxt % 2, tok, "f2")
                        wst2["w2"][nxt] = issue_w2(nxt, nxt % 2, tok)
                wst2["issue_next"] = issue_next_ffn2
                _, xtok, _ = emit_ffn(xw, wst2, ps_d2, ps_w2, ps_v2)

        for dc in range(NKC):
            sch.wait("sync", DVE, xtok[dc])
            def fy(sy, S, _dc=dc):
                sy.dma_start(out=y_h[_dc * 128:(_dc + 1) * 128, :],
                             in_=x_sb[:, _dc * NT:(_dc + 1) * NT]).then_inc(S[OUT], 16)
            sch.emit("sync", fy)
            sch.bump(OUT, 16)
        sch.wait("sync", OUT, sch.val(OUT))

        segments = sch.finish()
        for seg in segments:
            if not any(seg[e] for e in ENGINES):
                continue
            with nc.Block() as block:
                @block.gpsimd
                def _(g, _seg=seg):
                    for f in _seg["gpsimd"]:
                        f(g, sems)

                @block.tensor
                def _(t, _seg=seg):
                    for f in _seg["tensor"]:
                        f(t, sems)

                @block.vector
                def _(v, _seg=seg):
                    for f in _seg["vector"]:
                        f(v, sems)

                @block.scalar
                def _(s, _seg=seg):
                    for f in _seg["scalar"]:
                        f(s, sems)

                @block.sync
                def _(sy, _seg=seg):
                    for f in _seg["sync"]:
                        f(sy, sems)

    return nc


_NC_CACHE = {}


def _get_nc(phases):
    key = tuple(phases)
    if key not in _NC_CACHE:
        _NC_CACHE[key] = build(phases)
    return _NC_CACHE[key]


def make_inputs(L0_attn_x, L0_attn_wq, L0_attn_bq, L0_attn_wu, L0_attn_bu,
                L0_ffn_gate_w, L0_ffn_w1, L0_ffn_b1, L0_ffn_w2, L0_ffn_b2,
                L0_ffn_sw1, L0_ffn_sb1, L0_ffn_sw2, L0_ffn_sb2):
    f = lambda a: np.asarray(a, dtype=np.float32)
    x = f(L0_attn_x)
    w1 = f(L0_ffn_w1)
    w2 = f(L0_ffn_w2)
    w1cat = np.ascontiguousarray(np.concatenate([f(L0_ffn_sw1), w1[0], w1[1]], axis=1))
    w2cat = np.ascontiguousarray(np.concatenate([f(L0_ffn_sw2), w2[0], w2[1]], axis=0))
    wqt_ = np.ascontiguousarray(f(L0_attn_wq).T)
    wut_ = np.ascontiguousarray(f(L0_attn_wu).T)
    b1cat = np.concatenate([f(L0_ffn_sb1), f(L0_ffn_b1)[0], f(L0_ffn_b1)[1]])
    b1h_ = np.ascontiguousarray(b1cat.reshape(96, 128).T)
    b2 = f(L0_ffn_b2)
    b2x_ = np.ascontiguousarray(np.concatenate(
        [b2[0].reshape(8, 128).T, b2[1].reshape(8, 128).T,
         f(L0_ffn_sb2).reshape(8, 128).T], axis=1))
    bqc_ = np.ascontiguousarray(f(L0_attn_bq).reshape(8, 128).T)
    buc_ = np.ascontiguousarray(f(L0_attn_bu).reshape(8, 128).T)
    gw = f(L0_ffn_gate_w)
    gwc_ = np.ascontiguousarray(np.concatenate(
        [gw[0].reshape(8, 128).T, gw[1].reshape(8, 128).T], axis=1))
    shared = dict(w1c=w1cat, w2c=w2cat, wqt=wqt_, wut=wut_, b1h=b1h_, b2x=b2x_,
                  bqc=bqc_, buc=buc_, gwc=gwc_)
    in_maps = []
    for i in range(8):
        m = dict(shared)
        m["x"] = np.ascontiguousarray(x[i].T)
        in_maps.append(m)
    return in_maps


def kernel(L0_attn_x, L0_attn_wq, L0_attn_bq, L0_attn_wu, L0_attn_bu,
           L0_ffn_gate_w, L0_ffn_w1, L0_ffn_b1, L0_ffn_w2, L0_ffn_b2,
           L0_ffn_sw1, L0_ffn_sb1, L0_ffn_sw2, L0_ffn_sb2,
           _phases=("ffn1", "attn", "ffn2")):
    in_maps = make_inputs(L0_attn_x, L0_attn_wq, L0_attn_bq, L0_attn_wu,
                          L0_attn_bu, L0_ffn_gate_w, L0_ffn_w1, L0_ffn_b1,
                          L0_ffn_w2, L0_ffn_b2, L0_ffn_sw1, L0_ffn_sb1,
                          L0_ffn_sw2, L0_ffn_sb2)
    nc = _get_nc(_phases)
    res = run_bass_kernel_spmd(nc, in_maps, core_ids=list(range(8)))
    out = np.stack([np.ascontiguousarray(r["y"].T) for r in res.results], axis=0)
    return out.astype(np.float32)


# version 19
# speedup vs baseline: 1.0588x; 1.0321x over previous
import contextlib
import numpy as np

import concourse.bass as bass
import concourse.mybir as mybir
from concourse.bass_utils import run_bass_kernel_spmd

F32 = mybir.dt.float32
BF16 = mybir.dt.bfloat16
AF = mybir.ActivationFunctionType
ALU = mybir.AluOpType

NT = 2048
D = 1024
NKC = 8
TS = 512
NSTRIP = NT // TS
HE = 1024
NSUP = 12
EPS = 1e-5
SCALE = 1.0 / 32.0

ENGINES = ("gpsimd", "tensor", "vector", "scalar", "sync")


def pos_of_c(c):
    return (c // 2) if c % 2 == 0 else 8 + (c // 2)


def c_of_pos(p):
    return 2 * p if p < 8 else 2 * (p - 8) + 1


class Sched:

    def __init__(self):
        self.segs = []
        self.ops = {e: [] for e in ENGINES}
        self.cnt = {}
        self.lastwait = {e: {} for e in ENGINES}

    def seg_break(self):
        self.segs.append(self.ops)
        self.ops = {e: [] for e in ENGINES}

    def finish(self):
        self.seg_break()
        return self.segs

    def emit(self, eng, fn):
        self.ops[eng].append(fn)

    def wait(self, eng, sem, val):
        if not val or val <= 0:
            return
        if self.lastwait[eng].get(sem, -1) >= val:
            return
        self.lastwait[eng][sem] = val
        self.emit(eng, lambda e, S, _s=sem, _v=val: e.wait_ge(S[_s], _v))

    def bump(self, sem, by):
        self.cnt[sem] = self.cnt.get(sem, 0) + by
        return self.cnt[sem]

    def val(self, sem):
        return self.cnt.get(sem, 0)


def build(phases=("ffn1", "attn", "ffn2")):
    nc = bass.Bass("TRN2", target_bir_lowering=False)

    x_h = nc.dram_tensor("x", [D, NT], F32, kind="ExternalInput").ap()
    w1c = nc.dram_tensor("w1c", [D, 3 * 4096], F32, kind="ExternalInput").ap()
    w2c = nc.dram_tensor("w2c", [3 * 4096, D], F32, kind="ExternalInput").ap()
    wqt = nc.dram_tensor("wqt", [D, D], F32, kind="ExternalInput").ap()
    wut = nc.dram_tensor("wut", [D, D], F32, kind="ExternalInput").ap()
    b1h = nc.dram_tensor("b1h", [128, 96], F32, kind="ExternalInput").ap()
    b2x = nc.dram_tensor("b2x", [128, 24], F32, kind="ExternalInput").ap()
    bqc = nc.dram_tensor("bqc", [128, 8], F32, kind="ExternalInput").ap()
    buc = nc.dram_tensor("buc", [128, 8], F32, kind="ExternalInput").ap()
    gwc = nc.dram_tensor("gwc", [128, 16], F32, kind="ExternalInput").ap()
    y_h = nc.dram_tensor("y", [D, NT], F32, kind="ExternalOutput").ap()

    es = contextlib.ExitStack()
    with es:
        sb = lambda n, s, d: es.enter_context(nc.sbuf_tensor(n, s, d))
        mksem = lambda n: es.enter_context(nc.semaphore(n))

        x_sb = sb("x_sb", [128, NKC * NT], F32)
        ar_ln = sb("ar_ln", [128, 16384], BF16)
        ar_w1 = sb("ar_w1", [128, 16384], BF16)
        ar_w2 = sb("ar_w2", [128, 16384], BF16)
        ar_h = sb("ar_h", [128, 4096], BF16)
        ar_b = sb("ar_b", [128, 8192], BF16)
        p_ex = sb("p_ex", [128, 8192], BF16)
        rzb_sb = sb("rzb_sb", [128, TS], BF16)
        xb_s = rzb_sb
        m0row = sb("m0row", [1, TS], BF16)
        m1row = sb("m1row", [1, TS], BF16)
        ones_col = sb("ones_col", [128, 1], BF16)
        ones_colf = sb("ones_colf", [128, 1], F32)
        onesLf = sb("onesLf", [1, 128], F32)
        onesLb = sb("onesLb", [1, 128], BF16)
        ident = sb("ident", [128, 128], BF16)
        e2mat = sb("e2mat", [128, 128], BF16)
        sel16 = sb("sel16", [16, 1024], BF16)
        b1hf = sb("b1hf", [128, 96], F32)
        b2sh = sb("b2sh", [128, 8], F32)
        bq_sb = sb("bq_sb", [128, 8], F32)
        bu_sb = sb("bu_sb", [128, 8], F32)
        gw_bf = sb("gw_bf", [128, 16], BF16)
        zcols = sb("zcols", [128, 16], F32)
        zc4a = sb("zc4a", [128, 64], F32)
        zc4b = sb("zc4b", [128, 64], F32)
        rzbf = sb("rzbf", [128, 16], BF16)
        trz = sb("trz", [16, 128], BF16)

        ps_a = es.enter_context(nc.psum_tensor("ps_a", [128, TS], F32))
        ps_b = es.enter_context(nc.psum_tensor("ps_b", [128, TS], F32))
        ps_c = es.enter_context(nc.psum_tensor("ps_c", [128, TS], F32))

        sems = {}
        for s in ("w1a", "w1b", "w2a", "w2b", "wx", "x", "mm", "act", "dve", "gp", "out"):
            sems[s] = mksem("s_" + s)

        sch = Sched()
        X, MM, ACTs, DVE, GP, OUT = "x", "mm", "act", "dve", "gp", "out"

        scaleB = ar_b[:, 6144:7168].bitcast(F32)
        meanB = ar_b[:, 7168:8192].bitcast(F32)
        r1 = meanB[0:1, :]
        r2 = scaleB[0:1, :]
        r3 = m0row

        def lnm_ap(kc):
            return ar_b[:, kc * TS:(kc + 1) * TS]

        def mask_ap(which, strip):
            off = 8 * TS + which * NT
            return ar_b[:, off + strip * TS: off + (strip + 1) * TS]

        def q_ap(kc, t0, w):
            return ar_b[:, kc * 1024 + t0: kc * 1024 + t0 + w]

        def ln_ap(kc, t0, w):
            half = t0 // 1024
            tt = t0 - half * 1024
            base = half * 8192 + kc * 1024 + tt
            return ar_ln[:, base: base + w]

        def prow_ap(a):
            if a < 8:
                return ar_w2[:, a * NT:(a + 1) * NT]
            if a < 12:
                return ar_ln[:, (a - 8) * NT:(a - 7) * NT]
            return p_ex[:, (a - 12) * NT:(a - 11) * NT]

        def w1blk(slot, kc, hc):
            o = slot * 8192 + kc * 1024 + hc * 128
            return ar_w1[:, o:o + 128]

        def w2blk(slot, hc, dc):
            o = slot * 8192 + hc * 1024 + dc * 128
            return ar_w2[:, o:o + 128]

        def wqtblk(kc, dc):
            o = kc * 1024 + dc * 128
            return ar_w1[:, o:o + 128]

        def wutblk(kc, dc):
            o = 8192 + kc * 1024 + dc * 128
            return ar_w1[:, o:o + 128]

        def qq_ap(posc, lo64):
            return ar_h[lo64:lo64 + 64, 128 * posc: 128 * posc + 128]

        def qT_ap(cp, w=64):
            return ar_h[:, 2048 + 64 * cp: 2048 + 64 * cp + w]

        def ofm_ap(ec, lo, hi):
            return ar_h[lo:hi, 3072 + 128 * ec: 3072 + 128 * ec + 128]

        def g_init(g, S):
            n = 0
            g.memset(ones_col[:, :], 1.0)
            g.memset(ones_colf[:, :], 1.0)
            g.memset(onesLf[:, :], 1.0)
            g.memset(onesLb[:, :], 1.0)
            g.memset(ident[:, :], 0.0).then_inc(S[GP], 1); n += 1
            g.wait_ge(S[GP], n)
            g.affine_select(out=ident[:, :], in_=ident[:, :], compare_op=ALU.not_equal,
                            fill=1.0, base=0, pattern=[[-1, 128]],
                            channel_multiplier=1).then_inc(S[GP], 1); n += 1
            g.memset(e2mat[:, :], 0.0).then_inc(S[GP], 1); n += 1
            for base in (0, 64, -64):
                g.wait_ge(S[GP], n)
                g.affine_select(out=e2mat[:, :], in_=e2mat[:, :], compare_op=ALU.not_equal,
                                fill=1.0, base=base, pattern=[[-1, 128]],
                                channel_multiplier=1).then_inc(S[GP], 1); n += 1
            g.memset(sel16[:, :], 0.0).then_inc(S[GP], 1); n += 1
            g.wait_ge(S[GP], n)
            g.affine_select(
                out=sel16[:, :].rearrange("p (c d) -> p c d", c=16),
                in_=sel16[:, :].rearrange("p (c d) -> p c d", c=16),
                compare_op=ALU.not_equal, fill=1.0, base=0,
                pattern=[[-1, 16], [0, 64]], channel_multiplier=1,
            ).then_inc(S[GP], 1); n += 1
        sch.emit("gpsimd", g_init)
        tok_gpinit = sch.bump(GP, 8)

        def sy_load(sy, S):
            sy.dma_start(
                out=x_sb[:, :].rearrange("p (kc t) -> p kc t", kc=NKC),
                in_=x_h.rearrange("(kc p) t -> p kc t", p=128),
            ).then_inc(S[X], 16)
            for src, dst in ((b1h, b1hf), (bqc, bq_sb), (buc, bu_sb)):
                sy.dma_start(out=dst[:, :], in_=src[:, :]).then_inc(S[X], 16)
            sy.dma_start(out=b2sh[:, :], in_=b2x[:, 16:24]).then_inc(S[X], 16)
            sy.dma_start(out=meanB[:, 0:16], in_=b2x[:, 0:16]).then_inc(S[X], 16)
        sch.emit("sync", sy_load)
        tok_xload = sch.bump(X, 16 * 6)

        def g_gw(g, S):
            g.dma_start(out=gw_bf[:, :], in_=gwc[:, :]).then_inc(S["wx"], 16)
        sch.emit("gpsimd", g_gw)
        tok_gw = ("wx", sch.bump("wx", 16))

        sch.wait("scalar", X, tok_xload)
        def sc_bias(s, S):
            s.mul(b1hf[:, :], b1hf[:, :], 0.5).then_inc(S[ACTs], 1)
        sch.emit("scalar", sc_bias)
        tok_b1 = sch.bump(ACTs, 1)

        sch.wait("vector", X, tok_xload)
        base_dve = sch.val(DVE)
        def dv_bias(v, S, _b=base_dve):
            v.tensor_add(b2sh[:, :], b2sh[:, :], meanB[:, 0:8]).then_inc(S[DVE], 1)
            v.wait_ge(S[DVE], _b + 1)
            v.tensor_add(b2sh[:, :], b2sh[:, :], meanB[:, 8:16]).then_inc(S[DVE], 1)
            v.wait_ge(S[DVE], _b + 2)
            v.tensor_scalar_mul(b2sh[:, :], b2sh[:, :], 0.5).then_inc(S[DVE], 1)
        sch.emit("vector", dv_bias)
        tok_b2 = sch.bump(DVE, 3)

        wready = {}

        def issue_w1(sup, slot, wait_mm, arena_name):
            sch.wait("gpsimd", MM, wait_mm)
            wsem = "w1a" if slot == 0 else "w1b"
            def f(g, S, _sup=sup, _slot=slot, _ws=wsem):
                g.dma_start(
                    out=ar_w1[:, _slot * 8192:(_slot + 1) * 8192].rearrange(
                        "p (kc h) -> p kc h", kc=NKC),
                    in_=w1c[:, _sup * HE:(_sup + 1) * HE].rearrange(
                        "(kc p) h -> p kc h", p=128),
                ).then_inc(S[_ws], 16)
            sch.emit("gpsimd", f)
            return (wsem, sch.bump(wsem, 16))

        def issue_w2(sup, slot, wait_mm):
            sch.wait("gpsimd", MM, wait_mm)
            wsem = "w2a" if slot == 0 else "w2b"
            def f(g, S, _sup=sup, _slot=slot, _ws=wsem):
                g.dma_start(
                    out=ar_w2[:, _slot * 8192:(_slot + 1) * 8192].rearrange(
                        "p (hc d) -> p hc d", hc=NKC),
                    in_=w2c[_sup * HE:(_sup + 1) * HE, :].rearrange(
                        "(hc p) d -> p hc d", p=128),
                ).then_inc(S[_ws], 16)
            sch.emit("gpsimd", f)
            return (wsem, sch.bump(wsem, 16))

        def issue_wqut(which, wait_mm):
            sch.wait("gpsimd", MM, wait_mm)
            src = wqt if which == "q" else wut
            off = 0 if which == "q" else 8192
            def f(g, S, _src=src, _off=off):
                g.dma_start(
                    out=ar_w1[:, _off:_off + 8192].rearrange("p (kc d) -> p kc d", kc=NKC),
                    in_=_src.rearrange("(kc p) d -> p kc d", p=128),
                ).then_inc(S["wx"], 16)
            sch.emit("gpsimd", f)
            return ("wx", sch.bump("wx", 16))

        pexf = p_ex[:, 0:4096].bitcast(F32)
        mean2 = pexf[:, 1024:1536]
        scale2 = pexf[:, 1536:2048]
        xb2 = p_ex[:, 4096:4608]
        gate_r1 = p_ex[:, 4608:5632].bitcast(F32)[0:1, :]
        gate_r2 = p_ex[:, 5632:6656].bitcast(F32)[0:1, :]

        def make_ln_stages(is_ffn, xwaits, statps, bcps, ext=None):
            ext = ext if ext is not None else {}
            ln_tok = [0] * NSTRIP
            mask_tok = [0] * NSTRIP
            s1ev = [0] * NSTRIP
            s2ev = [0] * NSTRIP
            statmm = [0] * NSTRIP
            sqtok = [0] * NSTRIP
            bcev = [0] * NSTRIP
            bfree = [0, 0]
            sq_mm_last = [0, 0]
            ln_ktok = [[0] * NKC for _ in range(NSTRIP)]
            ext["ln_ktok"] = ln_ktok

            def tiles(st):
                if st % 2 == 0:
                    return meanB, scaleB
                return mean2, scale2

            def scratch(st):
                return xb_s if st % 2 == 0 else xb2

            def stage_stats(st):
                t0 = st * TS
                sp1 = statps[0]
                sp2 = statps[1]
                mB, sB = tiles(st)
                r1, r2 = mB[0:1, :], sB[0:1, :]
                sch.wait("tensor", GP, tok_gpinit)
                sch.wait("tensor", X, tok_xload)
                sch.wait("tensor", DVE, max(xwaits[st], tok_b2,
                                            s1ev[st - 1] if st >= 1 else 0))
                for kc in range(NKC):
                    def fmm(t, S, _kc=kc, _t0=t0, _sp=sp1):
                        t.matmul(_sp, ones_colf[:, :],
                                 x_sb[:, _kc * NT + _t0: _kc * NT + _t0 + TS],
                                 start=(_kc == 0), stop=(_kc == NKC - 1)).then_inc(S[MM], 1)
                    sch.emit("tensor", fmm)
                    prev_mm = sch.bump(MM, 1)
                s1mm = prev_mm
                sch.wait("vector", MM, s1mm)
                sch.wait("vector", DVE, ln_tok[st - 2] if st >= 2 else 0)
                def fs1(v, S, _sp=sp1, _r=r1):
                    v.tensor_scalar_mul(_r, _sp, 1.0 / D).then_inc(S[DVE], 1)
                sch.emit("vector", fs1)
                s1ev[st] = sch.bump(DVE, 1)
                sch.wait("scalar", DVE, max(xwaits[st], tok_b2))
                sch.wait("scalar", X, tok_xload)
                for kc in range(NKC):
                    xb = xb_s if kc % 2 == 0 else xb2
                    sch.wait("scalar", MM, sq_mm_last[kc % 2])
                    def fsq(s, S, _kc=kc, _t0=t0, _xb=xb):
                        s.activation(_xb[:, :],
                                     x_sb[:, _kc * NT + _t0: _kc * NT + _t0 + TS],
                                     AF.Square).then_inc(S[ACTs], 1)
                    sch.emit("scalar", fsq)
                    sqtok[st] = sch.bump(ACTs, 1)
                    sch.wait("tensor", ACTs, sqtok[st])
                    if kc == 0:
                        sch.wait("tensor", DVE, s2ev[st - 1] if st >= 1 else 0)
                    def fmm2(t, S, _kc=kc, _sp=sp2, _xb=xb):
                        t.matmul(_sp, ones_col[:, :], _xb[:, :],
                                 start=(_kc == 0), stop=(_kc == NKC - 1)).then_inc(S[MM], 1)
                    sch.emit("tensor", fmm2)
                    prev_mm = sch.bump(MM, 1)
                    sq_mm_last[kc % 2] = prev_mm
                statmm[st] = prev_mm
                sch.wait("vector", MM, prev_mm)
                def fs2(v, S, _sp=sp2, _r=r2):
                    v.tensor_scalar(out=_r, in0=_sp, scalar1=1.0 / D, scalar2=EPS,
                                    op0=ALU.mult, op1=ALU.add).then_inc(S[DVE], 1)
                sch.emit("vector", fs2)
                s2ev[st] = sch.bump(DVE, 1)

            def stage_post(st):
                t0 = st * TS
                mB, sB = tiles(st)
                r1, r2 = mB[0:1, :], sB[0:1, :]
                bA = bcps[0]
                bB = bcps[1]
                sch.wait("vector", DVE, s2ev[st])
                base_r = sch.val(DVE)
                def frow2(v, S, _r1=r1, _r2=r2, _b=base_r):
                    v.wait_ge(S[DVE], _b)
                    v.tensor_mul(r3[:, :], _r1, _r1).then_inc(S[DVE], 1)
                    v.wait_ge(S[DVE], _b + 1)
                    v.scalar_tensor_tensor(out=_r2, in0=r3[:, :], scalar=-1.0,
                                           in1=_r2, op0=ALU.mult,
                                           op1=ALU.add).then_inc(S[DVE], 1)
                sch.emit("vector", frow2)
                rowb = sch.bump(DVE, 2)
                sch.wait("scalar", DVE, rowb)
                def frow3(s, S, _r2=r2):
                    s.activation(_r2, _r2, AF.Sqrt).then_inc(S[ACTs], 1)
                sch.emit("scalar", frow3)
                rowc = sch.bump(ACTs, 1)
                sch.wait("vector", ACTs, rowc)
                def frow4(v, S, _r2=r2):
                    v.reciprocal(_r2, _r2).then_inc(S[DVE], 1)
                sch.emit("vector", frow4)
                rowd = sch.bump(DVE, 1)
                sch.wait("tensor", DVE, rowd)
                sch.wait("tensor", DVE, bfree[0])
                def fbc1(t, S, _r=r1, _bp=bA):
                    t.matmul(_bp, onesLf[:, :], _r, start=True, stop=True).then_inc(S[MM], 1)
                sch.emit("tensor", fbc1)
                bct1 = sch.bump(MM, 1)
                sch.wait("tensor", DVE, bfree[1])
                def fbc2(t, S, _r=r2, _bp=bB):
                    t.matmul(_bp, onesLf[:, :], _r, start=True, stop=True).then_inc(S[MM], 1)
                sch.emit("tensor", fbc2)
                bct2 = sch.bump(MM, 1)
                sch.wait("vector", MM, bct2)
                base_e = sch.val(DVE)
                def fbe(v, S, _mB=mB, _sB=sB, _bA=bA, _bB=bB, _b=base_e):
                    v.tensor_copy(_mB[:, :], _bA).then_inc(S[DVE], 1)
                    v.wait_ge(S[DVE], _b + 1)
                    v.tensor_copy(_sB[:, :], _bB).then_inc(S[DVE], 1)
                sch.emit("vector", fbe)
                bcev[st] = sch.bump(DVE, 2)
                bfree[0] = bcev[st] - 1
                bfree[1] = bcev[st]
                sch.wait("vector", DVE, bcev[st])
                for kc in range(NKC):
                    base_n = sch.val(DVE)
                    def fn(v, S, _kc=kc, _t0=t0, _b=base_n, _mB=mB, _sB=sB):
                        v.wait_ge(S[DVE], _b)
                        v.tensor_sub(ln_ap(_kc, _t0, TS),
                                     x_sb[:, _kc * NT + _t0: _kc * NT + _t0 + TS],
                                     _mB[:, :]).then_inc(S[DVE], 1)
                        v.wait_ge(S[DVE], _b + 1)
                        v.tensor_mul(ln_ap(_kc, _t0, TS), ln_ap(_kc, _t0, TS),
                                     _sB[:, :]).then_inc(S[DVE], 1)
                    sch.emit("vector", fn)
                    ln_tok[st] = sch.bump(DVE, 2)
                    ln_ktok[st][kc] = ln_tok[st]


            def stage_gate(st):
                t0 = st * TS
                r1, r2 = gate_r1, gate_r2
                bA = bcps[0]
                gevac = 0
                for e in range(2):
                    gp_ps = bcps[e]
                    for kc in range(NKC):
                        sch.wait("tensor", *tok_gw)
                        sch.wait("tensor", DVE, ln_tok[st])
                        sch.wait("tensor", DVE, bfree[e])
                        def fg(t, S, _e=e, _kc=kc, _t0=t0, _sp=gp_ps):
                            t.matmul(_sp[0:1, :],
                                     gw_bf[:, _e * 8 + _kc:_e * 8 + _kc + 1],
                                     ln_ap(_kc, _t0, TS),
                                     start=(_kc == 0), stop=(_kc == NKC - 1)
                                     ).then_inc(S[MM], 1)
                        sch.emit("tensor", fg)
                        gtok = sch.bump(MM, 1)
                    sch.wait("vector", MM, gtok)
                    sch.wait("vector", DVE, ln_tok[st])
                    dstrow = r1 if e == 0 else r2
                    def fge(v, S, _d=dstrow, _sp=gp_ps):
                        v.tensor_copy(_d, _sp[0:1, :]).then_inc(S[DVE], 1)
                    sch.emit("vector", fge)
                    gevac = sch.bump(DVE, 1)
                    bfree[e] = gevac
                sch.wait("vector", DVE, gevac)
                base_mr = sch.val(DVE)
                def fmrow(v, S, _r1=r1, _r2=r2, _b=base_mr):
                    v.tensor_tensor(m0row[:, :], _r1, _r2,
                                    ALU.is_ge).then_inc(S[DVE], 1)
                    v.wait_ge(S[DVE], _b + 1)
                    v.tensor_tensor(m1row[:, :], _r1, _r2,
                                    ALU.is_lt).then_inc(S[DVE], 1)
                sch.emit("vector", fmrow)
                mrow_tok = sch.bump(DVE, 2)
                sch.wait("tensor", DVE, mrow_tok)
                sch.wait("tensor", DVE, max(bfree[0], bfree[1]))
                bB2 = bcps[1]
                def fmb(t, S, _bp=bA, _bp2=bB2):
                    t.matmul(_bp, onesLb[:, :], m0row[:, :],
                             start=True, stop=True).then_inc(S[MM], 1)
                    t.matmul(_bp2, onesLb[:, :], m1row[:, :],
                             start=True, stop=True).then_inc(S[MM], 1)
                sch.emit("tensor", fmb)
                mbt = sch.bump(MM, 2)
                sch.wait("vector", MM, mbt)
                sch.wait("vector", DVE, ext.get("all_ln", 0))
                base_me = sch.val(DVE)
                def fmbe(v, S, _st=st, _bp=bA, _bp2=bB2, _b=base_me):
                    v.tensor_copy(mask_ap(0, _st), _bp).then_inc(S[DVE], 1)
                    v.wait_ge(S[DVE], _b + 1)
                    v.tensor_copy(mask_ap(1, _st), _bp2).then_inc(S[DVE], 1)
                sch.emit("vector", fmbe)
                mask_tok[st] = sch.bump(DVE, 2)
                bfree[0] = mask_tok[st] - 1
                bfree[1] = mask_tok[st]

            stats_fns = [lambda _st=st: stage_stats(_st) for st in range(NSTRIP)]
            post_fns = [lambda _st=st: stage_post(_st) for st in range(NSTRIP)]
            gate_fns = [lambda _st=st: stage_gate(_st) for st in range(NSTRIP)]
            return stats_fns, post_fns, gate_fns, ln_tok, mask_tok

        def emit_ln(is_ffn, xwaits, statps, bcps):
            sf, pf, gf, ln_tok, mask_tok = make_ln_stages(is_ffn, xwaits, statps, bcps)
            for st in range(NSTRIP):
                sf[st]()
                if st > 0:
                    pf[st - 1]()
            pf[NSTRIP - 1]()
            if is_ffn:
                for st in range(NSTRIP):
                    gf[st]()
            return ln_tok, mask_tok

        def emit_ffn(xwait_dve, wst, ps_d, ps_w, ps_w2, ps_e, ps_f):
            ext = {}
            sf, pf, gf, ln_tok, mask_tok = make_ln_stages(
                True, [xwait_dve] * NSTRIP,
                (ps_w[:, :], ps_w2[:, :]), (ps_e[:, :], ps_f[:, :]), ext)
            ln_ktok = ext["ln_ktok"]
            h_relu_tok = [0] * NKC
            stt_tok = [[0] * NSTRIP for _ in range(NKC)]
            ps1_free = [0, 0]
            ps2_free = [0, 0]
            prev_mm1 = [0]
            lnm_tok = [None]
            lnm_ktok = [0] * NKC

            pending_lnm = []

            def build_lnm_pieces(st, klass, wait_mm):
                which = 0 if klass == 1 else 1
                def piece(kc):
                    sch.wait("vector", DVE, max(ln_tok[st], mask_tok[st]))
                    sch.wait("vector", MM, wait_mm)
                    def fm(v, S, _kc=kc, _st=st, _w=which):
                        v.tensor_mul(lnm_ap(_kc), ln_ap(_kc, _st * TS, TS),
                                     mask_ap(_w, _st)).then_inc(S[DVE], 1)
                    sch.emit("vector", fm)
                    lnm_tok[0] = sch.bump(DVE, 1)
                    lnm_ktok[kc] = lnm_tok[0]
                return [lambda _kc=kc: piece(_kc) for kc in range(NKC)]

            def build_lnm(st, klass, wait_mm):
                for p in build_lnm_pieces(st, klass, wait_mm):
                    p()
                return lnm_tok[0]

            def do_mm1(sup, st, klass, slot):
                if wst["w1"][sup]:
                    sch.wait("tensor", *wst["w1"][sup])
                if wst["w2"][sup]:
                    sch.wait("tensor", *wst["w2"][sup])
                if klass == 0:
                    rhs_sem, rhs_tok = DVE, ln_tok[st]
                    rhs = lambda kc, _t0=st * TS: ln_ap(kc, _t0, TS)
                    rhs_ktok = [ln_ktok[st][kc] for kc in range(NKC)]
                else:
                    rhs_sem, rhs_tok = DVE, lnm_tok[0]
                    rhs = lambda kc: lnm_ap(kc)
                    rhs_ktok = list(lnm_ktok)
                for hc in range(NKC):
                    pst = (ps_a, ps_b)[hc % 2]
                    sch.wait("tensor", ACTs, ps1_free[hc % 2])
                    if hc > 0:
                        sch.wait("tensor", rhs_sem, rhs_tok)
                    for kc in range(NKC):
                        if hc == 0:
                            sch.wait("tensor", rhs_sem, rhs_ktok[kc])
                        def f1(t, S, _kc=kc, _hc=hc, _slot=slot, _pst=pst, _rhs=rhs):
                            ins = t.matmul(_pst[:, :], w1blk(_slot, _kc, _hc),
                                           _rhs(_kc),
                                           start=(_kc == 0), stop=(_kc == NKC - 1))
                            if _kc == NKC - 1:
                                ins.then_inc(S[MM], 1)
                        sch.emit("tensor", f1)
                    g1 = sch.bump(MM, 1)
                    sch.wait("scalar", MM, g1)
                    sch.wait("scalar", ACTs, tok_b1)
                    def fr(s, S, _hc=hc, _sup=sup, _pst=pst):
                        s.activation(ar_h[:, _hc * TS:(_hc + 1) * TS], _pst[:, :],
                                     AF.Relu,
                                     bias=b1hf[:, _sup * 8 + _hc:_sup * 8 + _hc + 1],
                                     scale=0.5).then_inc(S[ACTs], 1)
                    sch.emit("scalar", fr)
                    h_relu_tok[hc] = sch.bump(ACTs, 1)
                    ps1_free[hc % 2] = h_relu_tok[hc]
                prev_mm1[0] = sch.val(MM)
                if klass > 0:
                    nst, nkl = (st + 1, klass) if st < NSTRIP - 1 else \
                        (0, (sup + 1) // 4 if sup + 1 < NSUP else None)
                    if nkl:
                        pending_lnm.extend(build_lnm_pieces(nst, nkl, prev_mm1[0]))
                    else:
                        lnm_tok[0] = None

            def do_mm2(sup, st, slot, mm2_extra):
                h_all = max(h_relu_tok)
                for dc in range(NKC):
                    pst = (ps_c, ps_d)[dc % 2]
                    sch.wait("tensor", DVE, max(ps2_free[dc % 2], mm2_extra))
                    if dc > 0:
                        sch.wait("tensor", ACTs, h_all)
                    for hc in range(NKC):
                        if dc == 0:
                            sch.wait("tensor", ACTs, h_relu_tok[hc])
                        def f2(t, S, _hc=hc, _dc=dc, _slot=slot, _pst=pst):
                            ins = t.matmul(_pst[:, :], w2blk(_slot, _hc, _dc),
                                           ar_h[:, _hc * TS:(_hc + 1) * TS],
                                           start=(_hc == 0), stop=(_hc == NKC - 1))
                            if _hc == NKC - 1:
                                ins.then_inc(S[MM], 1)
                        sch.emit("tensor", f2)
                    g2 = sch.bump(MM, 1)
                    sch.wait("vector", MM, g2)
                    sch.wait("vector", DVE, tok_b2)
                    o0 = dc * NT + st * TS
                    if sup == 0:
                        def fx(v, S, _dc=dc, _o0=o0, _pst=pst):
                            xsl = x_sb[:, _o0:_o0 + TS]
                            v.scalar_tensor_tensor(
                                out=xsl, in0=_pst[:, :],
                                scalar=b2sh[:, _dc:_dc + 1], in1=xsl,
                                op0=ALU.add, op1=ALU.add).then_inc(S[DVE], 1)
                    else:
                        def fx(v, S, _o0=o0, _pst=pst):
                            xsl = x_sb[:, _o0:_o0 + TS]
                            v.tensor_add(xsl, xsl, _pst[:, :]).then_inc(S[DVE], 1)
                    sch.emit("vector", fx)
                    stt_tok[dc][st] = sch.bump(DVE, 1)
                    ps2_free[dc % 2] = stt_tok[dc][st]
                    if pending_lnm:
                        pending_lnm.pop(0)()
                while pending_lnm:
                    pending_lnm.pop(0)()

            sf[0](); sf[1](); pf[0]()
            for st in range(NSTRIP):
                do_mm1(0, st, 0, 0)
                if st + 2 < NSTRIP:
                    sf[st + 2]()
                if st + 1 < NSTRIP:
                    pf[st + 1]()
                do_mm2(0, st, 0, sch.val(DVE))
                if st == 2:
                    ext["all_ln"] = max(ln_tok)
                if st >= 2:
                    gf[st - 2]()
            for st in (2, 3):
                gf[st]()
            wst["sup_last_mm"][0] = sch.val(MM)
            if wst["issue_next"] is not None:
                wst["issue_next"](0)
            for sup in range(1, NSUP):
                klass = sup // 4
                slot = sup % 2
                if klass > 0 and lnm_tok[0] is None:
                    lnm_tok[0] = build_lnm(0, klass, prev_mm1[0])
                for st in range(NSTRIP):
                    do_mm1(sup, st, klass, slot)
                    do_mm2(sup, st, slot, 0)
                wst["sup_last_mm"][sup] = sch.val(MM)
                if wst["issue_next"] is not None:
                    wst["issue_next"](sup)
            xdve = max(max(row) for row in stt_tok)
            xtok = [stt_tok[dc][NSTRIP - 1] for dc in range(NKC)]
            return xdve, xtok, sch.val(MM)

        def emit_attn(xwait_dve, s0, s1, d_att, ps_t, ps_rz, hooks):
            ln_tok, _ = emit_ln(False, [xwait_dve] * NSTRIP,
                                (s0[0:1, :], s1[0:1, :]),
                                (ps_c[:, :], d_att[:, :]))
            ln_done = sch.val(DVE)
            av_of_strip = [(ps_a, 0), (ps_b, 0), (ps_c, 64), (d_att, 64)]
            sregs3 = (s0, s1, ps_t[:, :].bitcast(F32))
            stg = (s0, s1)
            stg_u = (ps_a, ps_b)

            state = {
                "exp_tok": {},
                "avtok": {},
                "avfree": 0,
                "stage_tok": {},
                "stage_ev": 0,
                "ufree": 0,
                "qT_tok": {},
                "zsum_tok": 0,
                "last_qT": 0,
                "trz_evac": 0,
                "u_last": 0,
                "sreg_last": {},
            }

            def staging(i, q_ready):
                qoff = (i % 8) * 128
                act_prev = state["stage_ev"]
                for b4 in range(4):
                    ps = stg[b4 % 2]
                    sch.wait("tensor", ACTs, state["sreg_last"].get(b4 % 2, 0))
                    sch.wait("tensor", DVE, max(q_ready, act_prev))
                    for j in range(4):
                        c = c_of_pos(b4 * 4 + j)
                        hb = (c % 2) * 64
                        def fqq(t, S, _c=c, _hb=hb, _q=qoff, _j=j, _ps=ps):
                            ins = t.matmul(_ps[:, _j * 128:(_j + 1) * 128],
                                           e2mat[_hb:_hb + 64, :],
                                           q_ap(_c // 2, _q, 128)[_hb:_hb + 64, :],
                                           start=(_j == 0), stop=(_j == 3))
                            if _j == 3:
                                ins.then_inc(S[MM], 1)
                        sch.emit("tensor", fqq)
                    gqq = sch.bump(MM, 1)
                    sch.wait("vector", MM, gqq)
                    def fqqe(v, S, _b4=b4, _ps=ps):
                        v.tensor_copy(ar_h[:, _b4 * TS:(_b4 + 1) * TS],
                                      _ps[:, :]).then_inc(S[DVE], 1)
                    sch.emit("vector", fqqe)
                    act_prev = sch.bump(DVE, 1)
                state["stage_ev"] = act_prev
                state["stage_tok"][i] = act_prev

            def transposes(i):
                sch.wait("tensor", DVE, state["stage_tok"][i])
                sch.wait("tensor", ACTs, state["trz_evac"])
                sch.wait("tensor", ACTs, state["sreg_last"].get(2, 0))
                for cp in range(16):
                    def ftr(t, S, _cp=cp):
                        ins = t.matmul(ps_t[:, _cp * 64:(_cp + 1) * 64],
                                       qq_ap(pos_of_c(_cp), 0),
                                       ident[0:64, 0:64],
                                       start=(_cp == 0), stop=(_cp == 15),
                                       is_transpose=True)
                        if _cp == 15:
                            ins.then_inc(S[MM], 1)
                    sch.emit("tensor", ftr)
                gtr = sch.bump(MM, 1)
                sch.wait("vector", MM, gtr)
                def ftre(v, S):
                    v.tensor_copy(ar_h[:, 2048:3072], ps_t[:, 0:1024]).then_inc(S[DVE], 1)
                sch.emit("vector", ftre)
                state["qT_tok"][i] = sch.bump(DVE, 1)
                state["last_qT"] = state["qT_tok"][i]

            def sc_row(i, a):
                pa = pos_of_c(a)
                h = (a % 2) * 64
                for q4 in range(4):
                    qi = a * 4 + q4
                    ridx = qi % 2 if qi < 16 else (qi - 16) % 3
                    sreg = sregs3[ridx]
                    sch.wait("tensor", DVE, state["stage_tok"][i])
                    sch.wait("tensor", ACTs, state["sreg_last"].get(ridx, 0))
                    if ridx == 2:
                        sch.wait("tensor", ACTs, state["trz_evac"])
                        sch.wait("tensor", DVE, state["qT_tok"][i])
                    def fs(t, S, _pa=pa, _h=h, _q4=q4, _sr=sreg):
                        t.matmul(_sr[:, :],
                                 qq_ap(_pa, _h),
                                 ar_h[_h:_h + 64, _q4 * TS:(_q4 + 1) * TS],
                                 start=True, stop=True).then_inc(S[MM], 1)
                    sch.emit("tensor", fs)
                    gsc = sch.bump(MM, 1)
                    sch.wait("scalar", MM, gsc)
                    sch.wait("scalar", MM, state["avtok"].get((i - 1, a), 0))
                    sch.wait("scalar", DVE, state["zsum_tok"])
                    zdst = zc4a if i % 2 == 0 else zc4b
                    def fexp(s, S, _a=a, _q4=q4, _sr=sreg, _z=zdst):
                        s.activation(prow_ap(_a)[:, _q4 * TS:(_q4 + 1) * TS],
                                     _sr[:, :], AF.Exp, scale=SCALE,
                                     accum_out=_z[:, _a * 4 + _q4:_a * 4 + _q4 + 1]
                                     ).then_inc(S[ACTs], 1)
                    sch.emit("scalar", fexp)
                    state["exp_tok"][(i, a, q4)] = sch.bump(ACTs, 1)
                    state["sreg_last"][ridx] = state["exp_tok"][(i, a, q4)]

            def av_row(i, a):
                if a == 0:
                    sch.wait("tensor", DVE, max(state["avfree"], ln_done))
                sch.wait("tensor", ACTs, state["exp_tok"][(i, a, 3)])
                sch.wait("tensor", DVE, state["qT_tok"][i])
                for sg in range(4):
                    bank, lo = av_of_strip[sg]
                    tpos = (0, 0) if lo == 0 else (0, 64)
                    def fav(t, S, _a=a, _sg=sg, _bk=bank, _lo=lo, _tp=tpos):
                        ins = t.matmul(_bk[_lo:_lo + 64, :],
                                       qT_ap(_a),
                                       prow_ap(_a)[:, _sg * TS:(_sg + 1) * TS],
                                       start=(_a == 0), stop=(_a == 15),
                                       tile_position=_tp)
                        if _sg == 3:
                            ins.then_inc(S[MM], 1)
                    sch.emit("tensor", fav)
                state["avtok"][(i, a)] = sch.bump(MM, 1)

            def make_tail_stages(i):
                act_base = sch.val(ACTs)
                zsrc = zc4a if i % 2 == 0 else zc4b
                rzbev = [0]
                avev = [0]
                uev = [0, 0]

                def t_rz():
                    sch.wait("vector", ACTs, state["exp_tok"][(i, 15, 3)])
                    sch.wait("vector", DVE, sch.val(DVE))
                    base_z = sch.val(DVE)
                    def fz(v, S, _z=zsrc, _b=base_z):
                        v.tensor_reduce(
                            zcols[:, :].rearrange("p (a o) -> p a o", o=1),
                            _z[:, :].rearrange("p (a q) -> p a q", q=4),
                            mybir.AxisListType.X, ALU.add).then_inc(S[DVE], 1)
                        v.wait_ge(S[DVE], _b + 1)
                        v.reciprocal(zcols[:, :], zcols[:, :]).then_inc(S[DVE], 1)
                        v.wait_ge(S[DVE], _b + 2)
                        v.tensor_copy(rzbf[:, :], zcols[:, :]).then_inc(S[DVE], 1)
                    sch.emit("vector", fz)
                    state["zsum_tok"] = sch.bump(DVE, 3)
                    sch.wait("tensor", DVE, max(state["zsum_tok"], state["last_qT"]))
                    def frzt(t, S):
                        t.matmul(ps_t[0:16, 0:128], rzbf[:, :], ident[:, :],
                                 start=True, stop=True,
                                 is_transpose=True).then_inc(S[MM], 1)
                    sch.emit("tensor", frzt)
                    grzt = sch.bump(MM, 1)
                    sch.wait("scalar", MM, grzt)
                    def frzte(s, S):
                        s.copy(trz[:, :], ps_t[0:16, 0:128]).then_inc(S[ACTs], 1)
                    sch.emit("scalar", frzte)
                    state["trz_evac"] = sch.bump(ACTs, 1)

                def mk_strip(sg):
                    def t_strip():
                        bk, lo = av_of_strip[sg]
                        tpos = (0, 0) if lo == 0 else (0, 64)
                        sch.wait("tensor", ACTs, max(state["trz_evac"], act_base))
                        sch.wait("tensor", DVE, rzbev[0])
                        for j in range(4):
                            cj = c_of_pos(sg * 4 + j)
                            def frzb(t, S, _cj=cj, _j=j, _lo=lo, _tp=tpos):
                                ins = t.matmul(
                                    ps_rz[_lo:_lo + 64, _j * 128:(_j + 1) * 128],
                                    sel16[:, _cj * 64:(_cj + 1) * 64],
                                    trz[:, :],
                                    start=(_j == 0), stop=(_j == 3),
                                    tile_position=_tp)
                                if _j == 3:
                                    ins.then_inc(S[MM], 1)
                            sch.emit("tensor", frzb)
                        grzb = sch.bump(MM, 1)
                        sch.wait("vector", MM, grzb)
                        sch.wait("vector", DVE, avev[0])
                        def frzbe(v, S, _lo=lo):
                            v.tensor_copy(rzb_sb[_lo:_lo + 64, :],
                                          ps_rz[_lo:_lo + 64, :]).then_inc(S[DVE], 1)
                        sch.emit("vector", frzbe)
                        rzbev[0] = sch.bump(DVE, 1)
                        sch.wait("vector", MM, state["avtok"][(i, 15)])
                        sch.wait("vector", DVE, rzbev[0])
                        def fave(v, S, _sg=sg, _lo=lo, _bk=bk):
                            for j in range(4):
                                c = c_of_pos(_sg * 4 + j)
                                ins = v.tensor_mul(
                                    ofm_ap(c // 2, _lo, _lo + 64),
                                    _bk[_lo:_lo + 64, j * 128:(j + 1) * 128],
                                    rzb_sb[_lo:_lo + 64, j * 128:(j + 1) * 128])
                                if j == 3:
                                    ins.then_inc(S[DVE], 1)
                        sch.emit("vector", fave)
                        avev[0] = sch.bump(DVE, 1)
                        if sg == 3:
                            state["avfree"] = avev[0]
                    return t_strip

                def mk_u(pair):
                    def t_u():
                        if pair == 0:
                            sch.wait("tensor", *wready["wu"])
                        for dc in (pair * 2, pair * 2 + 1):
                            ups = stg_u[dc % 2]
                            sch.wait("tensor", DVE, max(avev[0], uev[dc % 2]))
                            sch.wait("tensor", ACTs, act_base)
                            for ec in range(NKC):
                                def fu(t, S, _ec=ec, _dc=dc, _u=ups):
                                    ins = t.matmul(_u[:, 0:128], wutblk(_ec, _dc),
                                                   ar_h[:, 3072 + 128 * _ec:
                                                        3072 + 128 * _ec + 128],
                                                   start=(_ec == 0),
                                                   stop=(_ec == NKC - 1))
                                    if _ec == NKC - 1:
                                        ins.then_inc(S[MM], 1)
                                sch.emit("tensor", fu)
                            gu = sch.bump(MM, 1)
                            sch.wait("vector", MM, gu)
                            o0 = dc * NT + i * 128
                            def fux(v, S, _dc=dc, _o0=o0, _u=ups):
                                xsl = x_sb[:, _o0:_o0 + 128]
                                v.scalar_tensor_tensor(
                                    out=xsl, in0=_u[:, 0:128],
                                    scalar=bu_sb[:, _dc:_dc + 1], in1=xsl,
                                    op0=ALU.add, op1=ALU.add).then_inc(S[DVE], 1)
                            sch.emit("vector", fux)
                            uev[dc % 2] = sch.bump(DVE, 1)
                        if pair == 3:
                            state["ufree"] = max(uev)
                            state["avfree"] = max(state["avfree"], max(uev))
                            state["u_last"] = max(uev)
                    return t_u

                return [t_rz] + [mk_strip(sg) for sg in range(4)] + \
                    [mk_u(p) for p in range(4)]

            def qproj(half):
                t0h = half * 1024
                qev = [0, 0]
                for dc in range(NKC):
                    for st2 in range(2):
                        t0 = t0h + st2 * TS
                        pst = stg_u[(dc * 2 + st2) % 2]
                        sch.wait("tensor", *wready["wq"])
                        sch.wait("tensor", DVE, max(ln_tok[half * 2 + st2],
                                                    state["avfree"],
                                                    state["ufree"]))
                        sch.wait("tensor", DVE, qev[(dc * 2 + st2) % 2])
                        for kc in range(NKC):
                            def fq(t, S, _kc=kc, _dc=dc, _t0=t0, _pst=pst):
                                ins = t.matmul(_pst[:, :], wqtblk(_kc, _dc),
                                               ln_ap(_kc, _t0, TS),
                                               start=(_kc == 0), stop=(_kc == NKC - 1))
                                if _kc == NKC - 1:
                                    ins.then_inc(S[MM], 1)
                            sch.emit("tensor", fq)
                        gq = sch.bump(MM, 1)
                        sch.wait("vector", MM, gq)
                        def fqe(v, S, _dc=dc, _toff=t0 - t0h, _pst=pst):
                            v.tensor_scalar_add(q_ap(_dc, _toff, TS), _pst[:, :],
                                                bq_sb[:, _dc:_dc + 1]).then_inc(S[DVE], 1)
                        sch.emit("vector", fqe)
                        qev[(dc * 2 + st2) % 2] = sch.bump(DVE, 1)
                return max(qev)

            q_ready = qproj(0)
            if "after_qproj0" in hooks:
                hooks["after_qproj0"](sch.val(MM))
            staging(0, q_ready)
            transposes(0)
            for i in range(16):
                if i == 8:
                    for a in range(3, 16):
                        av_row(7, a)
                    for st_fn in make_tail_stages(7):
                        st_fn()
                    q_ready = qproj(1)
                    if "after_qproj1" in hooks:
                        hooks["after_qproj1"](sch.val(MM))
                    staging(8, q_ready)
                    transposes(8)
                for a in range(3):
                    sc_row(i, a)
                if i % 8 != 0:
                    for a in range(3, 16):
                        av_row(i - 1, a)
                    transposes(i)
                stages = make_tail_stages(i - 1) if i % 8 != 0 else []
                for a in range(3, 16):
                    sc_row(i, a)
                    si = a - 3
                    if 0 <= si < len(stages):
                        stages[si]()
                for a in range(3):
                    av_row(i, a)
                if i % 8 != 7:
                    staging(i + 1, q_ready)
            for a in range(3, 16):
                av_row(15, a)
            for st_fn in make_tail_stages(15):
                st_fn()
            return state["u_last"], sch.val(MM)

        do1, doa, do2 = ("ffn1" in phases), ("attn" in phases), ("ffn2" in phases)
        seg_psums = []

        wst1 = {"w1": [None] * NSUP, "w2": [None] * NSUP,
                "sup_last_mm": [0] * NSUP, "issue_next": None}
        wst2 = {"w1": [None] * NSUP, "w2": [None] * NSUP,
                "sup_last_mm": [0] * NSUP, "issue_next": None}

        xw = 0
        ffn1_mm = 0
        if do1:
            with nc.psum_tensor("ps_d1", [128, TS], F32) as ps_d1, \
                 nc.psum_tensor("ps_w1", [1, TS], F32) as ps_w1, \
                 nc.psum_tensor("ps_v1", [1, TS], F32) as ps_v1, \
                 nc.psum_tensor("ps_e1", [128, TS], F32) as ps_e1, \
                 nc.psum_tensor("ps_f1", [128, TS], F32) as ps_f1:
                for s0 in range(2):
                    wst1["w1"][s0] = issue_w1(s0, s0 % 2, 0, "f1")
                    wst1["w2"][s0] = issue_w2(s0, s0 % 2, 0)

                def issue_next_ffn1(sup):
                    nxt = sup + 2
                    tok = wst1["sup_last_mm"][sup]
                    if nxt < NSUP:
                        wst1["w1"][nxt] = issue_w1(nxt, nxt % 2, tok, "f1")
                        wst1["w2"][nxt] = issue_w2(nxt, nxt % 2, tok)
                wst1["issue_next"] = issue_next_ffn1
                xw, _, ffn1_mm = emit_ffn(0, wst1, ps_d1, ps_w1, ps_v1, ps_e1, ps_f1)
                if doa:
                    wready["wq"] = issue_wqut("q", wst1["sup_last_mm"][10])
            sch.seg_break()

        attn_mm = ffn1_mm
        if doa:
            if "wq" not in wready:
                wready["wq"] = issue_wqut("q", ffn1_mm)
            hooks = {}
            def after_qproj0(mmtok):
                wu_wait = max(mmtok, wst1["sup_last_mm"][11] if do1 else 0)
                wready["wu"] = issue_wqut("u", wu_wait)
            def after_qproj1(mmtok):
                if do2:
                    wst2["w1"][0] = issue_w1(0, 0, mmtok, "f2")
            hooks["after_qproj0"] = after_qproj0
            hooks["after_qproj1"] = after_qproj1
            with nc.psum_tensor("s0", [128, TS], F32) as s0, \
                 nc.psum_tensor("s1", [128, TS], F32) as s1, \
                 nc.psum_tensor("d_att", [128, TS], F32) as d_att, \
                 nc.psum_tensor("ps_t", [128, 1024], BF16) as ps_t, \
                 nc.psum_tensor("ps_rz", [128, TS], F32) as ps_rz:
                xw, attn_mm = emit_attn(xw, s0, s1, d_att, ps_t, ps_rz, hooks)
            sch.seg_break()

        xtok = [xw] * NKC
        if do2:
            with nc.psum_tensor("ps_d2", [128, TS], F32) as ps_d2, \
                 nc.psum_tensor("ps_w2", [1, TS], F32) as ps_w2, \
                 nc.psum_tensor("ps_v2", [1, TS], F32) as ps_v2, \
                 nc.psum_tensor("ps_e2", [128, TS], F32) as ps_e2, \
                 nc.psum_tensor("ps_f2", [128, TS], F32) as ps_f2:
                if wst2["w1"][0] is None:
                    wst2["w1"][0] = issue_w1(0, 0, attn_mm, "f2")
                wst2["w2"][0] = issue_w2(0, 0, attn_mm)
                wst2["w1"][1] = issue_w1(1, 1, attn_mm, "f2")
                wst2["w2"][1] = issue_w2(1, 1, attn_mm)

                def issue_next_ffn2(sup):
                    nxt = sup + 2
                    tok = wst2["sup_last_mm"][sup]
                    if nxt < NSUP:
                        wst2["w1"][nxt] = issue_w1(nxt, nxt % 2, tok, "f2")
                        wst2["w2"][nxt] = issue_w2(nxt, nxt % 2, tok)
                wst2["issue_next"] = issue_next_ffn2
                _, xtok, _ = emit_ffn(xw, wst2, ps_d2, ps_w2, ps_v2, ps_e2, ps_f2)

        for dc in range(NKC):
            sch.wait("sync", DVE, xtok[dc])
            def fy(sy, S, _dc=dc):
                sy.dma_start(out=y_h[_dc * 128:(_dc + 1) * 128, :],
                             in_=x_sb[:, _dc * NT:(_dc + 1) * NT]).then_inc(S[OUT], 16)
            sch.emit("sync", fy)
            sch.bump(OUT, 16)
        sch.wait("sync", OUT, sch.val(OUT))

        segments = sch.finish()
        for seg in segments:
            if not any(seg[e] for e in ENGINES):
                continue
            with nc.Block() as block:
                @block.gpsimd
                def _(g, _seg=seg):
                    for f in _seg["gpsimd"]:
                        f(g, sems)

                @block.tensor
                def _(t, _seg=seg):
                    for f in _seg["tensor"]:
                        f(t, sems)

                @block.vector
                def _(v, _seg=seg):
                    for f in _seg["vector"]:
                        f(v, sems)

                @block.scalar
                def _(s, _seg=seg):
                    for f in _seg["scalar"]:
                        f(s, sems)

                @block.sync
                def _(sy, _seg=seg):
                    for f in _seg["sync"]:
                        f(sy, sems)

    return nc


_NC_CACHE = {}


def _get_nc(phases):
    key = tuple(phases)
    if key not in _NC_CACHE:
        _NC_CACHE[key] = build(phases)
    return _NC_CACHE[key]


def make_inputs(L0_attn_x, L0_attn_wq, L0_attn_bq, L0_attn_wu, L0_attn_bu,
                L0_ffn_gate_w, L0_ffn_w1, L0_ffn_b1, L0_ffn_w2, L0_ffn_b2,
                L0_ffn_sw1, L0_ffn_sb1, L0_ffn_sw2, L0_ffn_sb2):
    f = lambda a: np.asarray(a, dtype=np.float32)
    x = f(L0_attn_x)
    w1 = f(L0_ffn_w1)
    w2 = f(L0_ffn_w2)
    w1cat = np.ascontiguousarray(np.concatenate([f(L0_ffn_sw1), w1[0], w1[1]], axis=1))
    w2cat = np.ascontiguousarray(np.concatenate([f(L0_ffn_sw2), w2[0], w2[1]], axis=0))
    wqt_ = np.ascontiguousarray(f(L0_attn_wq).T)
    wut_ = np.ascontiguousarray(f(L0_attn_wu).T)
    b1cat = np.concatenate([f(L0_ffn_sb1), f(L0_ffn_b1)[0], f(L0_ffn_b1)[1]])
    b1h_ = np.ascontiguousarray(b1cat.reshape(96, 128).T)
    b2 = f(L0_ffn_b2)
    b2x_ = np.ascontiguousarray(np.concatenate(
        [b2[0].reshape(8, 128).T, b2[1].reshape(8, 128).T,
         f(L0_ffn_sb2).reshape(8, 128).T], axis=1))
    bqc_ = np.ascontiguousarray(f(L0_attn_bq).reshape(8, 128).T)
    buc_ = np.ascontiguousarray(f(L0_attn_bu).reshape(8, 128).T)
    gw = f(L0_ffn_gate_w)
    gwc_ = np.ascontiguousarray(np.concatenate(
        [gw[0].reshape(8, 128).T, gw[1].reshape(8, 128).T], axis=1))
    shared = dict(w1c=w1cat, w2c=w2cat, wqt=wqt_, wut=wut_, b1h=b1h_, b2x=b2x_,
                  bqc=bqc_, buc=buc_, gwc=gwc_)
    in_maps = []
    for i in range(8):
        m = dict(shared)
        m["x"] = np.ascontiguousarray(x[i].T)
        in_maps.append(m)
    return in_maps


def kernel(L0_attn_x, L0_attn_wq, L0_attn_bq, L0_attn_wu, L0_attn_bu,
           L0_ffn_gate_w, L0_ffn_w1, L0_ffn_b1, L0_ffn_w2, L0_ffn_b2,
           L0_ffn_sw1, L0_ffn_sb1, L0_ffn_sw2, L0_ffn_sb2,
           _phases=("ffn1", "attn", "ffn2")):
    in_maps = make_inputs(L0_attn_x, L0_attn_wq, L0_attn_bq, L0_attn_wu,
                          L0_attn_bu, L0_ffn_gate_w, L0_ffn_w1, L0_ffn_b1,
                          L0_ffn_w2, L0_ffn_b2, L0_ffn_sw1, L0_ffn_sb1,
                          L0_ffn_sw2, L0_ffn_sb2)
    nc = _get_nc(_phases)
    res = run_bass_kernel_spmd(nc, in_maps, core_ids=list(range(8)))
    out = np.stack([np.ascontiguousarray(r["y"].T) for r in res.results], axis=0)
    return out.astype(np.float32)


# version 22
# speedup vs baseline: 1.1195x; 1.0573x over previous
import contextlib
import numpy as np

import concourse.bass as bass
import concourse.mybir as mybir
from concourse.bass_utils import run_bass_kernel_spmd

F32 = mybir.dt.float32
BF16 = mybir.dt.bfloat16
AF = mybir.ActivationFunctionType
ALU = mybir.AluOpType

NT = 2048
D = 1024
NKC = 8
TS = 512
NSTRIP = NT // TS
HE = 1024
NSUP = 12
EPS = 1e-5
SCALE = 1.0 / 32.0

ENGINES = ("gpsimd", "tensor", "vector", "scalar", "sync")


def pos_of_c(c):
    return (c // 2) if c % 2 == 0 else 8 + (c // 2)


def c_of_pos(p):
    return 2 * p if p < 8 else 2 * (p - 8) + 1


class Sched:

    def __init__(self):
        self.segs = []
        self.ops = {e: [] for e in ENGINES}
        self.cnt = {}
        self.lastwait = {e: {} for e in ENGINES}

    def seg_break(self):
        self.segs.append(self.ops)
        self.ops = {e: [] for e in ENGINES}

    def finish(self):
        self.seg_break()
        return self.segs

    def emit(self, eng, fn):
        self.ops[eng].append(fn)

    def wait(self, eng, sem, val):
        if not val or val <= 0:
            return
        if self.lastwait[eng].get(sem, -1) >= val:
            return
        self.lastwait[eng][sem] = val
        self.emit(eng, lambda e, S, _s=sem, _v=val: e.wait_ge(S[_s], _v))

    def bump(self, sem, by):
        self.cnt[sem] = self.cnt.get(sem, 0) + by
        return self.cnt[sem]

    def val(self, sem):
        return self.cnt.get(sem, 0)


def build(phases=("ffn1", "attn", "ffn2")):
    nc = bass.Bass("TRN2", target_bir_lowering=False)

    x_h = nc.dram_tensor("x", [D, NT], F32, kind="ExternalInput").ap()
    w1c = nc.dram_tensor("w1c", [D, 3 * 4096], F32, kind="ExternalInput").ap()
    w2c = nc.dram_tensor("w2c", [3 * 4096, D], F32, kind="ExternalInput").ap()
    wqt = nc.dram_tensor("wqt", [D, D], F32, kind="ExternalInput").ap()
    wut = nc.dram_tensor("wut", [D, D], F32, kind="ExternalInput").ap()
    b1h = nc.dram_tensor("b1h", [128, 96], F32, kind="ExternalInput").ap()
    b2x = nc.dram_tensor("b2x", [128, 24], F32, kind="ExternalInput").ap()
    bqc = nc.dram_tensor("bqc", [128, 8], F32, kind="ExternalInput").ap()
    buc = nc.dram_tensor("buc", [128, 8], F32, kind="ExternalInput").ap()
    gwc = nc.dram_tensor("gwc", [128, 16], F32, kind="ExternalInput").ap()
    y_h = nc.dram_tensor("y", [D, NT], F32, kind="ExternalOutput").ap()

    es = contextlib.ExitStack()
    with es:
        sb = lambda n, s, d: es.enter_context(nc.sbuf_tensor(n, s, d))
        mksem = lambda n: es.enter_context(nc.semaphore(n))

        x_sb = sb("x_sb", [128, NKC * NT], F32)
        ar_ln = sb("ar_ln", [128, 16384], BF16)
        ar_w1 = sb("ar_w1", [128, 16384], BF16)
        ar_w2 = sb("ar_w2", [128, 16384], BF16)
        ar_h = sb("ar_h", [128, 4096], BF16)
        ar_b = sb("ar_b", [128, 8192], BF16)
        p_ex = sb("p_ex", [128, 8192], BF16)
        rzb_sb = sb("rzb_sb", [128, TS], BF16)
        xb_s = rzb_sb
        m0row = sb("m0row", [1, TS], BF16)
        m1row = sb("m1row", [1, TS], BF16)
        ones_col = sb("ones_col", [128, 1], BF16)
        ones_colf = sb("ones_colf", [128, 1], F32)
        onesLf = sb("onesLf", [1, 128], F32)
        onesLb = sb("onesLb", [1, 128], BF16)
        ident = sb("ident", [128, 128], BF16)
        e2mat = sb("e2mat", [128, 128], BF16)
        sel16 = sb("sel16", [16, 1024], BF16)
        b1hf = sb("b1hf", [128, 96], F32)
        b2sh = sb("b2sh", [128, 8], F32)
        bq_sb = sb("bq_sb", [128, 8], F32)
        bu_sb = sb("bu_sb", [128, 8], F32)
        gw_bf = sb("gw_bf", [128, 16], BF16)
        zcols = sb("zcols", [128, 16], F32)
        zc4a = sb("zc4a", [128, 64], F32)
        zc4b = sb("zc4b", [128, 64], F32)
        rzbf = sb("rzbf", [128, 16], BF16)
        trz = sb("trz", [16, 128], BF16)

        ps_a = es.enter_context(nc.psum_tensor("ps_a", [128, TS], F32))
        ps_b = es.enter_context(nc.psum_tensor("ps_b", [128, TS], F32))
        ps_c = es.enter_context(nc.psum_tensor("ps_c", [128, TS], F32))

        sems = {}
        for s in ("w1a", "w1b", "w2a", "w2b", "wx", "x", "mm", "act", "dve", "gp", "out"):
            sems[s] = mksem("s_" + s)

        sch = Sched()
        X, MM, ACTs, DVE, GP, OUT = "x", "mm", "act", "dve", "gp", "out"

        scaleB = ar_b[:, 6144:7168].bitcast(F32)
        meanB = ar_b[:, 7168:8192].bitcast(F32)
        r1 = meanB[0:1, :]
        r2 = scaleB[0:1, :]
        r3 = m0row

        def lnm_ap(kc):
            return ar_b[:, kc * TS:(kc + 1) * TS]

        def mask_ap(which, strip):
            off = 8 * TS + which * NT
            return ar_b[:, off + strip * TS: off + (strip + 1) * TS]

        def q_ap(kc, t0, w):
            return ar_b[:, kc * 1024 + t0: kc * 1024 + t0 + w]

        def ln_ap(kc, t0, w):
            half = t0 // 1024
            tt = t0 - half * 1024
            base = half * 8192 + kc * 1024 + tt
            return ar_ln[:, base: base + w]

        def prow_ap(a):
            if a < 8:
                return ar_w2[:, a * NT:(a + 1) * NT]
            if a < 12:
                return ar_ln[:, (a - 8) * NT:(a - 7) * NT]
            return p_ex[:, (a - 12) * NT:(a - 11) * NT]

        def w1blk(slot, kc, hc):
            o = slot * 8192 + kc * 1024 + hc * 128
            return ar_w1[:, o:o + 128]

        def w2blk(slot, hc, dc):
            o = slot * 8192 + hc * 1024 + dc * 128
            return ar_w2[:, o:o + 128]

        def wqtblk(kc, dc):
            o = kc * 1024 + dc * 128
            return ar_w1[:, o:o + 128]

        def wutblk(kc, dc):
            o = 8192 + kc * 1024 + dc * 128
            return ar_w1[:, o:o + 128]

        def qq_ap(posc, lo64):
            return ar_h[lo64:lo64 + 64, 128 * posc: 128 * posc + 128]

        def qT_ap(cp, w=64):
            return ar_h[:, 2048 + 64 * cp: 2048 + 64 * cp + w]

        def ofm_ap(ec, lo, hi):
            return ar_h[lo:hi, 3072 + 128 * ec: 3072 + 128 * ec + 128]

        def g_init(g, S):
            n = 0
            g.memset(ones_col[:, :], 1.0)
            g.memset(ones_colf[:, :], 1.0)
            g.memset(onesLf[:, :], 1.0)
            g.memset(onesLb[:, :], 1.0)
            g.memset(ident[:, :], 0.0).then_inc(S[GP], 1); n += 1
            g.wait_ge(S[GP], n)
            g.affine_select(out=ident[:, :], in_=ident[:, :], compare_op=ALU.not_equal,
                            fill=1.0, base=0, pattern=[[-1, 128]],
                            channel_multiplier=1).then_inc(S[GP], 1); n += 1
            g.memset(e2mat[:, :], 0.0).then_inc(S[GP], 1); n += 1
            for base in (0, 64, -64):
                g.wait_ge(S[GP], n)
                g.affine_select(out=e2mat[:, :], in_=e2mat[:, :], compare_op=ALU.not_equal,
                                fill=1.0, base=base, pattern=[[-1, 128]],
                                channel_multiplier=1).then_inc(S[GP], 1); n += 1
            g.memset(sel16[:, :], 0.0).then_inc(S[GP], 1); n += 1
            g.wait_ge(S[GP], n)
            g.affine_select(
                out=sel16[:, :].rearrange("p (c d) -> p c d", c=16),
                in_=sel16[:, :].rearrange("p (c d) -> p c d", c=16),
                compare_op=ALU.not_equal, fill=1.0, base=0,
                pattern=[[-1, 16], [0, 64]], channel_multiplier=1,
            ).then_inc(S[GP], 1); n += 1
        sch.emit("gpsimd", g_init)
        tok_gpinit = sch.bump(GP, 8)

        def sy_load(sy, S):
            sy.dma_start(
                out=x_sb[:, :].rearrange("p (kc t) -> p kc t", kc=NKC),
                in_=x_h.rearrange("(kc p) t -> p kc t", p=128),
            ).then_inc(S[X], 16)
            for src, dst in ((b1h, b1hf), (bqc, bq_sb), (buc, bu_sb)):
                sy.dma_start(out=dst[:, :], in_=src[:, :]).then_inc(S[X], 16)
            sy.dma_start(out=b2sh[:, :], in_=b2x[:, 16:24]).then_inc(S[X], 16)
            sy.dma_start(out=meanB[:, 0:16], in_=b2x[:, 0:16]).then_inc(S[X], 16)
        sch.emit("sync", sy_load)
        tok_xload = sch.bump(X, 16 * 6)

        def g_gw(g, S):
            g.dma_start(out=gw_bf[:, :], in_=gwc[:, :]).then_inc(S["wx"], 16)
        sch.emit("gpsimd", g_gw)
        tok_gw = ("wx", sch.bump("wx", 16))

        sch.wait("scalar", X, tok_xload)
        def sc_bias(s, S):
            s.mul(b1hf[:, :], b1hf[:, :], 0.5).then_inc(S[ACTs], 1)
        sch.emit("scalar", sc_bias)
        tok_b1 = sch.bump(ACTs, 1)

        sch.wait("vector", X, tok_xload)
        base_dve = sch.val(DVE)
        def dv_bias(v, S, _b=base_dve):
            v.tensor_add(b2sh[:, :], b2sh[:, :], meanB[:, 0:8]).then_inc(S[DVE], 1)
            v.wait_ge(S[DVE], _b + 1)
            v.tensor_add(b2sh[:, :], b2sh[:, :], meanB[:, 8:16]).then_inc(S[DVE], 1)
            v.wait_ge(S[DVE], _b + 2)
            v.tensor_scalar_mul(b2sh[:, :], b2sh[:, :], 0.5).then_inc(S[DVE], 1)
        sch.emit("vector", dv_bias)
        tok_b2 = sch.bump(DVE, 3)

        wready = {}

        def issue_w1(sup, slot, wait_mm, arena_name):
            sch.wait("gpsimd", MM, wait_mm)
            wsem = "w1a" if slot == 0 else "w1b"
            def f(g, S, _sup=sup, _slot=slot, _ws=wsem):
                g.dma_start(
                    out=ar_w1[:, _slot * 8192:(_slot + 1) * 8192].rearrange(
                        "p (kc h) -> p kc h", kc=NKC),
                    in_=w1c[:, _sup * HE:(_sup + 1) * HE].rearrange(
                        "(kc p) h -> p kc h", p=128),
                ).then_inc(S[_ws], 16)
            sch.emit("gpsimd", f)
            return (wsem, sch.bump(wsem, 16))

        def issue_w2(sup, slot, wait_mm):
            sch.wait("gpsimd", MM, wait_mm)
            wsem = "w2a" if slot == 0 else "w2b"
            def f(g, S, _sup=sup, _slot=slot, _ws=wsem):
                g.dma_start(
                    out=ar_w2[:, _slot * 8192:(_slot + 1) * 8192].rearrange(
                        "p (hc d) -> p hc d", hc=NKC),
                    in_=w2c[_sup * HE:(_sup + 1) * HE, :].rearrange(
                        "(hc p) d -> p hc d", p=128),
                ).then_inc(S[_ws], 16)
            sch.emit("gpsimd", f)
            return (wsem, sch.bump(wsem, 16))

        def issue_wqut(which, wait_mm):
            sch.wait("gpsimd", MM, wait_mm)
            src = wqt if which == "q" else wut
            off = 0 if which == "q" else 8192
            def f(g, S, _src=src, _off=off):
                g.dma_start(
                    out=ar_w1[:, _off:_off + 8192].rearrange("p (kc d) -> p kc d", kc=NKC),
                    in_=_src.rearrange("(kc p) d -> p kc d", p=128),
                ).then_inc(S["wx"], 16)
            sch.emit("gpsimd", f)
            return ("wx", sch.bump("wx", 16))

        pexf = p_ex[:, 0:4096].bitcast(F32)
        mean2 = pexf[:, 1024:1536]
        scale2 = pexf[:, 1536:2048]
        xb2 = p_ex[:, 4096:4608]
        gate_r1 = p_ex[:, 4608:5632].bitcast(F32)[0:1, :]
        gate_r2 = p_ex[:, 5632:6656].bitcast(F32)[0:1, :]

        def make_ln_stages(is_ffn, xwaits, statps, bcps, ext=None):
            ext = ext if ext is not None else {}
            ln_tok = [0] * NSTRIP
            mask_tok = [0] * NSTRIP
            s1ev = [0] * NSTRIP
            s2ev = [0] * NSTRIP
            statmm = [0] * NSTRIP
            sqtok = [0] * NSTRIP
            bcev = [0] * NSTRIP
            bfree = [(DVE, 0), (DVE, 0)]
            sq_mm_last = [0, 0]
            ln_ktok = [[0] * NKC for _ in range(NSTRIP)]
            ext["ln_ktok"] = ln_ktok

            def tiles(st):
                if st % 2 == 0:
                    return meanB, scaleB
                return mean2, scale2

            def scratch(st):
                return xb_s if st % 2 == 0 else xb2

            def stage_stats(st):
                t0 = st * TS
                sp1 = statps[0]
                sp2 = statps[1]
                mB, sB = tiles(st)
                r1, r2 = mB[0:1, :], sB[0:1, :]
                sch.wait("tensor", GP, tok_gpinit)
                sch.wait("tensor", X, tok_xload)
                sch.wait("tensor", DVE, max(xwaits[st], tok_b2,
                                            s1ev[st - 1] if st >= 1 else 0))
                for kc in range(NKC):
                    def fmm(t, S, _kc=kc, _t0=t0, _sp=sp1):
                        t.matmul(_sp, ones_colf[:, :],
                                 x_sb[:, _kc * NT + _t0: _kc * NT + _t0 + TS],
                                 start=(_kc == 0), stop=(_kc == NKC - 1)).then_inc(S[MM], 1)
                    sch.emit("tensor", fmm)
                    prev_mm = sch.bump(MM, 1)
                s1mm = prev_mm
                sch.wait("vector", MM, s1mm)
                sch.wait("vector", DVE, ln_tok[st - 2] if st >= 2 else 0)
                def fs1(v, S, _sp=sp1, _r=r1):
                    v.tensor_scalar_mul(_r, _sp, 1.0 / D).then_inc(S[DVE], 1)
                sch.emit("vector", fs1)
                s1ev[st] = sch.bump(DVE, 1)
                sch.wait("scalar", DVE, max(xwaits[st], tok_b2))
                sch.wait("scalar", X, tok_xload)
                for kc in range(NKC):
                    xb = xb_s if kc % 2 == 0 else xb2
                    sch.wait("scalar", MM, sq_mm_last[kc % 2])
                    def fsq(s, S, _kc=kc, _t0=t0, _xb=xb):
                        s.activation(_xb[:, :],
                                     x_sb[:, _kc * NT + _t0: _kc * NT + _t0 + TS],
                                     AF.Square).then_inc(S[ACTs], 1)
                    sch.emit("scalar", fsq)
                    sqtok[st] = sch.bump(ACTs, 1)
                    sch.wait("tensor", ACTs, sqtok[st])
                    if kc == 0:
                        sch.wait("tensor", DVE, s2ev[st - 1] if st >= 1 else 0)
                    def fmm2(t, S, _kc=kc, _sp=sp2, _xb=xb):
                        t.matmul(_sp, ones_col[:, :], _xb[:, :],
                                 start=(_kc == 0), stop=(_kc == NKC - 1)).then_inc(S[MM], 1)
                    sch.emit("tensor", fmm2)
                    prev_mm = sch.bump(MM, 1)
                    sq_mm_last[kc % 2] = prev_mm
                statmm[st] = prev_mm
                sch.wait("vector", MM, prev_mm)
                def fs2(v, S, _sp=sp2, _r=r2):
                    v.tensor_scalar(out=_r, in0=_sp, scalar1=1.0 / D, scalar2=EPS,
                                    op0=ALU.mult, op1=ALU.add).then_inc(S[DVE], 1)
                sch.emit("vector", fs2)
                s2ev[st] = sch.bump(DVE, 1)

            def stage_post(st):
                t0 = st * TS
                mB, sB = tiles(st)
                r1, r2 = mB[0:1, :], sB[0:1, :]
                bP = bcps[st % 2]
                sch.wait("vector", DVE, s1ev[st])
                def frow2a(v, S, _r1=r1):
                    v.tensor_mul(r3[:, :], _r1, _r1).then_inc(S[DVE], 1)
                sch.emit("vector", frow2a)
                rowa = sch.bump(DVE, 1)
                sch.wait("vector", DVE, max(s2ev[st], rowa))
                def frow2b(v, S, _r2=r2):
                    v.scalar_tensor_tensor(out=_r2, in0=r3[:, :], scalar=-1.0,
                                           in1=_r2, op0=ALU.mult,
                                           op1=ALU.add).then_inc(S[DVE], 1)
                sch.emit("vector", frow2b)
                rowd = sch.bump(DVE, 1)
                sch.wait("scalar", DVE, rowd)
                def frln(s, S, _r2=r2):
                    s.activation(_r2, _r2, AF.Ln).then_inc(S[ACTs], 1)
                sch.emit("scalar", frln)
                rowl = sch.bump(ACTs, 1)
                sch.wait("tensor", ACTs, rowl)
                sch.wait("tensor", *bfree[st % 2])
                def fbc2(t, S, _r=r2, _bp=bP):
                    t.matmul(_bp, onesLf[:, :], _r, start=True, stop=True).then_inc(S[MM], 1)
                sch.emit("tensor", fbc2)
                bct2 = sch.bump(MM, 1)
                sch.wait("scalar", MM, bct2)
                sch.wait("scalar", DVE, ln_tok[st - 2] if st >= 2 else 0)
                def fsc(s, S, _sB=sB, _bp=bP):
                    s.activation(_sB[:, :], _bp, AF.Exp, scale=-0.5).then_inc(S[ACTs], 1)
                sch.emit("scalar", fsc)
                sctok = sch.bump(ACTs, 1)
                sch.wait("tensor", ACTs, sctok)
                def fbc1(t, S, _r=r1, _bp=bP):
                    t.matmul(_bp, onesLf[:, :], _r, start=True, stop=True).then_inc(S[MM], 1)
                sch.emit("tensor", fbc1)
                bct1 = sch.bump(MM, 1)
                sch.wait("vector", MM, bct1)
                sch.wait("vector", ACTs, sctok)
                for kc in range(NKC):
                    base_n = sch.val(DVE)
                    def fn(v, S, _kc=kc, _t0=t0, _b=base_n, _bp=bP, _sB=sB):
                        v.wait_ge(S[DVE], _b)
                        v.tensor_sub(ln_ap(_kc, _t0, TS),
                                     x_sb[:, _kc * NT + _t0: _kc * NT + _t0 + TS],
                                     _bp).then_inc(S[DVE], 1)
                        v.wait_ge(S[DVE], _b + 1)
                        v.tensor_mul(ln_ap(_kc, _t0, TS), ln_ap(_kc, _t0, TS),
                                     _sB[:, :]).then_inc(S[DVE], 1)
                    sch.emit("vector", fn)
                    ln_tok[st] = sch.bump(DVE, 2)
                    ln_ktok[st][kc] = ln_tok[st]
                bfree[st % 2] = (DVE, ln_tok[st])


            def stage_gate(st):
                t0 = st * TS
                r1, r2 = gate_r1, gate_r2
                bA = bcps[0]
                gevac = 0
                for e in range(2):
                    gp_ps = bcps[e]
                    for kc in range(NKC):
                        sch.wait("tensor", *tok_gw)
                        sch.wait("tensor", DVE, ln_tok[st])
                        sch.wait("tensor", *bfree[e])
                        def fg(t, S, _e=e, _kc=kc, _t0=t0, _sp=gp_ps):
                            t.matmul(_sp[0:1, :],
                                     gw_bf[:, _e * 8 + _kc:_e * 8 + _kc + 1],
                                     ln_ap(_kc, _t0, TS),
                                     start=(_kc == 0), stop=(_kc == NKC - 1)
                                     ).then_inc(S[MM], 1)
                        sch.emit("tensor", fg)
                        gtok = sch.bump(MM, 1)
                    sch.wait("vector", MM, gtok)
                    sch.wait("vector", DVE, ln_tok[st])
                    dstrow = r1 if e == 0 else r2
                    def fge(v, S, _d=dstrow, _sp=gp_ps):
                        v.tensor_copy(_d, _sp[0:1, :]).then_inc(S[DVE], 1)
                    sch.emit("vector", fge)
                    gevac = sch.bump(DVE, 1)
                    bfree[e] = (DVE, gevac)
                sch.wait("vector", DVE, gevac)
                base_mr = sch.val(DVE)
                def fmrow(v, S, _r1=r1, _r2=r2, _b=base_mr):
                    v.tensor_tensor(m0row[:, :], _r1, _r2,
                                    ALU.is_ge).then_inc(S[DVE], 1)
                    v.wait_ge(S[DVE], _b + 1)
                    v.tensor_tensor(m1row[:, :], _r1, _r2,
                                    ALU.is_lt).then_inc(S[DVE], 1)
                sch.emit("vector", fmrow)
                mrow_tok = sch.bump(DVE, 2)
                sch.wait("tensor", DVE, mrow_tok)
                sch.wait("tensor", *bfree[0])
                sch.wait("tensor", *bfree[1])
                bB2 = bcps[1]
                def fmb(t, S, _bp=bA, _bp2=bB2):
                    t.matmul(_bp, onesLb[:, :], m0row[:, :],
                             start=True, stop=True).then_inc(S[MM], 1)
                    t.matmul(_bp2, onesLb[:, :], m1row[:, :],
                             start=True, stop=True).then_inc(S[MM], 1)
                sch.emit("tensor", fmb)
                mbt = sch.bump(MM, 2)
                sch.wait("vector", MM, mbt)
                sch.wait("vector", DVE, ext.get("all_ln", 0))
                base_me = sch.val(DVE)
                def fmbe(v, S, _st=st, _bp=bA, _bp2=bB2, _b=base_me):
                    v.tensor_copy(mask_ap(0, _st), _bp).then_inc(S[DVE], 1)
                    v.wait_ge(S[DVE], _b + 1)
                    v.tensor_copy(mask_ap(1, _st), _bp2).then_inc(S[DVE], 1)
                sch.emit("vector", fmbe)
                mask_tok[st] = sch.bump(DVE, 2)
                bfree[0] = (DVE, mask_tok[st] - 1)
                bfree[1] = (DVE, mask_tok[st])

            stats_fns = [lambda _st=st: stage_stats(_st) for st in range(NSTRIP)]
            post_fns = [lambda _st=st: stage_post(_st) for st in range(NSTRIP)]
            gate_fns = [lambda _st=st: stage_gate(_st) for st in range(NSTRIP)]
            return stats_fns, post_fns, gate_fns, ln_tok, mask_tok

        def emit_ln(is_ffn, xwaits, statps, bcps):
            sf, pf, gf, ln_tok, mask_tok = make_ln_stages(is_ffn, xwaits, statps, bcps)
            for st in range(NSTRIP):
                sf[st]()
                if st > 0:
                    pf[st - 1]()
            pf[NSTRIP - 1]()
            if is_ffn:
                for st in range(NSTRIP):
                    gf[st]()
            return ln_tok, mask_tok

        def emit_ffn(xwait_dve, wst, ps_d, ps_w, ps_w2, ps_e, ps_f):
            ext = {}
            sf, pf, gf, ln_tok, mask_tok = make_ln_stages(
                True, [xwait_dve] * NSTRIP,
                (ps_w[:, :], ps_w2[:, :]), (ps_e[:, :], ps_f[:, :]), ext)
            ln_ktok = ext["ln_ktok"]
            h_relu_tok = [0] * NKC
            stt_tok = [[0] * NSTRIP for _ in range(NKC)]
            ps1_free = [0, 0]
            ps2_free = [0, 0]
            prev_mm1 = [0]
            lnm_tok = [None]
            lnm_ktok = [0] * NKC

            pending_lnm = []

            def build_lnm_pieces(st, klass, wait_mm):
                which = 0 if klass == 1 else 1
                def piece(kc):
                    sch.wait("vector", DVE, max(ln_tok[st], mask_tok[st]))
                    sch.wait("vector", MM, wait_mm)
                    def fm(v, S, _kc=kc, _st=st, _w=which):
                        v.tensor_mul(lnm_ap(_kc), ln_ap(_kc, _st * TS, TS),
                                     mask_ap(_w, _st)).then_inc(S[DVE], 1)
                    sch.emit("vector", fm)
                    lnm_tok[0] = sch.bump(DVE, 1)
                    lnm_ktok[kc] = lnm_tok[0]
                return [lambda _kc=kc: piece(_kc) for kc in range(NKC)]

            def build_lnm(st, klass, wait_mm):
                for p in build_lnm_pieces(st, klass, wait_mm):
                    p()
                return lnm_tok[0]

            def do_mm1(sup, st, klass, slot):
                if wst["w1"][sup]:
                    sch.wait("tensor", *wst["w1"][sup])
                if wst["w2"][sup]:
                    sch.wait("tensor", *wst["w2"][sup])
                if klass == 0:
                    rhs_sem, rhs_tok = DVE, ln_tok[st]
                    rhs = lambda kc, _t0=st * TS: ln_ap(kc, _t0, TS)
                    rhs_ktok = [ln_ktok[st][kc] for kc in range(NKC)]
                else:
                    rhs_sem, rhs_tok = DVE, lnm_tok[0]
                    rhs = lambda kc: lnm_ap(kc)
                    rhs_ktok = list(lnm_ktok)
                for hc in range(NKC):
                    pst = (ps_a, ps_b)[hc % 2]
                    sch.wait("tensor", ACTs, ps1_free[hc % 2])
                    if hc > 0:
                        sch.wait("tensor", rhs_sem, rhs_tok)
                    for kc in range(NKC):
                        if hc == 0:
                            sch.wait("tensor", rhs_sem, rhs_ktok[kc])
                        def f1(t, S, _kc=kc, _hc=hc, _slot=slot, _pst=pst, _rhs=rhs):
                            ins = t.matmul(_pst[:, :], w1blk(_slot, _kc, _hc),
                                           _rhs(_kc),
                                           start=(_kc == 0), stop=(_kc == NKC - 1))
                            if _kc == NKC - 1:
                                ins.then_inc(S[MM], 1)
                        sch.emit("tensor", f1)
                    g1 = sch.bump(MM, 1)
                    sch.wait("scalar", MM, g1)
                    sch.wait("scalar", ACTs, tok_b1)
                    def fr(s, S, _hc=hc, _sup=sup, _pst=pst):
                        s.activation(ar_h[:, _hc * TS:(_hc + 1) * TS], _pst[:, :],
                                     AF.Relu,
                                     bias=b1hf[:, _sup * 8 + _hc:_sup * 8 + _hc + 1],
                                     scale=0.5).then_inc(S[ACTs], 1)
                    sch.emit("scalar", fr)
                    h_relu_tok[hc] = sch.bump(ACTs, 1)
                    ps1_free[hc % 2] = h_relu_tok[hc]
                prev_mm1[0] = sch.val(MM)
                if klass > 0:
                    nst, nkl = (st + 1, klass) if st < NSTRIP - 1 else \
                        (0, (sup + 1) // 4 if sup + 1 < NSUP else None)
                    if nkl:
                        pending_lnm.extend(build_lnm_pieces(nst, nkl, prev_mm1[0]))
                    else:
                        lnm_tok[0] = None

            def do_mm2(sup, st, slot, mm2_extra):
                h_all = max(h_relu_tok)
                for dc in range(NKC):
                    pst = (ps_c, ps_d)[dc % 2]
                    sch.wait("tensor", DVE, max(ps2_free[dc % 2], mm2_extra))
                    if dc > 0:
                        sch.wait("tensor", ACTs, h_all)
                    for hc in range(NKC):
                        if dc == 0:
                            sch.wait("tensor", ACTs, h_relu_tok[hc])
                        def f2(t, S, _hc=hc, _dc=dc, _slot=slot, _pst=pst):
                            ins = t.matmul(_pst[:, :], w2blk(_slot, _hc, _dc),
                                           ar_h[:, _hc * TS:(_hc + 1) * TS],
                                           start=(_hc == 0), stop=(_hc == NKC - 1))
                            if _hc == NKC - 1:
                                ins.then_inc(S[MM], 1)
                        sch.emit("tensor", f2)
                    g2 = sch.bump(MM, 1)
                    sch.wait("vector", MM, g2)
                    sch.wait("vector", DVE, tok_b2)
                    o0 = dc * NT + st * TS
                    if sup == 0:
                        def fx(v, S, _dc=dc, _o0=o0, _pst=pst):
                            xsl = x_sb[:, _o0:_o0 + TS]
                            v.scalar_tensor_tensor(
                                out=xsl, in0=_pst[:, :],
                                scalar=b2sh[:, _dc:_dc + 1], in1=xsl,
                                op0=ALU.add, op1=ALU.add).then_inc(S[DVE], 1)
                    else:
                        def fx(v, S, _o0=o0, _pst=pst):
                            xsl = x_sb[:, _o0:_o0 + TS]
                            v.tensor_add(xsl, xsl, _pst[:, :]).then_inc(S[DVE], 1)
                    sch.emit("vector", fx)
                    stt_tok[dc][st] = sch.bump(DVE, 1)
                    ps2_free[dc % 2] = stt_tok[dc][st]
                    if pending_lnm:
                        pending_lnm.pop(0)()
                while pending_lnm:
                    pending_lnm.pop(0)()

            sf[0](); sf[1](); pf[0]()
            for st in range(NSTRIP):
                do_mm1(0, st, 0, 0)
                if st + 2 < NSTRIP:
                    sf[st + 2]()
                if st + 1 < NSTRIP:
                    pf[st + 1]()
                do_mm2(0, st, 0, sch.val(DVE))
                if st == 2:
                    ext["all_ln"] = max(ln_tok)
                if st >= 2:
                    gf[st - 2]()
            for st in (2, 3):
                gf[st]()
            wst["sup_last_mm"][0] = sch.val(MM)
            if wst["issue_next"] is not None:
                wst["issue_next"](0)
            for sup in range(1, NSUP):
                klass = sup // 4
                slot = sup % 2
                if klass > 0 and lnm_tok[0] is None:
                    lnm_tok[0] = build_lnm(0, klass, prev_mm1[0])
                for st in range(NSTRIP):
                    do_mm1(sup, st, klass, slot)
                    do_mm2(sup, st, slot, 0)
                wst["sup_last_mm"][sup] = sch.val(MM)
                if wst["issue_next"] is not None:
                    wst["issue_next"](sup)
            xdve = max(max(row) for row in stt_tok)
            xtok = [stt_tok[dc][NSTRIP - 1] for dc in range(NKC)]
            return xdve, xtok, sch.val(MM)

        def emit_attn(xwait_dve, s0, s1, d_att, ps_t, ps_rz, hooks):
            ln_tok, _ = emit_ln(False, [xwait_dve] * NSTRIP,
                                (s0[0:1, :], s1[0:1, :]),
                                (ps_c[:, :], d_att[:, :]))
            ln_done = sch.val(DVE)
            av_of_strip = [(ps_a, 0), (ps_b, 0), (ps_c, 64), (d_att, 64)]
            sregs3 = (s0, s1, ps_t[:, :].bitcast(F32))
            stg = (s0, s1)
            stg_u = (ps_a, ps_b)

            state = {
                "exp_tok": {},
                "avtok": {},
                "avfree": 0,
                "stage_tok": {},
                "stage_ev": 0,
                "ufree": 0,
                "qT_tok": {},
                "zsum_tok": 0,
                "last_qT": 0,
                "trz_evac": 0,
                "u_last": 0,
                "sreg_last": {},
            }

            def staging(i, q_ready):
                qoff = (i % 8) * 128
                act_prev = state["stage_ev"]
                for b4 in range(4):
                    ps = stg[b4 % 2]
                    sch.wait("tensor", ACTs, state["sreg_last"].get(b4 % 2, 0))
                    sch.wait("tensor", DVE, max(q_ready, act_prev))
                    for j in range(4):
                        c = c_of_pos(b4 * 4 + j)
                        hb = (c % 2) * 64
                        def fqq(t, S, _c=c, _hb=hb, _q=qoff, _j=j, _ps=ps):
                            ins = t.matmul(_ps[:, _j * 128:(_j + 1) * 128],
                                           e2mat[_hb:_hb + 64, :],
                                           q_ap(_c // 2, _q, 128)[_hb:_hb + 64, :],
                                           start=(_j == 0), stop=(_j == 3))
                            if _j == 3:
                                ins.then_inc(S[MM], 1)
                        sch.emit("tensor", fqq)
                    gqq = sch.bump(MM, 1)
                    sch.wait("vector", MM, gqq)
                    def fqqe(v, S, _b4=b4, _ps=ps):
                        v.tensor_copy(ar_h[:, _b4 * TS:(_b4 + 1) * TS],
                                      _ps[:, :]).then_inc(S[DVE], 1)
                    sch.emit("vector", fqqe)
                    act_prev = sch.bump(DVE, 1)
                state["stage_ev"] = act_prev
                state["stage_tok"][i] = act_prev

            def transposes(i):
                sch.wait("tensor", DVE, state["stage_tok"][i])
                sch.wait("tensor", ACTs, state["trz_evac"])
                sch.wait("tensor", ACTs, state["sreg_last"].get(2, 0))
                for cp in range(16):
                    def ftr(t, S, _cp=cp):
                        ins = t.matmul(ps_t[:, _cp * 64:(_cp + 1) * 64],
                                       qq_ap(pos_of_c(_cp), 0),
                                       ident[0:64, 0:64],
                                       start=(_cp == 0), stop=(_cp == 15),
                                       is_transpose=True)
                        if _cp == 15:
                            ins.then_inc(S[MM], 1)
                    sch.emit("tensor", ftr)
                gtr = sch.bump(MM, 1)
                sch.wait("vector", MM, gtr)
                def ftre(v, S):
                    v.tensor_copy(ar_h[:, 2048:3072], ps_t[:, 0:1024]).then_inc(S[DVE], 1)
                sch.emit("vector", ftre)
                state["qT_tok"][i] = sch.bump(DVE, 1)
                state["last_qT"] = state["qT_tok"][i]

            def sc_row(i, a):
                pa = pos_of_c(a)
                h = (a % 2) * 64
                for q4 in range(4):
                    qi = a * 4 + q4
                    ridx = qi % 2 if qi < 16 else (qi - 16) % 3
                    sreg = sregs3[ridx]
                    sch.wait("tensor", DVE, state["stage_tok"][i])
                    sch.wait("tensor", ACTs, state["sreg_last"].get(ridx, 0))
                    if ridx == 2:
                        sch.wait("tensor", ACTs, state["trz_evac"])
                        sch.wait("tensor", DVE, state["qT_tok"][i])
                    def fs(t, S, _pa=pa, _h=h, _q4=q4, _sr=sreg):
                        t.matmul(_sr[:, :],
                                 qq_ap(_pa, _h),
                                 ar_h[_h:_h + 64, _q4 * TS:(_q4 + 1) * TS],
                                 start=True, stop=True).then_inc(S[MM], 1)
                    sch.emit("tensor", fs)
                    gsc = sch.bump(MM, 1)
                    sch.wait("scalar", MM, gsc)
                    sch.wait("scalar", MM, state["avtok"].get((i - 1, a), 0))
                    sch.wait("scalar", DVE, state["zsum_tok"])
                    zdst = zc4a if i % 2 == 0 else zc4b
                    def fexp(s, S, _a=a, _q4=q4, _sr=sreg, _z=zdst):
                        s.activation(prow_ap(_a)[:, _q4 * TS:(_q4 + 1) * TS],
                                     _sr[:, :], AF.Exp, scale=SCALE,
                                     accum_out=_z[:, _a * 4 + _q4:_a * 4 + _q4 + 1]
                                     ).then_inc(S[ACTs], 1)
                    sch.emit("scalar", fexp)
                    state["exp_tok"][(i, a, q4)] = sch.bump(ACTs, 1)
                    state["sreg_last"][ridx] = state["exp_tok"][(i, a, q4)]

            def av_row(i, a):
                if a == 0:
                    sch.wait("tensor", DVE, max(state["avfree"], ln_done))
                sch.wait("tensor", ACTs, state["exp_tok"][(i, a, 3)])
                sch.wait("tensor", DVE, state["qT_tok"][i])
                for sg in range(4):
                    bank, lo = av_of_strip[sg]
                    tpos = (0, 0) if lo == 0 else (0, 64)
                    def fav(t, S, _a=a, _sg=sg, _bk=bank, _lo=lo, _tp=tpos):
                        ins = t.matmul(_bk[_lo:_lo + 64, :],
                                       qT_ap(_a),
                                       prow_ap(_a)[:, _sg * TS:(_sg + 1) * TS],
                                       start=(_a == 0), stop=(_a == 15),
                                       tile_position=_tp)
                        if _sg == 3:
                            ins.then_inc(S[MM], 1)
                    sch.emit("tensor", fav)
                state["avtok"][(i, a)] = sch.bump(MM, 1)

            def make_tail_stages(i):
                act_base = sch.val(ACTs)
                zsrc = zc4a if i % 2 == 0 else zc4b
                rzbev = [0]
                avev = [0]
                uev = [0, 0]

                def t_rz():
                    sch.wait("vector", ACTs, state["exp_tok"][(i, 15, 3)])
                    sch.wait("vector", DVE, sch.val(DVE))
                    base_z = sch.val(DVE)
                    def fz(v, S, _z=zsrc, _b=base_z):
                        v.tensor_reduce(
                            zcols[:, :].rearrange("p (a o) -> p a o", o=1),
                            _z[:, :].rearrange("p (a q) -> p a q", q=4),
                            mybir.AxisListType.X, ALU.add).then_inc(S[DVE], 1)
                        v.wait_ge(S[DVE], _b + 1)
                        v.reciprocal(zcols[:, :], zcols[:, :]).then_inc(S[DVE], 1)
                        v.wait_ge(S[DVE], _b + 2)
                        v.tensor_copy(rzbf[:, :], zcols[:, :]).then_inc(S[DVE], 1)
                    sch.emit("vector", fz)
                    state["zsum_tok"] = sch.bump(DVE, 3)
                    sch.wait("tensor", DVE, max(state["zsum_tok"], state["last_qT"]))
                    def frzt(t, S):
                        t.matmul(ps_t[0:16, 0:128], rzbf[:, :], ident[:, :],
                                 start=True, stop=True,
                                 is_transpose=True).then_inc(S[MM], 1)
                    sch.emit("tensor", frzt)
                    grzt = sch.bump(MM, 1)
                    sch.wait("scalar", MM, grzt)
                    def frzte(s, S):
                        s.copy(trz[:, :], ps_t[0:16, 0:128]).then_inc(S[ACTs], 1)
                    sch.emit("scalar", frzte)
                    state["trz_evac"] = sch.bump(ACTs, 1)

                def mk_strip(sg):
                    def t_strip():
                        bk, lo = av_of_strip[sg]
                        tpos = (0, 0) if lo == 0 else (0, 64)
                        sch.wait("tensor", ACTs, max(state["trz_evac"], act_base))
                        sch.wait("tensor", DVE, rzbev[0])
                        for j in range(4):
                            cj = c_of_pos(sg * 4 + j)
                            def frzb(t, S, _cj=cj, _j=j, _lo=lo, _tp=tpos):
                                ins = t.matmul(
                                    ps_rz[_lo:_lo + 64, _j * 128:(_j + 1) * 128],
                                    sel16[:, _cj * 64:(_cj + 1) * 64],
                                    trz[:, :],
                                    start=(_j == 0), stop=(_j == 3),
                                    tile_position=_tp)
                                if _j == 3:
                                    ins.then_inc(S[MM], 1)
                            sch.emit("tensor", frzb)
                        grzb = sch.bump(MM, 1)
                        sch.wait("vector", MM, grzb)
                        sch.wait("vector", DVE, avev[0])
                        def frzbe(v, S, _lo=lo):
                            v.tensor_copy(rzb_sb[_lo:_lo + 64, :],
                                          ps_rz[_lo:_lo + 64, :]).then_inc(S[DVE], 1)
                        sch.emit("vector", frzbe)
                        rzbev[0] = sch.bump(DVE, 1)
                        sch.wait("vector", MM, state["avtok"][(i, 15)])
                        sch.wait("vector", DVE, rzbev[0])
                        def fave(v, S, _sg=sg, _lo=lo, _bk=bk):
                            for j in range(4):
                                c = c_of_pos(_sg * 4 + j)
                                ins = v.tensor_mul(
                                    ofm_ap(c // 2, _lo, _lo + 64),
                                    _bk[_lo:_lo + 64, j * 128:(j + 1) * 128],
                                    rzb_sb[_lo:_lo + 64, j * 128:(j + 1) * 128])
                                if j == 3:
                                    ins.then_inc(S[DVE], 1)
                        sch.emit("vector", fave)
                        avev[0] = sch.bump(DVE, 1)
                        if sg == 3:
                            state["avfree"] = avev[0]
                    return t_strip

                def mk_u(pair):
                    def t_u():
                        if pair == 0:
                            sch.wait("tensor", *wready["wu"])
                        for dc in (pair * 2, pair * 2 + 1):
                            ups = stg_u[dc % 2]
                            sch.wait("tensor", DVE, max(avev[0], uev[dc % 2]))
                            sch.wait("tensor", ACTs, act_base)
                            for ec in range(NKC):
                                def fu(t, S, _ec=ec, _dc=dc, _u=ups):
                                    ins = t.matmul(_u[:, 0:128], wutblk(_ec, _dc),
                                                   ar_h[:, 3072 + 128 * _ec:
                                                        3072 + 128 * _ec + 128],
                                                   start=(_ec == 0),
                                                   stop=(_ec == NKC - 1))
                                    if _ec == NKC - 1:
                                        ins.then_inc(S[MM], 1)
                                sch.emit("tensor", fu)
                            gu = sch.bump(MM, 1)
                            sch.wait("vector", MM, gu)
                            o0 = dc * NT + i * 128
                            def fux(v, S, _dc=dc, _o0=o0, _u=ups):
                                xsl = x_sb[:, _o0:_o0 + 128]
                                v.scalar_tensor_tensor(
                                    out=xsl, in0=_u[:, 0:128],
                                    scalar=bu_sb[:, _dc:_dc + 1], in1=xsl,
                                    op0=ALU.add, op1=ALU.add).then_inc(S[DVE], 1)
                            sch.emit("vector", fux)
                            uev[dc % 2] = sch.bump(DVE, 1)
                        if pair == 3:
                            state["ufree"] = max(uev)
                            state["avfree"] = max(state["avfree"], max(uev))
                            state["u_last"] = max(uev)
                    return t_u

                return [t_rz] + [mk_strip(sg) for sg in range(4)] + \
                    [mk_u(p) for p in range(4)]

            def qproj(half):
                t0h = half * 1024
                qev = [0, 0]
                for dc in range(NKC):
                    for st2 in range(2):
                        t0 = t0h + st2 * TS
                        pst = stg_u[(dc * 2 + st2) % 2]
                        sch.wait("tensor", *wready["wq"])
                        sch.wait("tensor", DVE, max(ln_tok[half * 2 + st2],
                                                    state["avfree"],
                                                    state["ufree"]))
                        sch.wait("tensor", DVE, qev[(dc * 2 + st2) % 2])
                        for kc in range(NKC):
                            def fq(t, S, _kc=kc, _dc=dc, _t0=t0, _pst=pst):
                                ins = t.matmul(_pst[:, :], wqtblk(_kc, _dc),
                                               ln_ap(_kc, _t0, TS),
                                               start=(_kc == 0), stop=(_kc == NKC - 1))
                                if _kc == NKC - 1:
                                    ins.then_inc(S[MM], 1)
                            sch.emit("tensor", fq)
                        gq = sch.bump(MM, 1)
                        sch.wait("vector", MM, gq)
                        def fqe(v, S, _dc=dc, _toff=t0 - t0h, _pst=pst):
                            v.tensor_scalar_add(q_ap(_dc, _toff, TS), _pst[:, :],
                                                bq_sb[:, _dc:_dc + 1]).then_inc(S[DVE], 1)
                        sch.emit("vector", fqe)
                        qev[(dc * 2 + st2) % 2] = sch.bump(DVE, 1)
                return max(qev)

            q_ready = qproj(0)
            if "after_qproj0" in hooks:
                hooks["after_qproj0"](sch.val(MM))
            staging(0, q_ready)
            transposes(0)
            for i in range(16):
                if i == 8:
                    for a in range(3, 16):
                        av_row(7, a)
                    for st_fn in make_tail_stages(7):
                        st_fn()
                    q_ready = qproj(1)
                    if "after_qproj1" in hooks:
                        hooks["after_qproj1"](sch.val(MM))
                    staging(8, q_ready)
                    transposes(8)
                for a in range(3):
                    sc_row(i, a)
                if i % 8 != 0:
                    for a in range(3, 16):
                        av_row(i - 1, a)
                    transposes(i)
                stages = make_tail_stages(i - 1) if i % 8 != 0 else []
                for a in range(3, 16):
                    sc_row(i, a)
                    si = a - 3
                    if 0 <= si < len(stages):
                        stages[si]()
                for a in range(3):
                    av_row(i, a)
                if i % 8 != 7:
                    staging(i + 1, q_ready)
            for a in range(3, 16):
                av_row(15, a)
            for st_fn in make_tail_stages(15):
                st_fn()
            return state["u_last"], sch.val(MM)

        do1, doa, do2 = ("ffn1" in phases), ("attn" in phases), ("ffn2" in phases)
        seg_psums = []

        wst1 = {"w1": [None] * NSUP, "w2": [None] * NSUP,
                "sup_last_mm": [0] * NSUP, "issue_next": None}
        wst2 = {"w1": [None] * NSUP, "w2": [None] * NSUP,
                "sup_last_mm": [0] * NSUP, "issue_next": None}

        xw = 0
        ffn1_mm = 0
        if do1:
            with nc.psum_tensor("ps_d1", [128, TS], F32) as ps_d1, \
                 nc.psum_tensor("ps_w1", [1, TS], F32) as ps_w1, \
                 nc.psum_tensor("ps_v1", [1, TS], F32) as ps_v1, \
                 nc.psum_tensor("ps_e1", [128, TS], F32) as ps_e1, \
                 nc.psum_tensor("ps_f1", [128, TS], F32) as ps_f1:
                for s0 in range(2):
                    wst1["w1"][s0] = issue_w1(s0, s0 % 2, 0, "f1")
                    wst1["w2"][s0] = issue_w2(s0, s0 % 2, 0)

                def issue_next_ffn1(sup):
                    nxt = sup + 2
                    tok = wst1["sup_last_mm"][sup]
                    if nxt < NSUP:
                        wst1["w1"][nxt] = issue_w1(nxt, nxt % 2, tok, "f1")
                        wst1["w2"][nxt] = issue_w2(nxt, nxt % 2, tok)
                wst1["issue_next"] = issue_next_ffn1
                xw, _, ffn1_mm = emit_ffn(0, wst1, ps_d1, ps_w1, ps_v1, ps_e1, ps_f1)
                if doa:
                    wready["wq"] = issue_wqut("q", wst1["sup_last_mm"][10])
            sch.seg_break()

        attn_mm = ffn1_mm
        if doa:
            if "wq" not in wready:
                wready["wq"] = issue_wqut("q", ffn1_mm)
            hooks = {}
            def after_qproj0(mmtok):
                wu_wait = max(mmtok, wst1["sup_last_mm"][11] if do1 else 0)
                wready["wu"] = issue_wqut("u", wu_wait)
            def after_qproj1(mmtok):
                if do2:
                    wst2["w1"][0] = issue_w1(0, 0, mmtok, "f2")
            hooks["after_qproj0"] = after_qproj0
            hooks["after_qproj1"] = after_qproj1
            with nc.psum_tensor("s0", [128, TS], F32) as s0, \
                 nc.psum_tensor("s1", [128, TS], F32) as s1, \
                 nc.psum_tensor("d_att", [128, TS], F32) as d_att, \
                 nc.psum_tensor("ps_t", [128, 1024], BF16) as ps_t, \
                 nc.psum_tensor("ps_rz", [128, TS], F32) as ps_rz:
                xw, attn_mm = emit_attn(xw, s0, s1, d_att, ps_t, ps_rz, hooks)
            sch.seg_break()

        xtok = [xw] * NKC
        if do2:
            with nc.psum_tensor("ps_d2", [128, TS], F32) as ps_d2, \
                 nc.psum_tensor("ps_w2", [1, TS], F32) as ps_w2, \
                 nc.psum_tensor("ps_v2", [1, TS], F32) as ps_v2, \
                 nc.psum_tensor("ps_e2", [128, TS], F32) as ps_e2, \
                 nc.psum_tensor("ps_f2", [128, TS], F32) as ps_f2:
                if wst2["w1"][0] is None:
                    wst2["w1"][0] = issue_w1(0, 0, attn_mm, "f2")
                wst2["w2"][0] = issue_w2(0, 0, attn_mm)
                wst2["w1"][1] = issue_w1(1, 1, attn_mm, "f2")
                wst2["w2"][1] = issue_w2(1, 1, attn_mm)

                def issue_next_ffn2(sup):
                    nxt = sup + 2
                    tok = wst2["sup_last_mm"][sup]
                    if nxt < NSUP:
                        wst2["w1"][nxt] = issue_w1(nxt, nxt % 2, tok, "f2")
                        wst2["w2"][nxt] = issue_w2(nxt, nxt % 2, tok)
                wst2["issue_next"] = issue_next_ffn2
                _, xtok, _ = emit_ffn(xw, wst2, ps_d2, ps_w2, ps_v2, ps_e2, ps_f2)

        for dc in range(NKC):
            sch.wait("sync", DVE, xtok[dc])
            def fy(sy, S, _dc=dc):
                sy.dma_start(out=y_h[_dc * 128:(_dc + 1) * 128, :],
                             in_=x_sb[:, _dc * NT:(_dc + 1) * NT]).then_inc(S[OUT], 16)
            sch.emit("sync", fy)
            sch.bump(OUT, 16)
        sch.wait("sync", OUT, sch.val(OUT))

        segments = sch.finish()
        for seg in segments:
            if not any(seg[e] for e in ENGINES):
                continue
            with nc.Block() as block:
                @block.gpsimd
                def _(g, _seg=seg):
                    for f in _seg["gpsimd"]:
                        f(g, sems)

                @block.tensor
                def _(t, _seg=seg):
                    for f in _seg["tensor"]:
                        f(t, sems)

                @block.vector
                def _(v, _seg=seg):
                    for f in _seg["vector"]:
                        f(v, sems)

                @block.scalar
                def _(s, _seg=seg):
                    for f in _seg["scalar"]:
                        f(s, sems)

                @block.sync
                def _(sy, _seg=seg):
                    for f in _seg["sync"]:
                        f(sy, sems)

    return nc


_NC_CACHE = {}


def _get_nc(phases):
    key = tuple(phases)
    if key not in _NC_CACHE:
        _NC_CACHE[key] = build(phases)
    return _NC_CACHE[key]


def make_inputs(L0_attn_x, L0_attn_wq, L0_attn_bq, L0_attn_wu, L0_attn_bu,
                L0_ffn_gate_w, L0_ffn_w1, L0_ffn_b1, L0_ffn_w2, L0_ffn_b2,
                L0_ffn_sw1, L0_ffn_sb1, L0_ffn_sw2, L0_ffn_sb2):
    f = lambda a: np.asarray(a, dtype=np.float32)
    x = f(L0_attn_x)
    w1 = f(L0_ffn_w1)
    w2 = f(L0_ffn_w2)
    w1cat = np.ascontiguousarray(np.concatenate([f(L0_ffn_sw1), w1[0], w1[1]], axis=1))
    w2cat = np.ascontiguousarray(np.concatenate([f(L0_ffn_sw2), w2[0], w2[1]], axis=0))
    wqt_ = np.ascontiguousarray(f(L0_attn_wq).T)
    wut_ = np.ascontiguousarray(f(L0_attn_wu).T)
    b1cat = np.concatenate([f(L0_ffn_sb1), f(L0_ffn_b1)[0], f(L0_ffn_b1)[1]])
    b1h_ = np.ascontiguousarray(b1cat.reshape(96, 128).T)
    b2 = f(L0_ffn_b2)
    b2x_ = np.ascontiguousarray(np.concatenate(
        [b2[0].reshape(8, 128).T, b2[1].reshape(8, 128).T,
         f(L0_ffn_sb2).reshape(8, 128).T], axis=1))
    bqc_ = np.ascontiguousarray(f(L0_attn_bq).reshape(8, 128).T)
    buc_ = np.ascontiguousarray(f(L0_attn_bu).reshape(8, 128).T)
    gw = f(L0_ffn_gate_w)
    gwc_ = np.ascontiguousarray(np.concatenate(
        [gw[0].reshape(8, 128).T, gw[1].reshape(8, 128).T], axis=1))
    shared = dict(w1c=w1cat, w2c=w2cat, wqt=wqt_, wut=wut_, b1h=b1h_, b2x=b2x_,
                  bqc=bqc_, buc=buc_, gwc=gwc_)
    in_maps = []
    for i in range(8):
        m = dict(shared)
        m["x"] = np.ascontiguousarray(x[i].T)
        in_maps.append(m)
    return in_maps


def kernel(L0_attn_x, L0_attn_wq, L0_attn_bq, L0_attn_wu, L0_attn_bu,
           L0_ffn_gate_w, L0_ffn_w1, L0_ffn_b1, L0_ffn_w2, L0_ffn_b2,
           L0_ffn_sw1, L0_ffn_sb1, L0_ffn_sw2, L0_ffn_sb2,
           _phases=("ffn1", "attn", "ffn2")):
    in_maps = make_inputs(L0_attn_x, L0_attn_wq, L0_attn_bq, L0_attn_wu,
                          L0_attn_bu, L0_ffn_gate_w, L0_ffn_w1, L0_ffn_b1,
                          L0_ffn_w2, L0_ffn_b2, L0_ffn_sw1, L0_ffn_sb1,
                          L0_ffn_sw2, L0_ffn_sb2)
    nc = _get_nc(_phases)
    res = run_bass_kernel_spmd(nc, in_maps, core_ids=list(range(8)))
    out = np.stack([np.ascontiguousarray(r["y"].T) for r in res.results], axis=0)
    return out.astype(np.float32)


# version 23
# speedup vs baseline: 1.1322x; 1.0113x over previous
import contextlib
import numpy as np

import concourse.bass as bass
import concourse.mybir as mybir
from concourse.bass_utils import run_bass_kernel_spmd

F32 = mybir.dt.float32
BF16 = mybir.dt.bfloat16
AF = mybir.ActivationFunctionType
ALU = mybir.AluOpType

NT = 2048
D = 1024
NKC = 8
TS = 512
NSTRIP = NT // TS
HE = 1024
NSUP = 12
EPS = 1e-5
SCALE = 1.0 / 32.0

ENGINES = ("gpsimd", "tensor", "vector", "scalar", "sync")


def pos_of_c(c):
    return (c // 2) if c % 2 == 0 else 8 + (c // 2)


def c_of_pos(p):
    return 2 * p if p < 8 else 2 * (p - 8) + 1


class Sched:

    def __init__(self):
        self.segs = []
        self.ops = {e: [] for e in ENGINES}
        self.cnt = {}
        self.lastwait = {e: {} for e in ENGINES}

    def seg_break(self):
        self.segs.append(self.ops)
        self.ops = {e: [] for e in ENGINES}

    def finish(self):
        self.seg_break()
        return self.segs

    def emit(self, eng, fn):
        self.ops[eng].append(fn)

    def wait(self, eng, sem, val):
        if not val or val <= 0:
            return
        if self.lastwait[eng].get(sem, -1) >= val:
            return
        self.lastwait[eng][sem] = val
        self.emit(eng, lambda e, S, _s=sem, _v=val: e.wait_ge(S[_s], _v))

    def bump(self, sem, by):
        self.cnt[sem] = self.cnt.get(sem, 0) + by
        return self.cnt[sem]

    def val(self, sem):
        return self.cnt.get(sem, 0)


def build(phases=("ffn1", "attn", "ffn2")):
    nc = bass.Bass("TRN2", target_bir_lowering=False)

    x_h = nc.dram_tensor("x", [D, NT], F32, kind="ExternalInput").ap()
    w1c = nc.dram_tensor("w1c", [D, 3 * 4096], F32, kind="ExternalInput").ap()
    w2c = nc.dram_tensor("w2c", [3 * 4096, D], F32, kind="ExternalInput").ap()
    wqt = nc.dram_tensor("wqt", [D, D], F32, kind="ExternalInput").ap()
    wut = nc.dram_tensor("wut", [D, D], F32, kind="ExternalInput").ap()
    b1h = nc.dram_tensor("b1h", [128, 96], F32, kind="ExternalInput").ap()
    b2x = nc.dram_tensor("b2x", [128, 24], F32, kind="ExternalInput").ap()
    bqc = nc.dram_tensor("bqc", [128, 8], F32, kind="ExternalInput").ap()
    buc = nc.dram_tensor("buc", [128, 8], F32, kind="ExternalInput").ap()
    gwc = nc.dram_tensor("gwc", [128, 16], F32, kind="ExternalInput").ap()
    y_h = nc.dram_tensor("y", [D, NT], F32, kind="ExternalOutput").ap()

    es = contextlib.ExitStack()
    with es:
        sb = lambda n, s, d: es.enter_context(nc.sbuf_tensor(n, s, d))
        mksem = lambda n: es.enter_context(nc.semaphore(n))

        x_sb = sb("x_sb", [128, NKC * NT], F32)
        ar_ln = sb("ar_ln", [128, 16384], BF16)
        ar_w1 = sb("ar_w1", [128, 16384], BF16)
        ar_w2 = sb("ar_w2", [128, 16384], BF16)
        ar_h = sb("ar_h", [128, 4096], BF16)
        ar_b = sb("ar_b", [128, 8192], BF16)
        p_ex = sb("p_ex", [128, 8192], BF16)
        rzb_sb = sb("rzb_sb", [128, TS], BF16)
        xb_s = rzb_sb
        m0row = sb("m0row", [1, TS], BF16)
        m1row = sb("m1row", [1, TS], BF16)
        ones_col = sb("ones_col", [128, 1], BF16)
        ones_colf = sb("ones_colf", [128, 1], F32)
        onesLf = sb("onesLf", [1, 128], F32)
        onesLb = sb("onesLb", [1, 128], BF16)
        ident = sb("ident", [128, 128], BF16)
        e2mat = sb("e2mat", [128, 128], BF16)
        sel16 = sb("sel16", [16, 1024], BF16)
        b1hf = sb("b1hf", [128, 96], F32)
        b2sh = sb("b2sh", [128, 8], F32)
        bq_sb = sb("bq_sb", [128, 8], F32)
        bu_sb = sb("bu_sb", [128, 8], F32)
        gw_bf = sb("gw_bf", [128, 16], BF16)
        zcols = sb("zcols", [128, 16], F32)
        zc4a = sb("zc4a", [128, 64], F32)
        zc4b = sb("zc4b", [128, 64], F32)
        rzbf = sb("rzbf", [128, 16], BF16)
        trz = sb("trz", [16, 128], BF16)

        ps_a = es.enter_context(nc.psum_tensor("ps_a", [128, TS], F32))
        ps_b = es.enter_context(nc.psum_tensor("ps_b", [128, TS], F32))
        ps_c = es.enter_context(nc.psum_tensor("ps_c", [128, TS], F32))

        sems = {}
        for s in ("w1a", "w1b", "w2a", "w2b", "wx", "x", "mm", "act", "dve", "gp", "out"):
            sems[s] = mksem("s_" + s)

        sch = Sched()
        X, MM, ACTs, DVE, GP, OUT = "x", "mm", "act", "dve", "gp", "out"

        scaleB = ar_b[:, 6144:7168].bitcast(F32)
        meanB = ar_b[:, 7168:8192].bitcast(F32)
        r1 = meanB[0:1, :]
        r2 = scaleB[0:1, :]
        r3 = m0row

        def lnm_ap(kc):
            return ar_b[:, kc * TS:(kc + 1) * TS]

        def mask_ap(which, strip):
            off = 8 * TS + which * NT
            return ar_b[:, off + strip * TS: off + (strip + 1) * TS]

        def q_ap(kc, t0, w):
            return ar_b[:, kc * 1024 + t0: kc * 1024 + t0 + w]

        def ln_ap(kc, t0, w):
            half = t0 // 1024
            tt = t0 - half * 1024
            base = half * 8192 + kc * 1024 + tt
            return ar_ln[:, base: base + w]

        def prow_ap(a):
            if a < 8:
                return ar_w2[:, a * NT:(a + 1) * NT]
            if a < 12:
                return ar_ln[:, (a - 8) * NT:(a - 7) * NT]
            return p_ex[:, (a - 12) * NT:(a - 11) * NT]

        def w1blk(slot, kc, hc):
            o = slot * 8192 + kc * 1024 + hc * 128
            return ar_w1[:, o:o + 128]

        def w2blk(slot, hc, dc):
            o = slot * 8192 + hc * 1024 + dc * 128
            return ar_w2[:, o:o + 128]

        def wqtblk(kc, dc):
            o = kc * 1024 + dc * 128
            return ar_w1[:, o:o + 128]

        def wutblk(kc, dc):
            o = 8192 + kc * 1024 + dc * 128
            return ar_w1[:, o:o + 128]

        def qq_ap(posc, lo64):
            return ar_h[lo64:lo64 + 64, 128 * posc: 128 * posc + 128]

        def qT_ap(cp, w=64):
            return ar_h[:, 2048 + 64 * cp: 2048 + 64 * cp + w]

        def ofm_ap(ec, lo, hi):
            return ar_h[lo:hi, 3072 + 128 * ec: 3072 + 128 * ec + 128]

        def g_init(g, S):
            n = 0
            g.memset(ones_col[:, :], 1.0)
            g.memset(ones_colf[:, :], 1.0)
            g.memset(onesLf[:, :], 1.0)
            g.memset(onesLb[:, :], 1.0)
            g.memset(ident[:, :], 0.0).then_inc(S[GP], 1); n += 1
            g.wait_ge(S[GP], n)
            g.affine_select(out=ident[:, :], in_=ident[:, :], compare_op=ALU.not_equal,
                            fill=1.0, base=0, pattern=[[-1, 128]],
                            channel_multiplier=1).then_inc(S[GP], 1); n += 1
            g.memset(e2mat[:, :], 0.0).then_inc(S[GP], 1); n += 1
            for base in (0, 64, -64):
                g.wait_ge(S[GP], n)
                g.affine_select(out=e2mat[:, :], in_=e2mat[:, :], compare_op=ALU.not_equal,
                                fill=1.0, base=base, pattern=[[-1, 128]],
                                channel_multiplier=1).then_inc(S[GP], 1); n += 1
            g.memset(sel16[:, :], 0.0).then_inc(S[GP], 1); n += 1
            g.wait_ge(S[GP], n)
            g.affine_select(
                out=sel16[:, :].rearrange("p (c d) -> p c d", c=16),
                in_=sel16[:, :].rearrange("p (c d) -> p c d", c=16),
                compare_op=ALU.not_equal, fill=1.0, base=0,
                pattern=[[-1, 16], [0, 64]], channel_multiplier=1,
            ).then_inc(S[GP], 1); n += 1
        sch.emit("gpsimd", g_init)
        tok_gpinit = sch.bump(GP, 8)

        def sy_load(sy, S):
            sy.dma_start(
                out=x_sb[:, :].rearrange("p (kc t) -> p kc t", kc=NKC),
                in_=x_h.rearrange("(kc p) t -> p kc t", p=128),
            ).then_inc(S[X], 16)
            for src, dst in ((b1h, b1hf), (bqc, bq_sb), (buc, bu_sb)):
                sy.dma_start(out=dst[:, :], in_=src[:, :]).then_inc(S[X], 16)
            sy.dma_start(out=b2sh[:, :], in_=b2x[:, 16:24]).then_inc(S[X], 16)
            sy.dma_start(out=meanB[:, 0:16], in_=b2x[:, 0:16]).then_inc(S[X], 16)
        sch.emit("sync", sy_load)
        tok_xload = sch.bump(X, 16 * 6)

        def g_gw(g, S):
            g.dma_start(out=gw_bf[:, :], in_=gwc[:, :]).then_inc(S["wx"], 16)
        sch.emit("gpsimd", g_gw)
        tok_gw = ("wx", sch.bump("wx", 16))

        sch.wait("scalar", X, tok_xload)
        def sc_bias(s, S):
            s.mul(b1hf[:, :], b1hf[:, :], 0.5).then_inc(S[ACTs], 1)
        sch.emit("scalar", sc_bias)
        tok_b1 = sch.bump(ACTs, 1)

        sch.wait("vector", X, tok_xload)
        base_dve = sch.val(DVE)
        def dv_bias(v, S, _b=base_dve):
            v.tensor_add(b2sh[:, :], b2sh[:, :], meanB[:, 0:8]).then_inc(S[DVE], 1)
            v.wait_ge(S[DVE], _b + 1)
            v.tensor_add(b2sh[:, :], b2sh[:, :], meanB[:, 8:16]).then_inc(S[DVE], 1)
            v.wait_ge(S[DVE], _b + 2)
            v.tensor_scalar_mul(b2sh[:, :], b2sh[:, :], 0.5).then_inc(S[DVE], 1)
        sch.emit("vector", dv_bias)
        tok_b2 = sch.bump(DVE, 3)

        wready = {}

        def issue_w1(sup, slot, wait_mm, arena_name):
            sch.wait("gpsimd", MM, wait_mm)
            wsem = "w1a" if slot == 0 else "w1b"
            def f(g, S, _sup=sup, _slot=slot, _ws=wsem):
                g.dma_start(
                    out=ar_w1[:, _slot * 8192:(_slot + 1) * 8192].rearrange(
                        "p (kc h) -> p kc h", kc=NKC),
                    in_=w1c[:, _sup * HE:(_sup + 1) * HE].rearrange(
                        "(kc p) h -> p kc h", p=128),
                ).then_inc(S[_ws], 16)
            sch.emit("gpsimd", f)
            return (wsem, sch.bump(wsem, 16))

        def issue_w2(sup, slot, wait_mm):
            sch.wait("gpsimd", MM, wait_mm)
            wsem = "w2a" if slot == 0 else "w2b"
            def f(g, S, _sup=sup, _slot=slot, _ws=wsem):
                g.dma_start(
                    out=ar_w2[:, _slot * 8192:(_slot + 1) * 8192].rearrange(
                        "p (hc d) -> p hc d", hc=NKC),
                    in_=w2c[_sup * HE:(_sup + 1) * HE, :].rearrange(
                        "(hc p) d -> p hc d", p=128),
                ).then_inc(S[_ws], 16)
            sch.emit("gpsimd", f)
            return (wsem, sch.bump(wsem, 16))

        def issue_wqut(which, wait_mm):
            sch.wait("gpsimd", MM, wait_mm)
            src = wqt if which == "q" else wut
            off = 0 if which == "q" else 8192
            def f(g, S, _src=src, _off=off):
                g.dma_start(
                    out=ar_w1[:, _off:_off + 8192].rearrange("p (kc d) -> p kc d", kc=NKC),
                    in_=_src.rearrange("(kc p) d -> p kc d", p=128),
                ).then_inc(S["wx"], 16)
            sch.emit("gpsimd", f)
            return ("wx", sch.bump("wx", 16))

        pexf = p_ex[:, 0:4096].bitcast(F32)
        mean2 = pexf[:, 1024:1536]
        scale2 = pexf[:, 1536:2048]
        xb2 = p_ex[:, 4096:4608]
        gate_r1 = p_ex[:, 4608:5632].bitcast(F32)[0:1, :]
        gate_r2 = p_ex[:, 5632:6656].bitcast(F32)[0:1, :]

        def make_ln_stages(is_ffn, xwaits, statps, bcps, ext=None):
            ext = ext if ext is not None else {}
            ln_tok = [0] * NSTRIP
            mask_tok = [0] * NSTRIP
            s1ev = [0] * NSTRIP
            s2ev = [0] * NSTRIP
            statmm = [0] * NSTRIP
            sqtok = [0] * NSTRIP
            bcev = [0] * NSTRIP
            bfree = [(DVE, 0), (DVE, 0)]
            sq_mm_last = [0, 0]
            ln_ktok = [[0] * NKC for _ in range(NSTRIP)]
            ext["ln_ktok"] = ln_ktok

            def tiles(st):
                if st % 2 == 0:
                    return meanB, scaleB
                return mean2, scale2

            def scratch(st):
                return xb_s if st % 2 == 0 else xb2

            def stage_stats(st):
                t0 = st * TS
                sp1 = statps[0]
                sp2 = statps[1]
                mB, sB = tiles(st)
                r1, r2 = mB[0:1, :], sB[0:1, :]
                sch.wait("tensor", GP, tok_gpinit)
                sch.wait("tensor", X, tok_xload)
                sch.wait("tensor", DVE, max(xwaits[st], tok_b2,
                                            s1ev[st - 1] if st >= 1 else 0))
                for kc in range(NKC):
                    def fmm(t, S, _kc=kc, _t0=t0, _sp=sp1):
                        t.matmul(_sp, ones_colf[:, :],
                                 x_sb[:, _kc * NT + _t0: _kc * NT + _t0 + TS],
                                 start=(_kc == 0), stop=(_kc == NKC - 1)).then_inc(S[MM], 1)
                    sch.emit("tensor", fmm)
                    prev_mm = sch.bump(MM, 1)
                s1mm = prev_mm
                sch.wait("vector", MM, s1mm)
                sch.wait("vector", DVE, ln_tok[st - 2] if st >= 2 else 0)
                def fs1(v, S, _sp=sp1, _r=r1):
                    v.tensor_scalar_mul(_r, _sp, 1.0 / D).then_inc(S[DVE], 1)
                sch.emit("vector", fs1)
                s1ev[st] = sch.bump(DVE, 1)
                sch.wait("scalar", DVE, max(xwaits[st], tok_b2))
                sch.wait("scalar", X, tok_xload)
                for kc in range(NKC):
                    xb = xb_s if kc % 2 == 0 else xb2
                    sch.wait("scalar", MM, sq_mm_last[kc % 2])
                    def fsq(s, S, _kc=kc, _t0=t0, _xb=xb):
                        s.activation(_xb[:, :],
                                     x_sb[:, _kc * NT + _t0: _kc * NT + _t0 + TS],
                                     AF.Square).then_inc(S[ACTs], 1)
                    sch.emit("scalar", fsq)
                    sqtok[st] = sch.bump(ACTs, 1)
                    sch.wait("tensor", ACTs, sqtok[st])
                    if kc == 0:
                        sch.wait("tensor", DVE, s2ev[st - 1] if st >= 1 else 0)
                    def fmm2(t, S, _kc=kc, _sp=sp2, _xb=xb):
                        t.matmul(_sp, ones_col[:, :], _xb[:, :],
                                 start=(_kc == 0), stop=(_kc == NKC - 1)).then_inc(S[MM], 1)
                    sch.emit("tensor", fmm2)
                    prev_mm = sch.bump(MM, 1)
                    sq_mm_last[kc % 2] = prev_mm
                statmm[st] = prev_mm
                sch.wait("vector", MM, prev_mm)
                def fs2(v, S, _sp=sp2, _r=r2):
                    v.tensor_scalar(out=_r, in0=_sp, scalar1=1.0 / D, scalar2=EPS,
                                    op0=ALU.mult, op1=ALU.add).then_inc(S[DVE], 1)
                sch.emit("vector", fs2)
                s2ev[st] = sch.bump(DVE, 1)

            def stage_post(st):
                t0 = st * TS
                mB, sB = tiles(st)
                r1, r2 = mB[0:1, :], sB[0:1, :]
                bP = bcps[st % 2]
                sch.wait("vector", DVE, s1ev[st])
                def frow2a(v, S, _r1=r1):
                    v.tensor_mul(r3[:, :], _r1, _r1).then_inc(S[DVE], 1)
                sch.emit("vector", frow2a)
                rowa = sch.bump(DVE, 1)
                sch.wait("vector", DVE, max(s2ev[st], rowa))
                def frow2b(v, S, _r2=r2):
                    v.scalar_tensor_tensor(out=_r2, in0=r3[:, :], scalar=-1.0,
                                           in1=_r2, op0=ALU.mult,
                                           op1=ALU.add).then_inc(S[DVE], 1)
                sch.emit("vector", frow2b)
                rowd = sch.bump(DVE, 1)
                sch.wait("scalar", DVE, rowd)
                def frln(s, S, _r2=r2):
                    s.activation(_r2, _r2, AF.Ln).then_inc(S[ACTs], 1)
                sch.emit("scalar", frln)
                rowl = sch.bump(ACTs, 1)
                sch.wait("tensor", ACTs, rowl)
                sch.wait("tensor", *bfree[st % 2])
                def fbc2(t, S, _r=r2, _bp=bP):
                    t.matmul(_bp, onesLf[:, :], _r, start=True, stop=True).then_inc(S[MM], 1)
                sch.emit("tensor", fbc2)
                bct2 = sch.bump(MM, 1)
                sch.wait("scalar", MM, bct2)
                sch.wait("scalar", DVE, ln_tok[st - 2] if st >= 2 else 0)
                def fsc(s, S, _sB=sB, _bp=bP):
                    s.activation(_sB[:, :], _bp, AF.Exp, scale=-0.5).then_inc(S[ACTs], 1)
                sch.emit("scalar", fsc)
                sctok = sch.bump(ACTs, 1)
                sch.wait("tensor", ACTs, sctok)
                def fbc1(t, S, _r=r1, _bp=bP):
                    t.matmul(_bp, onesLf[:, :], _r, start=True, stop=True).then_inc(S[MM], 1)
                sch.emit("tensor", fbc1)
                bct1 = sch.bump(MM, 1)
                sch.wait("vector", MM, bct1)
                sch.wait("vector", ACTs, sctok)
                for kc in range(NKC):
                    base_n = sch.val(DVE)
                    def fn(v, S, _kc=kc, _t0=t0, _b=base_n, _bp=bP, _sB=sB):
                        v.wait_ge(S[DVE], _b)
                        v.tensor_sub(ln_ap(_kc, _t0, TS),
                                     x_sb[:, _kc * NT + _t0: _kc * NT + _t0 + TS],
                                     _bp).then_inc(S[DVE], 1)
                        v.wait_ge(S[DVE], _b + 1)
                        v.tensor_mul(ln_ap(_kc, _t0, TS), ln_ap(_kc, _t0, TS),
                                     _sB[:, :]).then_inc(S[DVE], 1)
                    sch.emit("vector", fn)
                    ln_tok[st] = sch.bump(DVE, 2)
                    ln_ktok[st][kc] = ln_tok[st]
                bfree[st % 2] = (DVE, ln_tok[st])


            def stage_gate(st):
                t0 = st * TS
                r1, r2 = gate_r1, gate_r2
                bA = bcps[0]
                gevac = 0
                for e in range(2):
                    gp_ps = bcps[e]
                    for kc in range(NKC):
                        sch.wait("tensor", *tok_gw)
                        sch.wait("tensor", DVE, ln_tok[st])
                        sch.wait("tensor", *bfree[e])
                        def fg(t, S, _e=e, _kc=kc, _t0=t0, _sp=gp_ps):
                            t.matmul(_sp[0:1, :],
                                     gw_bf[:, _e * 8 + _kc:_e * 8 + _kc + 1],
                                     ln_ap(_kc, _t0, TS),
                                     start=(_kc == 0), stop=(_kc == NKC - 1)
                                     ).then_inc(S[MM], 1)
                        sch.emit("tensor", fg)
                        gtok = sch.bump(MM, 1)
                    sch.wait("vector", MM, gtok)
                    sch.wait("vector", DVE, ln_tok[st])
                    dstrow = r1 if e == 0 else r2
                    def fge(v, S, _d=dstrow, _sp=gp_ps):
                        v.tensor_copy(_d, _sp[0:1, :]).then_inc(S[DVE], 1)
                    sch.emit("vector", fge)
                    gevac = sch.bump(DVE, 1)
                    bfree[e] = (DVE, gevac)
                sch.wait("vector", DVE, gevac)
                base_mr = sch.val(DVE)
                def fmrow(v, S, _r1=r1, _r2=r2, _b=base_mr):
                    v.tensor_tensor(m0row[:, :], _r1, _r2,
                                    ALU.is_ge).then_inc(S[DVE], 1)
                    v.wait_ge(S[DVE], _b + 1)
                    v.tensor_tensor(m1row[:, :], _r1, _r2,
                                    ALU.is_lt).then_inc(S[DVE], 1)
                sch.emit("vector", fmrow)
                mrow_tok = sch.bump(DVE, 2)
                sch.wait("tensor", DVE, mrow_tok)
                sch.wait("tensor", *bfree[0])
                sch.wait("tensor", *bfree[1])
                bB2 = bcps[1]
                def fmb(t, S, _bp=bA, _bp2=bB2):
                    t.matmul(_bp, onesLb[:, :], m0row[:, :],
                             start=True, stop=True).then_inc(S[MM], 1)
                    t.matmul(_bp2, onesLb[:, :], m1row[:, :],
                             start=True, stop=True).then_inc(S[MM], 1)
                sch.emit("tensor", fmb)
                mbt = sch.bump(MM, 2)
                sch.wait("vector", MM, mbt)
                sch.wait("vector", DVE, ext.get("all_ln", 0))
                base_me = sch.val(DVE)
                def fmbe(v, S, _st=st, _bp=bA, _bp2=bB2, _b=base_me):
                    v.tensor_copy(mask_ap(0, _st), _bp).then_inc(S[DVE], 1)
                    v.wait_ge(S[DVE], _b + 1)
                    v.tensor_copy(mask_ap(1, _st), _bp2).then_inc(S[DVE], 1)
                sch.emit("vector", fmbe)
                mask_tok[st] = sch.bump(DVE, 2)
                bfree[0] = (DVE, mask_tok[st] - 1)
                bfree[1] = (DVE, mask_tok[st])

            stats_fns = [lambda _st=st: stage_stats(_st) for st in range(NSTRIP)]
            post_fns = [lambda _st=st: stage_post(_st) for st in range(NSTRIP)]
            gate_fns = [lambda _st=st: stage_gate(_st) for st in range(NSTRIP)]
            return stats_fns, post_fns, gate_fns, ln_tok, mask_tok

        def emit_ln(is_ffn, xwaits, statps, bcps):
            sf, pf, gf, ln_tok, mask_tok = make_ln_stages(is_ffn, xwaits, statps, bcps)
            for st in range(NSTRIP):
                sf[st]()
                if st > 0:
                    pf[st - 1]()
            pf[NSTRIP - 1]()
            if is_ffn:
                for st in range(NSTRIP):
                    gf[st]()
            return ln_tok, mask_tok

        def emit_ffn(xwait_dve, wst, ps_d, ps_w, ps_w2, ps_e, ps_f):
            ext = {}
            sf, pf, gf, ln_tok, mask_tok = make_ln_stages(
                True, [xwait_dve] * NSTRIP,
                (ps_w[:, :], ps_w2[:, :]), (ps_e[:, :], ps_f[:, :]), ext)
            ln_ktok = ext["ln_ktok"]
            h_relu_tok = [0] * NKC
            stt_tok = [[0] * NSTRIP for _ in range(NKC)]
            ps1_free = [0, 0]
            ps2_free = [0, 0]
            prev_mm1 = [0]
            lnm_tok = [None]
            lnm_ktok = [0] * NKC

            pending_lnm = []

            def build_lnm_pieces(st, klass, wait_mm):
                which = 0 if klass == 1 else 1
                def piece(kc):
                    sch.wait("vector", DVE, max(ln_tok[st], mask_tok[st]))
                    sch.wait("vector", MM, wait_mm)
                    def fm(v, S, _kc=kc, _st=st, _w=which):
                        v.tensor_mul(lnm_ap(_kc), ln_ap(_kc, _st * TS, TS),
                                     mask_ap(_w, _st)).then_inc(S[DVE], 1)
                    sch.emit("vector", fm)
                    lnm_tok[0] = sch.bump(DVE, 1)
                    lnm_ktok[kc] = lnm_tok[0]
                return [lambda _kc=kc: piece(_kc) for kc in range(NKC)]

            def build_lnm(st, klass, wait_mm):
                for p in build_lnm_pieces(st, klass, wait_mm):
                    p()
                return lnm_tok[0]

            def do_mm1(sup, st, klass, slot):
                if wst["w1"][sup]:
                    sch.wait("tensor", *wst["w1"][sup])
                if wst["w2"][sup]:
                    sch.wait("tensor", *wst["w2"][sup])
                if klass == 0:
                    rhs_sem, rhs_tok = DVE, ln_tok[st]
                    rhs = lambda kc, _t0=st * TS: ln_ap(kc, _t0, TS)
                    rhs_ktok = [ln_ktok[st][kc] for kc in range(NKC)]
                else:
                    rhs_sem, rhs_tok = DVE, lnm_tok[0]
                    rhs = lambda kc: lnm_ap(kc)
                    rhs_ktok = list(lnm_ktok)
                for hc in range(NKC):
                    pst = (ps_a, ps_b)[hc % 2]
                    sch.wait("tensor", ACTs, ps1_free[hc % 2])
                    if hc > 0:
                        sch.wait("tensor", rhs_sem, rhs_tok)
                    for kc in range(NKC):
                        if hc == 0:
                            sch.wait("tensor", rhs_sem, rhs_ktok[kc])
                        def f1(t, S, _kc=kc, _hc=hc, _slot=slot, _pst=pst, _rhs=rhs):
                            ins = t.matmul(_pst[:, :], w1blk(_slot, _kc, _hc),
                                           _rhs(_kc),
                                           start=(_kc == 0), stop=(_kc == NKC - 1))
                            if _kc == NKC - 1:
                                ins.then_inc(S[MM], 1)
                        sch.emit("tensor", f1)
                    g1 = sch.bump(MM, 1)
                    sch.wait("scalar", MM, g1)
                    sch.wait("scalar", ACTs, tok_b1)
                    def fr(s, S, _hc=hc, _sup=sup, _pst=pst):
                        s.activation(ar_h[:, _hc * TS:(_hc + 1) * TS], _pst[:, :],
                                     AF.Relu,
                                     bias=b1hf[:, _sup * 8 + _hc:_sup * 8 + _hc + 1],
                                     scale=0.5).then_inc(S[ACTs], 1)
                    sch.emit("scalar", fr)
                    h_relu_tok[hc] = sch.bump(ACTs, 1)
                    ps1_free[hc % 2] = h_relu_tok[hc]
                prev_mm1[0] = sch.val(MM)
                if klass > 0:
                    nst, nkl = (st + 1, klass) if st < NSTRIP - 1 else \
                        (0, (sup + 1) // 4 if sup + 1 < NSUP else None)
                    if nkl:
                        pending_lnm.extend(build_lnm_pieces(nst, nkl, prev_mm1[0]))
                    else:
                        lnm_tok[0] = None

            def do_mm2(sup, st, slot, mm2_extra):
                h_all = max(h_relu_tok)
                for dc in range(NKC):
                    pst = (ps_c, ps_d)[dc % 2]
                    sch.wait("tensor", DVE, max(ps2_free[dc % 2], mm2_extra))
                    if dc > 0:
                        sch.wait("tensor", ACTs, h_all)
                    for hc in range(NKC):
                        if dc == 0:
                            sch.wait("tensor", ACTs, h_relu_tok[hc])
                        def f2(t, S, _hc=hc, _dc=dc, _slot=slot, _pst=pst):
                            ins = t.matmul(_pst[:, :], w2blk(_slot, _hc, _dc),
                                           ar_h[:, _hc * TS:(_hc + 1) * TS],
                                           start=(_hc == 0), stop=(_hc == NKC - 1))
                            if _hc == NKC - 1:
                                ins.then_inc(S[MM], 1)
                        sch.emit("tensor", f2)
                    g2 = sch.bump(MM, 1)
                    sch.wait("vector", MM, g2)
                    sch.wait("vector", DVE, tok_b2)
                    o0 = dc * NT + st * TS
                    if sup == 0:
                        def fx(v, S, _dc=dc, _o0=o0, _pst=pst):
                            xsl = x_sb[:, _o0:_o0 + TS]
                            v.scalar_tensor_tensor(
                                out=xsl, in0=_pst[:, :],
                                scalar=b2sh[:, _dc:_dc + 1], in1=xsl,
                                op0=ALU.add, op1=ALU.add).then_inc(S[DVE], 1)
                    else:
                        def fx(v, S, _o0=o0, _pst=pst):
                            xsl = x_sb[:, _o0:_o0 + TS]
                            v.tensor_add(xsl, xsl, _pst[:, :]).then_inc(S[DVE], 1)
                    sch.emit("vector", fx)
                    stt_tok[dc][st] = sch.bump(DVE, 1)
                    ps2_free[dc % 2] = stt_tok[dc][st]
                    if pending_lnm:
                        pending_lnm.pop(0)()
                while pending_lnm:
                    pending_lnm.pop(0)()

            sf[0](); sf[1](); pf[0]()
            for st in range(NSTRIP):
                do_mm1(0, st, 0, 0)
                if st + 2 < NSTRIP:
                    sf[st + 2]()
                if st + 1 < NSTRIP:
                    pf[st + 1]()
                do_mm2(0, st, 0, xwait_dve)
                if st == 2:
                    ext["all_ln"] = max(ln_tok)
                if st >= 2:
                    gf[st - 2]()
            for st in (2, 3):
                gf[st]()
            wst["sup_last_mm"][0] = sch.val(MM)
            if wst["issue_next"] is not None:
                wst["issue_next"](0)
            for sup in range(1, NSUP):
                klass = sup // 4
                slot = sup % 2
                if klass > 0 and lnm_tok[0] is None:
                    lnm_tok[0] = build_lnm(0, klass, prev_mm1[0])
                for st in range(NSTRIP):
                    do_mm1(sup, st, klass, slot)
                    do_mm2(sup, st, slot, 0)
                wst["sup_last_mm"][sup] = sch.val(MM)
                if wst["issue_next"] is not None:
                    wst["issue_next"](sup)
            xdve = max(max(row) for row in stt_tok)
            xtok = [stt_tok[dc][NSTRIP - 1] for dc in range(NKC)]
            return xdve, xtok, sch.val(MM)

        def emit_attn(xwait_dve, s0, s1, d_att, ps_t, ps_rz, hooks):
            ln_tok, _ = emit_ln(False, [xwait_dve] * NSTRIP,
                                (s0[0:1, :], s1[0:1, :]),
                                (ps_c[:, :], d_att[:, :]))
            ln_done = sch.val(DVE)
            av_of_strip = [(ps_a, 0), (ps_b, 0), (ps_c, 64), (d_att, 64)]
            sregs3 = (s0, s1, ps_t[:, :].bitcast(F32))
            stg = (s0, s1)
            stg_u = (ps_a, ps_b)

            state = {
                "exp_tok": {},
                "avtok": {},
                "avfree": 0,
                "stage_tok": {},
                "stage_ev2": [0, 0],
                "ufree": 0,
                "qT_tok": {},
                "zsum_tok": 0,
                "last_qT": 0,
                "trz_evac": 0,
                "u_last": 0,
                "sreg_last": {},
            }

            def staging(i, q_ready):
                qoff = (i % 8) * 128
                for b4 in range(4):
                    ps = stg[b4 % 2]
                    sch.wait("tensor", ACTs, state["sreg_last"].get(b4 % 2, 0))
                    sch.wait("tensor", DVE,
                             max(q_ready, state["stage_ev2"][b4 % 2]))
                    for j in range(4):
                        c = c_of_pos(b4 * 4 + j)
                        hb = (c % 2) * 64
                        def fqq(t, S, _c=c, _hb=hb, _q=qoff, _j=j, _ps=ps):
                            ins = t.matmul(_ps[:, _j * 128:(_j + 1) * 128],
                                           e2mat[_hb:_hb + 64, :],
                                           q_ap(_c // 2, _q, 128)[_hb:_hb + 64, :],
                                           start=(_j == 0), stop=(_j == 3))
                            if _j == 3:
                                ins.then_inc(S[MM], 1)
                        sch.emit("tensor", fqq)
                    gqq = sch.bump(MM, 1)
                    sch.wait("vector", MM, gqq)
                    def fqqe(v, S, _b4=b4, _ps=ps):
                        v.tensor_copy(ar_h[:, _b4 * TS:(_b4 + 1) * TS],
                                      _ps[:, :]).then_inc(S[DVE], 1)
                    sch.emit("vector", fqqe)
                    state["stage_ev2"][b4 % 2] = sch.bump(DVE, 1)
                state["stage_tok"][i] = max(state["stage_ev2"])

            def transposes(i):
                sch.wait("tensor", DVE, state["stage_tok"][i])
                sch.wait("tensor", ACTs, state["trz_evac"])
                sch.wait("tensor", ACTs, state["sreg_last"].get(2, 0))
                for cp in range(16):
                    def ftr(t, S, _cp=cp):
                        ins = t.matmul(ps_t[:, _cp * 64:(_cp + 1) * 64],
                                       qq_ap(pos_of_c(_cp), 0),
                                       ident[0:64, 0:64],
                                       start=(_cp == 0), stop=(_cp == 15),
                                       is_transpose=True)
                        if _cp == 15:
                            ins.then_inc(S[MM], 1)
                    sch.emit("tensor", ftr)
                gtr = sch.bump(MM, 1)
                sch.wait("vector", MM, gtr)
                def ftre(v, S):
                    v.tensor_copy(ar_h[:, 2048:3072], ps_t[:, 0:1024]).then_inc(S[DVE], 1)
                sch.emit("vector", ftre)
                state["qT_tok"][i] = sch.bump(DVE, 1)
                state["last_qT"] = state["qT_tok"][i]

            def sc_row(i, a):
                pa = pos_of_c(a)
                h = (a % 2) * 64
                for q4 in range(4):
                    qi = a * 4 + q4
                    ridx = qi % 2 if qi < 16 else (qi - 16) % 3
                    sreg = sregs3[ridx]
                    sch.wait("tensor", DVE, state["stage_tok"][i])
                    sch.wait("tensor", ACTs, state["sreg_last"].get(ridx, 0))
                    if ridx == 2:
                        sch.wait("tensor", ACTs, state["trz_evac"])
                        sch.wait("tensor", DVE, state["qT_tok"][i])
                    def fs(t, S, _pa=pa, _h=h, _q4=q4, _sr=sreg):
                        t.matmul(_sr[:, :],
                                 qq_ap(_pa, _h),
                                 ar_h[_h:_h + 64, _q4 * TS:(_q4 + 1) * TS],
                                 start=True, stop=True).then_inc(S[MM], 1)
                    sch.emit("tensor", fs)
                    gsc = sch.bump(MM, 1)
                    sch.wait("scalar", MM, gsc)
                    sch.wait("scalar", MM, state["avtok"].get((i - 1, a), 0))
                    sch.wait("scalar", DVE, state["zsum_tok"])
                    zdst = zc4a if i % 2 == 0 else zc4b
                    def fexp(s, S, _a=a, _q4=q4, _sr=sreg, _z=zdst):
                        s.activation(prow_ap(_a)[:, _q4 * TS:(_q4 + 1) * TS],
                                     _sr[:, :], AF.Exp, scale=SCALE,
                                     accum_out=_z[:, _a * 4 + _q4:_a * 4 + _q4 + 1]
                                     ).then_inc(S[ACTs], 1)
                    sch.emit("scalar", fexp)
                    state["exp_tok"][(i, a, q4)] = sch.bump(ACTs, 1)
                    state["sreg_last"][ridx] = state["exp_tok"][(i, a, q4)]

            def av_row(i, a):
                if a == 0:
                    sch.wait("tensor", DVE, max(state["avfree"], ln_done))
                sch.wait("tensor", ACTs, state["exp_tok"][(i, a, 3)])
                sch.wait("tensor", DVE, state["qT_tok"][i])
                for sg in range(4):
                    bank, lo = av_of_strip[sg]
                    tpos = (0, 0) if lo == 0 else (0, 64)
                    def fav(t, S, _a=a, _sg=sg, _bk=bank, _lo=lo, _tp=tpos):
                        ins = t.matmul(_bk[_lo:_lo + 64, :],
                                       qT_ap(_a),
                                       prow_ap(_a)[:, _sg * TS:(_sg + 1) * TS],
                                       start=(_a == 0), stop=(_a == 15),
                                       tile_position=_tp)
                        if _sg == 3:
                            ins.then_inc(S[MM], 1)
                    sch.emit("tensor", fav)
                state["avtok"][(i, a)] = sch.bump(MM, 1)

            def make_tail_stages(i):
                act_base = sch.val(ACTs)
                zsrc = zc4a if i % 2 == 0 else zc4b
                rzbev = [0]
                avev = [0]
                uev = [0, 0]

                def t_rz():
                    sch.wait("vector", ACTs, state["exp_tok"][(i, 15, 3)])
                    sch.wait("vector", DVE, sch.val(DVE))
                    base_z = sch.val(DVE)
                    def fz(v, S, _z=zsrc, _b=base_z):
                        v.tensor_reduce(
                            zcols[:, :].rearrange("p (a o) -> p a o", o=1),
                            _z[:, :].rearrange("p (a q) -> p a q", q=4),
                            mybir.AxisListType.X, ALU.add).then_inc(S[DVE], 1)
                        v.wait_ge(S[DVE], _b + 1)
                        v.reciprocal(zcols[:, :], zcols[:, :]).then_inc(S[DVE], 1)
                        v.wait_ge(S[DVE], _b + 2)
                        v.tensor_copy(rzbf[:, :], zcols[:, :]).then_inc(S[DVE], 1)
                    sch.emit("vector", fz)
                    state["zsum_tok"] = sch.bump(DVE, 3)
                    sch.wait("tensor", DVE, max(state["zsum_tok"], state["last_qT"]))
                    def frzt(t, S):
                        t.matmul(ps_t[0:16, 0:128], rzbf[:, :], ident[:, :],
                                 start=True, stop=True,
                                 is_transpose=True).then_inc(S[MM], 1)
                    sch.emit("tensor", frzt)
                    grzt = sch.bump(MM, 1)
                    sch.wait("scalar", MM, grzt)
                    def frzte(s, S):
                        s.copy(trz[:, :], ps_t[0:16, 0:128]).then_inc(S[ACTs], 1)
                    sch.emit("scalar", frzte)
                    state["trz_evac"] = sch.bump(ACTs, 1)

                def mk_strip(sg):
                    def t_strip():
                        bk, lo = av_of_strip[sg]
                        tpos = (0, 0) if lo == 0 else (0, 64)
                        sch.wait("tensor", ACTs, max(state["trz_evac"], act_base))
                        sch.wait("tensor", DVE, rzbev[0])
                        for j in range(4):
                            cj = c_of_pos(sg * 4 + j)
                            def frzb(t, S, _cj=cj, _j=j, _lo=lo, _tp=tpos):
                                ins = t.matmul(
                                    ps_rz[_lo:_lo + 64, _j * 128:(_j + 1) * 128],
                                    sel16[:, _cj * 64:(_cj + 1) * 64],
                                    trz[:, :],
                                    start=(_j == 0), stop=(_j == 3),
                                    tile_position=_tp)
                                if _j == 3:
                                    ins.then_inc(S[MM], 1)
                            sch.emit("tensor", frzb)
                        grzb = sch.bump(MM, 1)
                        sch.wait("vector", MM, grzb)
                        sch.wait("vector", DVE, avev[0])
                        def frzbe(v, S, _lo=lo):
                            v.tensor_copy(rzb_sb[_lo:_lo + 64, :],
                                          ps_rz[_lo:_lo + 64, :]).then_inc(S[DVE], 1)
                        sch.emit("vector", frzbe)
                        rzbev[0] = sch.bump(DVE, 1)
                        sch.wait("vector", MM, state["avtok"][(i, 15)])
                        sch.wait("vector", DVE, rzbev[0])
                        def fave(v, S, _sg=sg, _lo=lo, _bk=bk):
                            for j in range(4):
                                c = c_of_pos(_sg * 4 + j)
                                ins = v.tensor_mul(
                                    ofm_ap(c // 2, _lo, _lo + 64),
                                    _bk[_lo:_lo + 64, j * 128:(j + 1) * 128],
                                    rzb_sb[_lo:_lo + 64, j * 128:(j + 1) * 128])
                                if j == 3:
                                    ins.then_inc(S[DVE], 1)
                        sch.emit("vector", fave)
                        avev[0] = sch.bump(DVE, 1)
                        if sg == 3:
                            state["avfree"] = avev[0]
                    return t_strip

                def mk_u(pair):
                    def t_u():
                        if pair == 0:
                            sch.wait("tensor", *wready["wu"])
                        for dc in (pair * 2, pair * 2 + 1):
                            ups = stg_u[dc % 2]
                            sch.wait("tensor", DVE, max(avev[0], uev[dc % 2]))
                            sch.wait("tensor", ACTs, act_base)
                            for ec in range(NKC):
                                def fu(t, S, _ec=ec, _dc=dc, _u=ups):
                                    ins = t.matmul(_u[:, 0:128], wutblk(_ec, _dc),
                                                   ar_h[:, 3072 + 128 * _ec:
                                                        3072 + 128 * _ec + 128],
                                                   start=(_ec == 0),
                                                   stop=(_ec == NKC - 1))
                                    if _ec == NKC - 1:
                                        ins.then_inc(S[MM], 1)
                                sch.emit("tensor", fu)
                            gu = sch.bump(MM, 1)
                            sch.wait("vector", MM, gu)
                            o0 = dc * NT + i * 128
                            def fux(v, S, _dc=dc, _o0=o0, _u=ups):
                                xsl = x_sb[:, _o0:_o0 + 128]
                                v.scalar_tensor_tensor(
                                    out=xsl, in0=_u[:, 0:128],
                                    scalar=bu_sb[:, _dc:_dc + 1], in1=xsl,
                                    op0=ALU.add, op1=ALU.add).then_inc(S[DVE], 1)
                            sch.emit("vector", fux)
                            uev[dc % 2] = sch.bump(DVE, 1)
                        if pair == 3:
                            state["ufree"] = max(uev)
                            state["avfree"] = max(state["avfree"], max(uev))
                            state["u_last"] = max(uev)
                    return t_u

                return [t_rz] + [mk_strip(sg) for sg in range(4)] + \
                    [mk_u(p) for p in range(4)]

            def qproj(half):
                t0h = half * 1024
                qev = [0, 0]
                for dc in range(NKC):
                    for st2 in range(2):
                        t0 = t0h + st2 * TS
                        pst = stg_u[(dc * 2 + st2) % 2]
                        sch.wait("tensor", *wready["wq"])
                        sch.wait("tensor", DVE, max(ln_tok[half * 2 + st2],
                                                    state["avfree"],
                                                    state["ufree"]))
                        sch.wait("tensor", DVE, qev[(dc * 2 + st2) % 2])
                        for kc in range(NKC):
                            def fq(t, S, _kc=kc, _dc=dc, _t0=t0, _pst=pst):
                                ins = t.matmul(_pst[:, :], wqtblk(_kc, _dc),
                                               ln_ap(_kc, _t0, TS),
                                               start=(_kc == 0), stop=(_kc == NKC - 1))
                                if _kc == NKC - 1:
                                    ins.then_inc(S[MM], 1)
                            sch.emit("tensor", fq)
                        gq = sch.bump(MM, 1)
                        sch.wait("vector", MM, gq)
                        def fqe(v, S, _dc=dc, _toff=t0 - t0h, _pst=pst):
                            v.tensor_scalar_add(q_ap(_dc, _toff, TS), _pst[:, :],
                                                bq_sb[:, _dc:_dc + 1]).then_inc(S[DVE], 1)
                        sch.emit("vector", fqe)
                        qev[(dc * 2 + st2) % 2] = sch.bump(DVE, 1)
                return max(qev)

            q_ready = qproj(0)
            if "after_qproj0" in hooks:
                hooks["after_qproj0"](sch.val(MM))
            staging(0, q_ready)
            transposes(0)
            for i in range(16):
                if i == 8:
                    for a in range(3, 16):
                        av_row(7, a)
                    for st_fn in make_tail_stages(7):
                        st_fn()
                    q_ready = qproj(1)
                    if "after_qproj1" in hooks:
                        hooks["after_qproj1"](sch.val(MM))
                    staging(8, q_ready)
                    transposes(8)
                for a in range(3):
                    sc_row(i, a)
                if i % 8 != 0:
                    for a in range(3, 16):
                        av_row(i - 1, a)
                    transposes(i)
                stages = make_tail_stages(i - 1) if i % 8 != 0 else []
                for a in range(3, 16):
                    sc_row(i, a)
                    si = a - 3
                    if 0 <= si < len(stages):
                        stages[si]()
                for a in range(3):
                    av_row(i, a)
                if i % 8 != 7:
                    staging(i + 1, q_ready)
            for a in range(3, 16):
                av_row(15, a)
            for st_fn in make_tail_stages(15):
                st_fn()
            return state["u_last"], sch.val(MM)

        do1, doa, do2 = ("ffn1" in phases), ("attn" in phases), ("ffn2" in phases)
        seg_psums = []

        wst1 = {"w1": [None] * NSUP, "w2": [None] * NSUP,
                "sup_last_mm": [0] * NSUP, "issue_next": None}
        wst2 = {"w1": [None] * NSUP, "w2": [None] * NSUP,
                "sup_last_mm": [0] * NSUP, "issue_next": None}

        xw = 0
        ffn1_mm = 0
        if do1:
            with nc.psum_tensor("ps_d1", [128, TS], F32) as ps_d1, \
                 nc.psum_tensor("ps_w1", [1, TS], F32) as ps_w1, \
                 nc.psum_tensor("ps_v1", [1, TS], F32) as ps_v1, \
                 nc.psum_tensor("ps_e1", [128, TS], F32) as ps_e1, \
                 nc.psum_tensor("ps_f1", [128, TS], F32) as ps_f1:
                for s0 in range(2):
                    wst1["w1"][s0] = issue_w1(s0, s0 % 2, 0, "f1")
                    wst1["w2"][s0] = issue_w2(s0, s0 % 2, 0)

                def issue_next_ffn1(sup):
                    nxt = sup + 2
                    tok = wst1["sup_last_mm"][sup]
                    if nxt < NSUP:
                        wst1["w1"][nxt] = issue_w1(nxt, nxt % 2, tok, "f1")
                        wst1["w2"][nxt] = issue_w2(nxt, nxt % 2, tok)
                wst1["issue_next"] = issue_next_ffn1
                xw, _, ffn1_mm = emit_ffn(0, wst1, ps_d1, ps_w1, ps_v1, ps_e1, ps_f1)
                if doa:
                    wready["wq"] = issue_wqut("q", wst1["sup_last_mm"][10])
            sch.seg_break()

        attn_mm = ffn1_mm
        if doa:
            if "wq" not in wready:
                wready["wq"] = issue_wqut("q", ffn1_mm)
            hooks = {}
            def after_qproj0(mmtok):
                wu_wait = max(mmtok, wst1["sup_last_mm"][11] if do1 else 0)
                wready["wu"] = issue_wqut("u", wu_wait)
            def after_qproj1(mmtok):
                if do2:
                    wst2["w1"][0] = issue_w1(0, 0, mmtok, "f2")
            hooks["after_qproj0"] = after_qproj0
            hooks["after_qproj1"] = after_qproj1
            with nc.psum_tensor("s0", [128, TS], F32) as s0, \
                 nc.psum_tensor("s1", [128, TS], F32) as s1, \
                 nc.psum_tensor("d_att", [128, TS], F32) as d_att, \
                 nc.psum_tensor("ps_t", [128, 1024], BF16) as ps_t, \
                 nc.psum_tensor("ps_rz", [128, TS], F32) as ps_rz:
                xw, attn_mm = emit_attn(xw, s0, s1, d_att, ps_t, ps_rz, hooks)
            sch.seg_break()

        xtok = [xw] * NKC
        if do2:
            with nc.psum_tensor("ps_d2", [128, TS], F32) as ps_d2, \
                 nc.psum_tensor("ps_w2", [1, TS], F32) as ps_w2, \
                 nc.psum_tensor("ps_v2", [1, TS], F32) as ps_v2, \
                 nc.psum_tensor("ps_e2", [128, TS], F32) as ps_e2, \
                 nc.psum_tensor("ps_f2", [128, TS], F32) as ps_f2:
                if wst2["w1"][0] is None:
                    wst2["w1"][0] = issue_w1(0, 0, attn_mm, "f2")
                wst2["w2"][0] = issue_w2(0, 0, attn_mm)
                wst2["w1"][1] = issue_w1(1, 1, attn_mm, "f2")
                wst2["w2"][1] = issue_w2(1, 1, attn_mm)

                def issue_next_ffn2(sup):
                    nxt = sup + 2
                    tok = wst2["sup_last_mm"][sup]
                    if nxt < NSUP:
                        wst2["w1"][nxt] = issue_w1(nxt, nxt % 2, tok, "f2")
                        wst2["w2"][nxt] = issue_w2(nxt, nxt % 2, tok)
                wst2["issue_next"] = issue_next_ffn2
                _, xtok, _ = emit_ffn(xw, wst2, ps_d2, ps_w2, ps_v2, ps_e2, ps_f2)

        for dc in range(NKC):
            sch.wait("sync", DVE, xtok[dc])
            def fy(sy, S, _dc=dc):
                sy.dma_start(out=y_h[_dc * 128:(_dc + 1) * 128, :],
                             in_=x_sb[:, _dc * NT:(_dc + 1) * NT]).then_inc(S[OUT], 16)
            sch.emit("sync", fy)
            sch.bump(OUT, 16)
        sch.wait("sync", OUT, sch.val(OUT))

        segments = sch.finish()
        for seg in segments:
            if not any(seg[e] for e in ENGINES):
                continue
            with nc.Block() as block:
                @block.gpsimd
                def _(g, _seg=seg):
                    for f in _seg["gpsimd"]:
                        f(g, sems)

                @block.tensor
                def _(t, _seg=seg):
                    for f in _seg["tensor"]:
                        f(t, sems)

                @block.vector
                def _(v, _seg=seg):
                    for f in _seg["vector"]:
                        f(v, sems)

                @block.scalar
                def _(s, _seg=seg):
                    for f in _seg["scalar"]:
                        f(s, sems)

                @block.sync
                def _(sy, _seg=seg):
                    for f in _seg["sync"]:
                        f(sy, sems)

    return nc


_NC_CACHE = {}


def _get_nc(phases):
    key = tuple(phases)
    if key not in _NC_CACHE:
        _NC_CACHE[key] = build(phases)
    return _NC_CACHE[key]


def make_inputs(L0_attn_x, L0_attn_wq, L0_attn_bq, L0_attn_wu, L0_attn_bu,
                L0_ffn_gate_w, L0_ffn_w1, L0_ffn_b1, L0_ffn_w2, L0_ffn_b2,
                L0_ffn_sw1, L0_ffn_sb1, L0_ffn_sw2, L0_ffn_sb2):
    f = lambda a: np.asarray(a, dtype=np.float32)
    x = f(L0_attn_x)
    w1 = f(L0_ffn_w1)
    w2 = f(L0_ffn_w2)
    w1cat = np.ascontiguousarray(np.concatenate([f(L0_ffn_sw1), w1[0], w1[1]], axis=1))
    w2cat = np.ascontiguousarray(np.concatenate([f(L0_ffn_sw2), w2[0], w2[1]], axis=0))
    wqt_ = np.ascontiguousarray(f(L0_attn_wq).T)
    wut_ = np.ascontiguousarray(f(L0_attn_wu).T)
    b1cat = np.concatenate([f(L0_ffn_sb1), f(L0_ffn_b1)[0], f(L0_ffn_b1)[1]])
    b1h_ = np.ascontiguousarray(b1cat.reshape(96, 128).T)
    b2 = f(L0_ffn_b2)
    b2x_ = np.ascontiguousarray(np.concatenate(
        [b2[0].reshape(8, 128).T, b2[1].reshape(8, 128).T,
         f(L0_ffn_sb2).reshape(8, 128).T], axis=1))
    bqc_ = np.ascontiguousarray(f(L0_attn_bq).reshape(8, 128).T)
    buc_ = np.ascontiguousarray(f(L0_attn_bu).reshape(8, 128).T)
    gw = f(L0_ffn_gate_w)
    gwc_ = np.ascontiguousarray(np.concatenate(
        [gw[0].reshape(8, 128).T, gw[1].reshape(8, 128).T], axis=1))
    shared = dict(w1c=w1cat, w2c=w2cat, wqt=wqt_, wut=wut_, b1h=b1h_, b2x=b2x_,
                  bqc=bqc_, buc=buc_, gwc=gwc_)
    in_maps = []
    for i in range(8):
        m = dict(shared)
        m["x"] = np.ascontiguousarray(x[i].T)
        in_maps.append(m)
    return in_maps


def kernel(L0_attn_x, L0_attn_wq, L0_attn_bq, L0_attn_wu, L0_attn_bu,
           L0_ffn_gate_w, L0_ffn_w1, L0_ffn_b1, L0_ffn_w2, L0_ffn_b2,
           L0_ffn_sw1, L0_ffn_sb1, L0_ffn_sw2, L0_ffn_sb2,
           _phases=("ffn1", "attn", "ffn2")):
    in_maps = make_inputs(L0_attn_x, L0_attn_wq, L0_attn_bq, L0_attn_wu,
                          L0_attn_bu, L0_ffn_gate_w, L0_ffn_w1, L0_ffn_b1,
                          L0_ffn_w2, L0_ffn_b2, L0_ffn_sw1, L0_ffn_sb1,
                          L0_ffn_sw2, L0_ffn_sb2)
    nc = _get_nc(_phases)
    res = run_bass_kernel_spmd(nc, in_maps, core_ids=list(range(8)))
    out = np.stack([np.ascontiguousarray(r["y"].T) for r in res.results], axis=0)
    return out.astype(np.float32)
